# version 17
# baseline (speedup 1.0000x reference)
import numpy as np
import concourse.bass as bass
import concourse.mybir as mybir
from concourse.bass_utils import run_bass_kernel_spmd
from contextlib import ExitStack

F32 = mybir.dt.float32
BF16 = mybir.dt.bfloat16
I32 = mybir.dt.int32
AF = mybir.ActivationFunctionType
ALU = mybir.AluOpType
AX = mybir.AxisListType

NCORES = 8
NTOK = 4096
TT = 512
NTILES = NTOK // TT
NS = TT // 128
D = 1024
DC = D // 128
H = 3072
HC = H // 128
EPS = 1e-5
QSC = 1.0 / 127.0

ENGINES = ("tensor", "vector", "scalar", "sync", "gpsimd")


class Plan:
    def __init__(self):
        from collections import defaultdict
        self.ops = {e: [] for e in ENGINES}
        self.cnt = defaultdict(int)

    def op(self, eng, fn, waits=(), inc=None, amt=1):
        w = {}
        for s, v in waits:
            if v:
                w[s] = max(w.get(s, 0), v)
        self.ops[eng].append((sorted(w.items()), fn, inc, amt))
        if inc is not None:
            self.cnt[inc] += amt
            return self.cnt[inc]
        return None


class Stream:

    def __init__(self, plan, name, buf, nslots, load_fn, queue="sync"):
        self.plan = plan
        self.queue = queue
        self.name = name
        self.buf = buf
        self.nslots = nslots
        self.load_fn = load_fn
        self.src_wait = []
        self.src_wait_fn = None
        self.slot_free = [0] * nslots
        self.loaded = {}
        self.next_g = 0

    def load(self, g):
        assert g == self.next_g, (g, self.next_g)
        self.next_g += 1
        slot = g % self.nslots
        sem = f"L{self.name}{slot}"
        sw = self.src_wait_fn(g) if self.src_wait_fn else self.src_wait
        waits = list(sw) + [("ct", self.slot_free[slot])]
        cnt = self.plan.op(self.queue, self.load_fn(g, slot), waits=waits, inc=sem, amt=16)
        self.loaded[g] = (sem, cnt)

    def slot(self, g):
        return g % self.nslots

    def consume(self, g, ct_cnt):
        self.slot_free[g % self.nslots] = ct_cnt


def build_nc(ntiles=NTILES, debug=False):
    nc = bass.Bass("TRN2", target_bir_lowering=False)
    p = Plan()

    xin = nc.dram_tensor("x", [NTOK, D], F32, kind="ExternalInput").ap()
    wq = nc.dram_tensor("wq", [D, D], F32, kind="ExternalInput").ap()
    wk = nc.dram_tensor("wk", [D, D], F32, kind="ExternalInput").ap()
    wv = nc.dram_tensor("wv", [D, D], F32, kind="ExternalInput").ap()
    wgt = nc.dram_tensor("wgt", [D, D], F32, kind="ExternalInput").ap()
    wo = nc.dram_tensor("wo", [D, D], F32, kind="ExternalInput").ap()
    fg = nc.dram_tensor("fg", [H, D], I32, kind="ExternalInput").ap()
    fu = nc.dram_tensor("fu", [H, D], I32, kind="ExternalInput").ap()
    fd = nc.dram_tensor("fd", [D, H], I32, kind="ExternalInput").ap()
    ld0 = nc.dram_tensor("ld0", [1, 1], F32, kind="ExternalInput").ap()
    ident = nc.dram_tensor("ident", [128, 128], BF16, kind="ExternalInput").ap()
    tmod = nc.dram_tensor("tmod", [128, TT], F32, kind="ExternalInput").ap()
    maskt = nc.dram_tensor("maskt", [128, 128], BF16, kind="ExternalInput").ap()
    out = nc.dram_tensor("out", [NTOK, D], F32, kind="ExternalOutput").ap()

    xbf_scr = nc.dram_tensor("xbf_scr", [NTOK, D], BF16).ap()
    x1_scr = nc.dram_tensor("x1_scr", [NTOK, D], BF16).ap()
    t2 = {"q": nc.dram_tensor("t2_q", [DC * DC * 128 * 128], BF16),
          "k": nc.dram_tensor("t2_k", [DC * DC * 128 * 128], BF16),
          "g": nc.dram_tensor("t2_g", [DC * DC * 128 * 128], BF16),
          "v": nc.dram_tensor("t2_v", [DC * 128 * 1024], BF16),
          "o": nc.dram_tensor("t2_o", [DC * 128 * 1024], BF16),
          "fd": nc.dram_tensor("t2_fd", [HC * 128 * 1024], BF16),
          "fg": nc.dram_tensor("t2_fg", [HC * DC * 128 * 128], BF16),
          "fu": nc.dram_tensor("t2_fu", [HC * DC * 128 * 128], BF16)}

    st = ExitStack()
    with st:
        def sb(name, shape, dt, **kw):
            return st.enter_context(nc.sbuf_tensor(name, shape, dt, **kw))

        DFt = sb("DF", [128, TT], BF16)
        DFI = sb("DFI", [128, TT], BF16)
        MSK = sb("MSK", [128, 128], BF16)
        sld = sb("sld", [1, 1], F32)
        ones = sb("ones", [1, 128], F32)
        lvec = sb("lvec", [128, 1], F32)
        epsT = sb("epsT", [128, 1], F32)
        tmpd = sb("tmpd", [128, TT], F32)

        xbf = sb("xbf", [128, NS * D], BF16)
        xfm2 = [sb(f"xfm{i}", [128, DC * TT], BF16, is_transpose_dest=True) for i in range(2)]
        qp = sb("qp", [128, DC * TT], BF16)
        kp = sb("kp", [128, DC * TT], BF16)
        vtm = sb("vtm", [128, NS * D], BF16)
        gfm = sb("gfm", [128, DC * TT], BF16)
        gof = sb("gof", [128, NS * D], BF16)
        stm = sb("stm", [128, NS * 128], BF16)
        r1 = sb("r1", [128, NS * D], BF16)
        x1b = [sb(f"x1b{i}", [128, NS * D], BF16) for i in range(2)]
        x1fm2 = [sb(f"x1fm{i}", [128, DC * TT], BF16, is_transpose_dest=True) for i in range(2)]
        sgate = sb("sgate", [128, 2 * TT], BF16)
        sup = sb("sup", [128, 2 * TT], BF16)
        hb = sb("hb", [128, HC * TT], BF16, is_transpose_dest=True)
        r2 = sb("r2", [128, NS * D], BF16)
        outb = [sb(f"outb{i}", [128, D], F32) for i in range(2)]
        junk = sb("junk", [128, D], BF16)

        sum1 = sb("sum1", [128, NS], F32)
        sq1 = sb("sq1", [128, NS], F32)
        mst = sb("mst", [128, NS], F32)
        msq = sb("msq", [128, NS], F32)
        vart = sb("vart", [128, NS], F32)
        rstd = sb("rstd", [128, NS], F32)
        sum2 = sb("sum2", [128, NS], F32)
        sq2 = sb("sq2", [128, NS], F32)
        mst2 = sb("mst2", [128, NS], F32)
        msq2 = sb("msq2", [128, NS], F32)
        vart2 = sb("vart2", [128, NS], F32)
        rstd2 = sb("rstd2", [128, NS], F32)

        bufq = sb("bufq", [128, 2 * 1024], BF16)
        bufk = sb("bufk", [128, 2 * 1024], BF16)
        bufg = sb("bufg", [128, 2 * 1024], BF16)
        bufv = sb("bufv", [128, 2 * 1024], BF16, is_transpose_dest=True)
        stgi = [sb(f"stgi{i}", [128, 1024], I32) for i in range(2)]
        xcb = [sb("xcb0", [128, 1024], BF16)]
        identS = sb("identS", [128, 128], BF16)
        tev = [sb(f"tev{i}", [128, 1024], BF16) for i in range(2)]
        bufo = sb("bufo", [128, 3 * 1024], BF16, is_transpose_dest=True)
        buffg = sb("buffg", [128, 2 * 1024], BF16)
        buffu = sb("buffu", [128, 2 * 1024], BF16)
        buffd = sb("buffd", [128, 3 * 1024], BF16, is_transpose_dest=True)

        ps = [st.enter_context(nc.psum_tensor(f"ps{i}", [128, 1024], F32)) for i in range(4)]

        def bank_ap(b, cols=512):
            return ps[b // 2][:, (b % 2) * 512:(b % 2) * 512 + cols]

        def pair_ap(pi):
            return ps[pi]

        bank_free = {b: ("ct", 0) for b in range(8)}
        _bkcur = [0]

        def grab(n=1):
            if n == 2 and _bkcur[0] % 2:
                _bkcur[0] += 1
            ids = [(_bkcur[0] + k) % 8 for k in range(n)]
            _bkcur[0] += n
            return ids

        def take(b):
            return [bank_free[b]]

        def free(b, sem, cnt):
            bank_free[b] = (sem, cnt)

        cast_cnt = {}
        cast_x = [f"x{i}c" for i in range(ntiles)]

        p.op("gpsimd", lambda e: e.memset(ones[:], 1.0))
        p.op("gpsimd", lambda e: e.memset(epsT[:], EPS), inc="cp")

        conv_pl = [None, None]
        conv_hb = {}
        W_LANES = [(f"W{k}", hb[:, 6144 + k * 1024: 6144 + (k + 1) * 1024]) for k in range(4)]
        X_LANES = [("X0", xcb[0][:])]
        _st = dict(pl=0, xl=0, wl=0, tev=0)
        tev_free = [None, None]
        t2_done = {}

        def _load_chunk(src_ap, r, cc, is_f32, waits):
            slot = _st["pl"] % 2
            _st["pl"] += 1
            sstg = stgi[slot][:].bitcast(F32) if is_f32 else stgi[slot][:]
            lw = list(waits)
            if conv_pl[slot]:
                lw.append(conv_pl[slot])
            lc = p.op("sync",
                      lambda en, s_=src_ap, rr=r, c_=cc, d_=sstg:
                          en.dma_start(out=d_, in_=s_[rr * 128:(rr + 1) * 128,
                                                      c_ * 1024:(c_ + 1) * 1024]),
                      waits=lw, inc=f"PL{slot}", amt=16)
            return slot, sstg, lc

        def conv_x(i):
            for r in range(NS):
                slot, sstg, lc = _load_chunk(xin[i * TT:(i + 1) * TT, :], r, 0, True, [])
                lane, lap = X_LANES[_st["xl"] % len(X_LANES)]
                _st["xl"] += 1
                hw = [(f"PL{slot}", lc)]
                if conv_hb.get(lane):
                    hw.append(conv_hb[lane])
                cc_ = p.op("gpsimd",
                           lambda en, la=lap, s_=sstg: en.tensor_copy(la, s_),
                           waits=hw, inc="cp")
                conv_pl[slot] = ("cp", cc_)
                sc_ = p.op("scalar",
                           lambda en, ii=i, rr=r, la=lap:
                               en.dma_start(out=xbf_scr[ii * TT + rr * 128:
                                                        ii * TT + (rr + 1) * 128, :], in_=la),
                           waits=[("cp", cc_)], inc="S" + lane, amt=16)
                conv_hb[lane] = ("S" + lane, sc_)
            cast_cnt[f"x{i}c"] = [("SX0", p.cnt["SX0"])]

        wdone = {}

        def wconv_chunk(name, src_ap, is_f32, dst, style, r, cg, lanes=None,
                        first_gate=()):
                    lanes = lanes or W_LANES
                    slot, sstg, lc = _load_chunk(src_ap, r, cg, is_f32, [])
                    lane, lap = lanes[_st["wl"] % len(lanes)]
                    _st["wl"] += 1
                    hw = [(f"PL{slot}", lc)]
                    if conv_hb.get(lane):
                        hw.append(conv_hb[lane])
                    else:
                        hw.extend(first_gate)
                    cc_ = p.op("vector",
                               lambda en, la=lap, s_=sstg: en.tensor_copy(la, s_),
                               waits=hw, inc="cv")
                    conv_pl[slot] = ("cv", cc_)
                    b = grab()[0]
                    bw = take(b)
                    tc = None
                    for c in range(DC):
                        tc = p.op("tensor",
                                  lambda en, bb=b, la=lap, c_=c:
                                      en.transpose(
                                          bank_ap(bb).bitcast(BF16)[:, c_ * 128:(c_ + 1) * 128],
                                          la[:, c_ * 128:(c_ + 1) * 128], identS[:]),
                                  waits=([("cv", cc_), ("dS", c_idt)] + bw) if c == 0 else (),
                                  inc="ct" if c == DC - 1 else None)
                    conv_hb[lane] = ("ct", tc)
                    ts_ = _st["tev"] % 2
                    _st["tev"] += 1
                    ew = [("ct", tc)]
                    if tev_free[ts_]:
                        ew.append(tev_free[ts_])
                    ev = p.op("vector",
                              lambda en, bb=b, t_=ts_:
                                  en.tensor_copy(tev[t_][:], bank_ap(bb).bitcast(BF16)),
                              waits=ew, inc="cv")
                    free(b, "cv", ev)
                    if style == "lhsT":
                        dap = bass.AP(dst, r * DC * 16384, [[1024, 128], [1, 1024]])
                    else:
                        dap = bass.AP(dst, cg * DC * 131072 + r * 128,
                                      [[1024, 128], [131072, DC], [1, 128]])
                    ssem = f"TV{ts_}"
                    sc_ = p.op("scalar",
                               lambda en, da=dap, t_=ts_, st_=style:
                                   en.dma_start(out=da, in_=tev[t_][:] if st_ == "lhsT"
                                                else tev[t_][:].rearrange("p (c l) -> p c l",
                                                                          l=128)),
                               waits=[("cv", ev)], inc=ssem, amt=16)
                    tev_free[ts_] = (ssem, sc_)
                    wdone.setdefault(name, []).append((ssem, sc_))

        def wconv(name, src_ap, R, C, is_f32, dst, style, cg_outer=False):
            if cg_outer:
                for cg in range(C // 1024):
                    for r in range(R // 128):
                        wconv_chunk(name, src_ap, is_f32, dst, style, r, cg)
            else:
                for r in range(R // 128):
                    for cg in range(C // 1024):
                        wconv_chunk(name, src_ap, is_f32, dst, style, r, cg)
            t2_done[name] = [("TV0", p.cnt["TV0"]), ("TV1", p.cnt["TV1"])]
            cast_cnt[name] = t2_done[name]

        p.op("sync", lambda e: e.dma_start(out=sld[:], in_=ld0), inc="dS", amt=16)
        p.op("sync", lambda e: e.dma_start(out=tmpd[:], in_=tmod), inc="dS", amt=16)
        p.op("sync", lambda e: e.dma_start(out=MSK[:], in_=maskt), inc="dS", amt=16)
        c_idt = p.op("sync", lambda e: e.dma_start(out=identS[:], in_=ident), inc="dS", amt=16)
        c_con = c_idt
        c_sld = c_tmd = c_msk = c_con

        c_bc = p.op("tensor", lambda e: e.matmul(ps[0][:, 0:1], ones[:], sld[:],
                                                 start=True, stop=True),
                    waits=[("dS", c_sld), ("cp", 1)], inc="ct")
        c_sg = p.op("scalar", lambda e: e.activation(lvec[:], ps[0][:, 0:1], AF.Sigmoid),
                    waits=[("ct", c_bc)], inc="cs")
        c_ln = p.op("scalar", lambda e: e.activation(lvec[:], lvec[:], AF.Ln),
                    waits=[("cs", c_sg)], inc="cs")
        c_tm = p.op("vector", lambda e: e.tensor_scalar(tmpd[:], tmpd[:], lvec[:, 0:1], None,
                                                        ALU.mult),
                    waits=[("cs", c_ln), ("dS", c_tmd)], inc="cv")
        p.op("scalar", lambda e: e.activation(DFt[:], tmpd[:], AF.Exp),
             waits=[("cv", c_tm)], inc="cs")
        c_dfi = p.op("scalar", lambda e: e.activation(DFI[:], tmpd[:], AF.Exp, scale=-1.0),
                     inc="cs")
        free(0, "cs", c_sg)

        wconv("wv", wv, D, D, True, t2["v"], "rhs")
        wconv("wq", wq, D, D, True, t2["q"], "lhsT")
        conv_x(0)
        if ntiles > 1:
            conv_x(1)

        def mk_scr2_loader(t2t, npt, buf):
            def mk(g, slot):
                j = g % npt
                def fn(e):
                    src = bass.AP(t2t, j * DC * 16384, [[128, 128], [16384, DC], [1, 128]])
                    dst = buf[:, slot * 1024:(slot + 1) * 1024].rearrange(
                        "p (e l) -> p e l", l=128)
                    return e.dma_start(out=dst, in_=src)
                return fn
            return mk

        def mk_piece_loader(t2t, npt, buf):
            def mk(g, slot):
                j = g % npt
                def fn(e):
                    src = bass.AP(t2t, j * 131072, [[1024, 128], [1, 1024]])
                    return e.dma_start(out=buf[:, slot * 1024:(slot + 1) * 1024], in_=src)
                return fn
            return mk

        MQ = Stream(p, "q", bufq, 2, mk_piece_loader(t2["q"], DC, bufq))
        MQ.src_wait_fn = lambda g: [wdone["wq"][g % DC]]
        MK = Stream(p, "k", bufk, 2, mk_piece_loader(t2["k"], DC, bufk))
        MK.src_wait_fn = lambda g: [wdone["wk"][g % DC]]
        MG = Stream(p, "g", bufg, 2, mk_piece_loader(t2["g"], DC, bufg))
        MG.src_wait_fn = lambda g: [wdone["wg"][g % DC]]
        MV = Stream(p, "v", bufv, 2, mk_piece_loader(t2["v"], DC, bufv))
        MV.src_wait_fn = lambda g: [wdone["wv"][-1]]
        MO = Stream(p, "o", bufo, 3, mk_piece_loader(t2["o"], DC, bufo))
        MO.src_wait_fn = lambda g: [wdone["wo"][-1]]
        MFG = Stream(p, "G", buffg, 2, mk_piece_loader(t2["fg"], HC, buffg))
        MFG.src_wait_fn = lambda g: [wdone["fg"][g % HC]]
        MFU = Stream(p, "U", buffu, 2, mk_piece_loader(t2["fu"], HC, buffu))
        MFU.src_wait_fn = lambda g: [wdone["fu"][g % HC]]
        MFD = Stream(p, "D", buffd, 3, mk_piece_loader(t2["fd"], HC, buffd))
        MFD.src_wait_fn = lambda g: [wdone["fd"][((g % HC) // DC) * DC + DC - 1]]

        state = dict(xfm_free0=0, xfm_free1=0, x1fm_free0=0, x1fm_free1=0, xbf_free=0,
                     qp_free=0, kp_free=0, r1_free_cs=0)
        x1b_store = [None, None]
        out_store = [None, None]

        def emit_inputs(ti):
            par = ti % 2
            t0 = ti * TT
            xfm = xfm2[par]
            xfm_loads = None
            for e in range(DC):
                w = (list(cast_cnt[cast_x[ti]]) + [("ct", state[f"xfm_free{par}"])]) if e == 0 else ()
                cnt = p.op("scalar",
                           lambda en, ee=e, tt0=t0, xf=xfm: en.dma_start(
                               out=xf[:, ee * TT:(ee + 1) * TT],
                               in_=xbf_scr[tt0:tt0 + TT, ee * 128:(ee + 1) * 128],
                               transpose=True),
                           waits=w, inc=f"DXF{par}", amt=16)
                xfm_loads = (f"DXF{par}", cnt)
            cnt = p.op("scalar",
                       lambda en, tt0=t0: en.dma_start(
                           out=xbf[:].rearrange("p (s d) -> p s d", d=D),
                           in_=bass.AP(xbf_scr.tensor, tt0 * D,
                                       [[D, 128], [128 * D, NS], [1, D]])),
                       waits=list(cast_cnt[cast_x[ti]]) + [("cv", state["xbf_free"])],
                       inc="DXB", amt=16)
            state[f"inp_{ti}"] = (xfm_loads, ("DXB", cnt))

        def emit_attn_body(ti):
            par = ti % 2
            t0 = ti * TT
            xfm = xfm2[par]
            xfm_loads, xbf_load = state[f"inp_{ti}"]

            q_evac_last = k_evac_last = 0
            for which, M, dst, dfap, freekey in (("q", MQ, qp, DFt, "qp_free"),
                                                 ("k", MK, kp, DFI, "kp_free")):
                if ti == 0 and which == "k":
                    wconv("wk", wk, D, D, True, t2["k"], "lhsT")
                base_g = M.next_g
                M.load(base_g)
                M.load(base_g + 1)
                gc = None
                for j in range(DC):
                    g = base_g + j
                    b = grab()[0]
                    slot = M.slot(g)
                    waits = [M.loaded[g], xfm_loads] + take(b)
                    for e in range(DC):
                        gc = p.op("tensor",
                                  lambda en, bb=b, sl=slot, ee=e, Mb=M.buf, xf=xfm:
                                      en.matmul(bank_ap(bb),
                                                Mb[:, sl * 1024 + ee * 128:sl * 1024 + (ee + 1) * 128],
                                                xf[:, ee * TT:(ee + 1) * TT],
                                                start=(ee == 0), stop=(ee == DC - 1)),
                                  waits=waits if e == 0 else (),
                                  inc="ct" if e == DC - 1 else None)
                    M.consume(g, gc)
                    if g + 2 < base_g + DC:
                        M.load(g + 2)
                    ew = [("ct", gc), ("cs", c_dfi)]
                    if j == 0:
                        ew.append(("ct", state[freekey]))
                    ec = p.op("vector",
                              lambda en, bb=b, jj=j, dd=dst, df=dfap:
                                  en.tensor_tensor(dd[:, jj * TT:(jj + 1) * TT],
                                                   bank_ap(bb), df[:], ALU.mult),
                              waits=ew, inc="cv")
                    free(b, "cv", ec)
                    if which == "q":
                        q_evac_last = ec
                    else:
                        k_evac_last = ec

            v_evac_last = 0
            for sub in range(1):
                groups = [(s, h) for s in range(NS) for h in range(2)]
                gbanks = [grab()[0] for _ in groups]
                base_g = MV.next_g
                MV.load(base_g)
                MV.load(base_g + 1)
                gct = None
                for e in range(DC):
                    g = base_g + e
                    slot = MV.slot(g)
                    for gi, (s, h) in enumerate(groups):
                        b = gbanks[gi]
                        if e == 0:
                            waits = [MV.loaded[g], xfm_loads] + take(b)
                        elif gi == 0:
                            waits = [MV.loaded[g]]
                        else:
                            waits = ()
                        gct = p.op("tensor",
                                   lambda en, bb=b, ss=s, hh=h, ee=e, sl=slot, xf=xfm:
                                       en.matmul(bank_ap(bb),
                                                 xf[:, ee * TT + ss * 128:ee * TT + (ss + 1) * 128],
                                                 bufv[:, sl * 1024 + hh * 512:sl * 1024 + (hh + 1) * 512],
                                                 start=(ee == 0), stop=(ee == DC - 1)),
                                   waits=waits,
                                   inc="ct" if gi == len(groups) - 1 else None)
                    MV.consume(g, gct)
                    if e + 2 < DC:
                        MV.load(g + 2)
                for gi, (s, h) in enumerate(groups):
                    b = gbanks[gi]
                    ec = p.op("vector",
                              lambda en, bb=b, ss=s, hh=h:
                                  en.tensor_copy(vtm[:, ss * D + hh * 512:ss * D + (hh + 1) * 512],
                                                 bank_ap(bb)),
                              waits=[("ct", gct)], inc="cv")
                    free(b, "cv", ec)
                    v_evac_last = ec

            if ti == 0:
                wconv("wg", wgt, D, D, True, t2["g"], "lhsT")
                wconv("wo", wo, D, D, True, t2["o"], "rhs")
            g_evac_last = 0
            base_g = MG.next_g
            MG.load(base_g)
            MG.load(base_g + 1)
            gc = None
            for j in range(DC):
                g = base_g + j
                b = grab()[0]
                slot = MG.slot(g)
                waits = [MG.loaded[g], xfm_loads] + take(b)
                for e in range(DC):
                    gc = p.op("tensor",
                              lambda en, bb=b, sl=slot, ee=e, xf=xfm:
                                  en.matmul(bank_ap(bb),
                                            bufg[:, sl * 1024 + ee * 128:sl * 1024 + (ee + 1) * 128],
                                            xf[:, ee * TT:(ee + 1) * TT],
                                            start=(ee == 0), stop=(ee == DC - 1)),
                              waits=waits if e == 0 else (),
                              inc="ct" if e == DC - 1 else None)
                MG.consume(g, gc)
                if g + 2 < base_g + DC:
                    MG.load(g + 2)
                ec = p.op("scalar",
                          lambda en, bb=b, jj=j:
                              en.activation(gfm[:, jj * TT:(jj + 1) * TT], bank_ap(bb),
                                            AF.Sigmoid),
                          waits=[("ct", gc)], inc="cs")
                free(b, "cs", ec)
                g_evac_last = ec
            xfm_free_new = gc

            s_evac = []
            gc = None
            for s in range(NS):
                b = grab()[0]
                waits = [("cv", max(q_evac_last, k_evac_last))] + take(b)
                for e in range(DC):
                    gc = p.op("tensor",
                              lambda en, bb=b, ss=s, ee=e:
                                  en.matmul(bank_ap(bb, 128),
                                            kp[:, ee * TT + ss * 128:ee * TT + (ss + 1) * 128],
                                            qp[:, ee * TT + ss * 128:ee * TT + (ss + 1) * 128],
                                            start=(ee == 0), stop=(ee == DC - 1)),
                              waits=waits if e == 0 else (),
                              inc="ct" if e == DC - 1 else None)
                ec = p.op("vector",
                          lambda en, bb=b, ss=s:
                              en.tensor_tensor(stm[:, ss * 128:(ss + 1) * 128],
                                               bank_ap(bb, 128), MSK[:], ALU.mult),
                          waits=[("ct", gc), ("dS", c_msk)], inc="cv")
                free(b, "cv", ec)
                s_evac.append(ec)
            s_phase_end = gc

            gof_evac = []
            gc = None
            for s in range(NS):
                pids = grab(2)
                pi = pids[0] // 2
                waits = [("cv", v_evac_last), ("cv", s_evac[s])] + take(pids[0]) + take(pids[1])
                for j in range(DC):
                    gc = p.op("tensor",
                              lambda en, pp=pi, ss=s, jj=j:
                                  en.matmul(pair_ap(pp)[:, jj * 128:(jj + 1) * 128],
                                            vtm[:, ss * D + jj * 128:ss * D + (jj + 1) * 128],
                                            stm[:, ss * 128:(ss + 1) * 128],
                                            start=True, stop=True),
                              waits=waits if j == 0 else (),
                              inc="ct" if j == DC - 1 else None)
                ec = p.op("vector",
                          lambda en, pp=pi, ss=s:
                              en.tensor_tensor(
                                  gof[:, ss * D:(ss + 1) * D].rearrange("p (j l) -> p j l", l=128),
                                  pair_ap(pp).rearrange("p (j l) -> p j l", l=128),
                                  gfm.rearrange("p (j t) -> p j t", t=TT)[:, :, ss * 128:(ss + 1) * 128],
                                  ALU.mult),
                          waits=[("ct", gc), ("cs", g_evac_last)], inc="cv")
                free(pids[0], "cv", ec)
                free(pids[1], "cv", ec)
                gof_evac.append(ec)
            o_phase_end = gc

            r1_evac_last = 0
            wgroups = [(s, h) for s in range(NS) for h in range(2)]
            wbanks = [grab()[0] for _ in wgroups]
            base_g = MO.next_g
            for la in range(2):
                MO.load(base_g + la)
            gct = None
            for e in range(DC):
                g = base_g + e
                slot = MO.slot(g)
                for gi, (s, h) in enumerate(wgroups):
                    b = wbanks[gi]
                    if e == 0:
                        waits = [MO.loaded[g], ("cv", gof_evac[NS - 1])] + take(b)
                    elif gi == 0:
                        waits = [MO.loaded[g]]
                    else:
                        waits = ()
                    gct = p.op("tensor",
                               lambda en, bb=b, ss=s, ee=e, sl=slot, hh=h:
                                   en.matmul(bank_ap(bb),
                                             gof[:, (ss * DC + ee) * 128:(ss * DC + ee + 1) * 128],
                                             bufo[:, sl * 1024 + hh * 512:sl * 1024 + (hh + 1) * 512],
                                             start=(ee == 0), stop=(ee == DC - 1)),
                               waits=waits,
                               inc="ct" if gi == len(wgroups) - 1 else None)
                MO.consume(g, gct)
                if e + 2 < DC:
                    MO.load(g + 2)
            for gi, (s, h) in enumerate(wgroups):
                b = wbanks[gi]
                ec = p.op("vector",
                          lambda en, bb=b, ss=s, hh=h:
                              en.scalar_tensor_tensor(
                                  r1[:, ss * D + hh * 512:ss * D + (hh + 1) * 512],
                                  xbf[:, ss * D + hh * 512:ss * D + (hh + 1) * 512],
                                  2.0, bank_ap(bb), ALU.mult, ALU.add),
                          waits=[("ct", gct), xbf_load,
                                 ("cs", state["r1_free_cs"])],
                          inc="cv")
                free(b, "cv", ec)
                r1_evac_last = ec

            c_red = p.op("vector",
                         lambda en: en.reduce_sum(sum1[:, 0:NS],
                                                  r1[:].rearrange("p (s d) -> p s d", d=D), AX.X),
                         waits=[("cv", r1_evac_last)], inc="cv")
            sq_last = 0
            for s in range(NS):
                sq_last = p.op("vector",
                               lambda en, ss=s: en.scalar_tensor_tensor(
                                   junk[:], r1[:, ss * D:(ss + 1) * D], 1.0,
                                   r1[:, ss * D:(ss + 1) * D], ALU.mult, ALU.mult,
                                   accum_out=sq1[:, ss:ss + 1]),
                               waits=(), inc="cv")
            c_mean = p.op("vector", lambda en: en.tensor_scalar(mst[:], sum1[:], 1.0 / D, None,
                                                                ALU.mult),
                          waits=[("cv", c_red)], inc="cv")
            c_msq = p.op("vector", lambda en: en.tensor_tensor(msq[:], mst[:], mst[:], ALU.mult),
                         waits=[("cv", c_mean)], inc="cv")
            c_var = p.op("vector",
                         lambda en: en.scalar_tensor_tensor(vart[:], sq1[:], 1.0 / D, msq[:],
                                                            ALU.mult, ALU.subtract),
                         waits=[("cv", sq_last), ("cv", c_msq)], inc="cv")
            c_lnv = p.op("scalar", lambda en: en.activation(vart[:], vart[:], AF.Ln, bias=epsT[:]),
                         waits=[("cv", c_var)], inc="cs")
            c_rstd = p.op("scalar", lambda en: en.activation(rstd[:], vart[:], AF.Exp, scale=-0.5),
                          waits=[("cs", c_lnv)], inc="cs")
            x1r = 0
            for s in range(NS):
                w = [("cs", c_rstd)]
                if s == 0 and x1b_store[par] is not None:
                    w.append(x1b_store[par])
                x1r = p.op("vector",
                           lambda en, ss=s, pp=par:
                               en.tensor_scalar(x1b[pp][:, ss * D:(ss + 1) * D],
                                                r1[:, ss * D:(ss + 1) * D],
                                                mst[:, ss:ss + 1], rstd[:, ss:ss + 1],
                                                ALU.subtract, ALU.mult),
                           waits=w, inc="cv")

            x1b_store[par] = ("DX1S", p.op("scalar",
                                  lambda en, tt0=t0, pp=par: en.dma_start(
                                      out=bass.AP(x1_scr.tensor, tt0 * D,
                                                  [[D, 128], [128 * D, NS], [1, D]]),
                                      in_=x1b[pp][:].rearrange("p (s d) -> p s d", d=D)),
                                  waits=[("cv", x1r)], inc="DX1S", amt=16))

            state[f"xfm_free{par}"] = xfm_free_new
            state["xbf_free"] = r1_evac_last
            state["qp_free"] = s_phase_end
            state["kp_free"] = s_phase_end
            state["r1_free_cs"] = 0

        def emit_x1fm(ti):
            par = ti % 2
            t0 = ti * TT
            x1f = x1fm2[par]
            x1fm_loads = None
            for e in range(DC):
                w = [x1b_store[par], ("ct", state[f"x1fm_free{par}"])] if e == 0 else ()
                cnt = p.op("scalar",
                           lambda en, ee=e, tt0=t0, xf=x1f: en.dma_start(
                               out=xf[:, ee * TT:(ee + 1) * TT],
                               in_=x1_scr[tt0:tt0 + TT, ee * 128:(ee + 1) * 128],
                               transpose=True),
                           waits=w, inc=f"DX1F{par}", amt=16)
                x1fm_loads = (f"DX1F{par}", cnt)
            state[f"x1fm_loads_{ti}"] = x1fm_loads

        def emit_ffn_gateup(ti):
            par = ti % 2
            t0 = ti * TT
            x1fm = x1fm2[par]
            x1fm_loads = state[f"x1fm_loads_{ti}"]
            if ti == 0:
                state["prologue_store_done"] = [("TV0", p.cnt["TV0"]),
                                                ("TV1", p.cnt["TV1"])]

            h_last = 0
            LAG = 4
            if ti == 0:
                F_LANES = [("F0", qp[:, 0:1024]), ("F1", qp[:, 1024:2048]),
                           ("F2", kp[:, 0:1024]), ("F3", kp[:, 1024:2048])]
                fgate = [("ct", state["qp_free"])]
                for jj in range(LAG):
                    wconv_chunk("fg", fg, False, t2["fg"], "lhsT", jj, 0, F_LANES, fgate)
                    wconv_chunk("fu", fu, False, t2["fu"], "lhsT", jj, 0, F_LANES, fgate)
                    wconv_chunk("fd", fd, False, t2["fd"], "rhs", jj % DC, jj // DC,
                                F_LANES, fgate)
            base_fg = MFG.next_g
            base_fu = MFU.next_g
            MFG.load(base_fg)
            MFU.load(base_fu)
            MFG.load(base_fg + 1)
            MFU.load(base_fu + 1)
            sg_free = [0, 0]
            gcu = None
            for j in range(HC):
                if ti == 0 and j + LAG < HC:
                    jj = j + LAG
                    wconv_chunk("fg", fg, False, t2["fg"], "lhsT", jj, 0, F_LANES)
                    wconv_chunk("fu", fu, False, t2["fu"], "lhsT", jj, 0, F_LANES)
                    wconv_chunk("fd", fd, False, t2["fd"], "rhs", jj % DC, jj // DC,
                                F_LANES)
                gg = base_fg + j
                gu = base_fu + j
                bgate = grab()[0]
                bup = grab()[0]
                slotg = MFG.slot(gg)
                slotu = MFU.slot(gu)
                waits = [MFG.loaded[gg], x1fm_loads] + take(bgate)
                gcg = None
                for e in range(DC):
                    gcg = p.op("tensor",
                               lambda en, bb=bgate, sl=slotg, ee=e, x1f_=x1fm:
                                   en.matmul(bank_ap(bb),
                                             buffg[:, sl * 1024 + ee * 128:sl * 1024 + (ee + 1) * 128],
                                             x1f_[:, ee * TT:(ee + 1) * TT],
                                             start=(ee == 0), stop=(ee == DC - 1)),
                               waits=waits if e == 0 else (),
                               inc="ct" if e == DC - 1 else None)
                MFG.consume(gg, gcg)
                waits = [MFU.loaded[gu]] + take(bup)
                for e in range(DC):
                    gcu = p.op("tensor",
                               lambda en, bb=bup, sl=slotu, ee=e, x1f_=x1fm:
                                   en.matmul(bank_ap(bb),
                                             buffu[:, sl * 1024 + ee * 128:sl * 1024 + (ee + 1) * 128],
                                             x1f_[:, ee * TT:(ee + 1) * TT],
                                             start=(ee == 0), stop=(ee == DC - 1)),
                               waits=waits if e == 0 else (),
                               inc="ct" if e == DC - 1 else None)
                MFU.consume(gu, gcu)
                if j + 2 < HC:
                    MFG.load(gg + 2)
                    MFU.load(gu + 2)
                sl2 = j % 2
                e1 = p.op("scalar",
                          lambda en, bb=bgate, s2=sl2:
                              en.activation(sgate[:, s2 * TT:(s2 + 1) * TT], bank_ap(bb),
                                            AF.Silu, scale=QSC),
                          waits=[("ct", gcg), ("cv", sg_free[sl2])], inc="cs")
                free(bgate, "cs", e1)
                e2 = p.op("vector",
                          lambda en, bb=bup, s2=sl2:
                              en.tensor_scalar(sup[:, s2 * TT:(s2 + 1) * TT], bank_ap(bb),
                                               QSC * QSC, None, ALU.mult),
                          waits=[("ct", gcu)], inc="cv")
                free(bup, "cv", e2)
                hw = [("cs", e1), ("cv", e2)]
                if ti == 0 and j == 0:
                    hw.extend(state["prologue_store_done"])
                h_last = p.op("vector",
                              lambda en, jj=j, s2=sl2:
                                  en.tensor_tensor(hb[:, jj * TT:(jj + 1) * TT],
                                                   sgate[:, s2 * TT:(s2 + 1) * TT],
                                                   sup[:, s2 * TT:(s2 + 1) * TT], ALU.mult),
                              waits=hw, inc="cv")
                sg_free[sl2] = h_last

            state[f"hlast_{ti}"] = h_last
            state[f"x1fm_free{par}"] = gcu

        def emit_ffn_down(ti):
            par = ti % 2
            t0 = ti * TT
            h_last = state[f"hlast_{ti}"]
            r2_evac_last = 0
            dgroups = [(s, h) for s in range(NS) for h in range(2)]
            dbanks = [grab()[0] for _ in dgroups]
            base_g = MFD.next_g
            for la in range(2):
                MFD.load(base_g + la)
            gct = None
            for jh in range(HC):
                g = base_g + jh
                slot = MFD.slot(g)
                for gi, (s, h) in enumerate(dgroups):
                    b = dbanks[gi]
                    if jh == 0:
                        waits = [MFD.loaded[g], ("cv", h_last)] + take(b)
                    elif gi == 0:
                        waits = [MFD.loaded[g]]
                    else:
                        waits = ()
                    gct = p.op("tensor",
                               lambda en, bb=b, ss=s, jj=jh, sl=slot, hh=h:
                                   en.matmul(bank_ap(bb),
                                             hb[:, jj * TT + ss * 128:jj * TT + (ss + 1) * 128],
                                             buffd[:, sl * 1024 + hh * 512:sl * 1024 + (hh + 1) * 512],
                                             start=(jj == 0), stop=(jj == HC - 1)),
                               waits=waits,
                               inc="ct" if gi == len(dgroups) - 1 else None)
                MFD.consume(g, gct)
                if jh + 2 < HC:
                    MFD.load(g + 2)
            for gi, (s, h) in enumerate(dgroups):
                b = dbanks[gi]
                ec = p.op("vector",
                          lambda en, bb=b, ss=s, hh=h, pp=par:
                              en.scalar_tensor_tensor(
                                  r2[:, ss * D + hh * 512:ss * D + (hh + 1) * 512],
                                  x1b[pp][:, ss * D + hh * 512:ss * D + (hh + 1) * 512],
                                  1.0, bank_ap(bb), ALU.mult, ALU.add),
                          waits=[("ct", gct)], inc="cv")
                free(b, "cv", ec)
                r2_evac_last = ec

            c_red2 = p.op("vector",
                          lambda en: en.reduce_sum(sum2[:, 0:NS],
                                                   r2[:].rearrange("p (s d) -> p s d", d=D), AX.X),
                          waits=[("cv", r2_evac_last)], inc="cv")
            sq_last2 = 0
            for s in range(NS):
                sq_last2 = p.op("vector",
                                lambda en, ss=s: en.scalar_tensor_tensor(
                                    junk[:], r2[:, ss * D:(ss + 1) * D], 1.0,
                                    r2[:, ss * D:(ss + 1) * D], ALU.mult, ALU.mult,
                                    accum_out=sq2[:, ss:ss + 1]),
                                waits=(), inc="cv")
            c_mean2 = p.op("vector", lambda en: en.tensor_scalar(mst2[:], sum2[:], 1.0 / D, None,
                                                                 ALU.mult),
                           waits=[("cv", c_red2)], inc="cv")
            c_msq2 = p.op("vector", lambda en: en.tensor_tensor(msq2[:], mst2[:], mst2[:], ALU.mult),
                          waits=[("cv", c_mean2)], inc="cv")
            c_var2 = p.op("vector",
                          lambda en: en.scalar_tensor_tensor(vart2[:], sq2[:], 1.0 / D, msq2[:],
                                                             ALU.mult, ALU.subtract),
                          waits=[("cv", sq_last2), ("cv", c_msq2)], inc="cv")
            c_lnv2 = p.op("scalar", lambda en: en.activation(vart2[:], vart2[:], AF.Ln, bias=epsT[:]),
                          waits=[("cv", c_var2)], inc="cs")
            c_rstd2 = p.op("scalar", lambda en: en.activation(rstd2[:], vart2[:], AF.Exp, scale=-0.5),
                           waits=[("cs", c_lnv2)], inc="cs")
            for s in range(NS):
                op_ = s % 2
                w = [("cs", c_rstd2)]
                if out_store[op_] is not None:
                    w.append(out_store[op_])
                cnrm = p.op("vector",
                            lambda en, ss=s, oo=op_:
                                en.tensor_scalar(outb[oo][:], r2[:, ss * D:(ss + 1) * D],
                                                 mst2[:, ss:ss + 1], rstd2[:, ss:ss + 1],
                                                 ALU.subtract, ALU.mult),
                            waits=w, inc="cv")
                out_store[op_] = (f"DOS{op_}", p.op("scalar",
                                      lambda en, ss=s, oo=op_, tt0=t0: en.dma_start(
                                          out=out[tt0 + ss * 128:tt0 + (ss + 1) * 128, :],
                                          in_=outb[oo][:]),
                                      waits=[("cv", cnrm)], inc=f"DOS{op_}", amt=16))

        emit_inputs(0)
        emit_attn_body(0)
        emit_x1fm(0)
        if ntiles > 1:
            if ntiles > 2:
                conv_x(2)
            if ntiles > 3:
                conv_x(3)
            emit_inputs(1)
            emit_attn_body(1)
        for _i in range(1, ntiles):
            emit_ffn_gateup(_i - 1)
            if _i + 1 < ntiles:
                if _i + 3 < ntiles:
                    conv_x(_i + 3)
                emit_inputs(_i + 1)
            emit_x1fm(_i)
            emit_ffn_down(_i - 1)
            if _i + 1 < ntiles:
                emit_attn_body(_i + 1)
        emit_ffn_gateup(ntiles - 1)
        emit_ffn_down(ntiles - 1)

        if debug:
            fin00 = dict(p.cnt)
            for nm, srcap, shp in [("x1scr", x1_scr, [TT, D]), ("xbfscr", xbf_scr, [TT, D])]:
                dout = nc.dram_tensor("dbg_" + nm, shp, BF16, kind="ExternalOutput").ap()
                p.op("gpsimd",
                     lambda en, d=dout, s=srcap, sh=shp: en.dma_start(out=d, in_=s[0:sh[0], :]),
                     waits=[(s_, v_) for s_, v_ in fin00.items() if v_ > 0],
                     inc="dS", amt=16)
            dbg_specs = [("qp", qp, BF16), ("kp", kp, BF16), ("vtm", vtm, BF16),
                         ("gfm", gfm, BF16), ("stm", stm, BF16), ("gof", gof, BF16),
                         ("r1", r1, BF16), ("x1b0", x1b[0], BF16), ("hb", hb, BF16),
                         ("r2", r2, BF16), ("xfm", xfm2[0], BF16), ("x1fm", x1fm2[0], BF16),
                         ("dft", DFt, BF16), ("dfi", DFI, BF16), ("xbf", xbf, BF16),
                         ("mstd", mst, F32), ("rstdd", rstd, F32)]
            fin0 = dict(p.cnt)
            w0 = [("ct", fin0["ct"]), ("cv", fin0["cv"]), ("cs", fin0["cs"])]
            for name, buf, dt in dbg_specs:
                shape = [buf.shape[0] if hasattr(buf, "shape") else 128, buf[:].shape[1]]
                dout = nc.dram_tensor("dbg_" + name, list(buf[:].shape), dt,
                                      kind="ExternalOutput").ap()
                p.op("sync", lambda en, d=dout, b=buf: en.dma_start(out=d, in_=b[:]),
                     waits=w0, inc="dS", amt=16)
                w0 = []

        fin = dict(p.cnt)
        p.op("gpsimd", lambda en: en.nop(),
             waits=[(s_, v_) for s_, v_ in fin.items() if v_ > 0])

        for _e, _ops in p.ops.items():
            for _w, _f, _i, _a in _ops:
                for _s, _v in _w:
                    assert _v <= p.cnt.get(_s, 0), (_e, _s, _v, p.cnt.get(_s, 0))
        _pos = {e_: 0 for e_ in p.ops}
        _cnt = {s_: 0 for s_ in p.cnt}
        _prog = True
        while _prog:
            _prog = False
            for _e, _ops in p.ops.items():
                while _pos[_e] < len(_ops):
                    _w, _f, _i, _a = _ops[_pos[_e]]
                    if all(_cnt.get(s_, 0) >= v_ for s_, v_ in _w):
                        if _i:
                            _cnt[_i] = _cnt.get(_i, 0) + _a
                        _pos[_e] += 1
                        _prog = True
                    else:
                        break
        assert all(_pos[e_] == len(p.ops[e_]) for e_ in p.ops), \
            {e_: (_pos[e_], len(p.ops[e_])) for e_ in p.ops if _pos[e_] < len(p.ops[e_])}

        sems = {s_: st.enter_context(nc.semaphore(s_)) for s_ in p.cnt}
        block = st.enter_context(nc.Block())

        def runner(ename):
            def f(eng):
                for waits, fn, inc, amt in p.ops[ename]:
                    for s_name, v in waits:
                        eng.wait_ge(sems[s_name], v)
                    ins = fn(eng)
                    if inc is not None:
                        ins.then_inc(sems[inc], amt)
            return f

        block.tensor(runner("tensor"))
        block.vector(runner("vector"))
        block.scalar(runner("scalar"))
        block.sync(runner("sync"))
        block.gpsimd(runner("gpsimd"))

    return nc


_NC_CACHE = {}


def _get_nc(ntiles=NTILES):
    if ntiles not in _NC_CACHE:
        _NC_CACHE[ntiles] = build_nc(ntiles)
    return _NC_CACHE[ntiles]


def _consts():
    import ml_dtypes
    tmod = np.tile((np.arange(TT) % 32).astype(np.float32), (128, 1))
    k = np.arange(128)
    c = np.arange(128)
    maskt = ((k[:, None] // 32 == c[None, :] // 32) & (c[None, :] >= k[:, None])
             ).astype(ml_dtypes.bfloat16)
    ident = np.eye(128).astype(ml_dtypes.bfloat16)
    return np.ascontiguousarray(tmod), np.ascontiguousarray(maskt), np.ascontiguousarray(ident)


def kernel(L0_attn_x, L0_attn_W_q, L0_attn_W_k, L0_attn_W_v, L0_attn_W_g, L0_attn_W_o,
           L0_attn_log_decay, L0_ffn_w_gate, L0_ffn_w_up, L0_ffn_w_down):
    x = np.ascontiguousarray(np.asarray(L0_attn_x, dtype=np.float32)).reshape(-1, D)
    tmod, maskt, ident = _consts()
    ld0 = np.ascontiguousarray(np.asarray(L0_attn_log_decay, np.float32)[0:1].reshape(1, 1))
    common = dict(
        wq=np.ascontiguousarray(np.asarray(L0_attn_W_q, np.float32)),
        wk=np.ascontiguousarray(np.asarray(L0_attn_W_k, np.float32)),
        wv=np.ascontiguousarray(np.asarray(L0_attn_W_v, np.float32)),
        wgt=np.ascontiguousarray(np.asarray(L0_attn_W_g, np.float32)),
        wo=np.ascontiguousarray(np.asarray(L0_attn_W_o, np.float32)),
        fg=np.ascontiguousarray(np.asarray(L0_ffn_w_gate, np.int32)),
        fu=np.ascontiguousarray(np.asarray(L0_ffn_w_up, np.int32)),
        fd=np.ascontiguousarray(np.asarray(L0_ffn_w_down, np.int32)),
        ld0=ld0, tmod=tmod, maskt=maskt, ident=ident,
    )
    in_maps = []
    for c_ in range(NCORES):
        m = dict(common)
        m["x"] = np.ascontiguousarray(x[c_ * NTOK:(c_ + 1) * NTOK])
        in_maps.append(m)
    nc = _get_nc()
    res = run_bass_kernel_spmd(nc, in_maps, core_ids=list(range(NCORES)))
    outs = [res.results[c_]["out"] for c_ in range(NCORES)]
    full = np.concatenate(outs, axis=0).reshape(np.asarray(L0_attn_x).shape)
    return full.astype(np.float32)


# version 19
# speedup vs baseline: 1.0012x; 1.0012x over previous
import numpy as np
import concourse.bass as bass
import concourse.mybir as mybir
from concourse.bass_utils import run_bass_kernel_spmd
from contextlib import ExitStack

F32 = mybir.dt.float32
BF16 = mybir.dt.bfloat16
I32 = mybir.dt.int32
AF = mybir.ActivationFunctionType
ALU = mybir.AluOpType
AX = mybir.AxisListType

NCORES = 8
NTOK = 4096
TT = 512
NTILES = NTOK // TT
NS = TT // 128
D = 1024
DC = D // 128
H = 3072
HC = H // 128
EPS = 1e-5
QSC = 1.0 / 127.0

ENGINES = ("tensor", "vector", "scalar", "sync", "gpsimd")


class Plan:
    def __init__(self):
        from collections import defaultdict
        self.ops = {e: [] for e in ENGINES}
        self.cnt = defaultdict(int)

    def op(self, eng, fn, waits=(), inc=None, amt=1):
        w = {}
        for s, v in waits:
            if v:
                w[s] = max(w.get(s, 0), v)
        self.ops[eng].append((sorted(w.items()), fn, inc, amt))
        if inc is not None:
            self.cnt[inc] += amt
            return self.cnt[inc]
        return None


class Stream:

    def __init__(self, plan, name, buf, nslots, load_fn, queue="sync"):
        self.plan = plan
        self.queue = queue
        self.name = name
        self.buf = buf
        self.nslots = nslots
        self.load_fn = load_fn
        self.src_wait = []
        self.src_wait_fn = None
        self.slot_free = [0] * nslots
        self.loaded = {}
        self.next_g = 0

    def load(self, g):
        assert g == self.next_g, (g, self.next_g)
        self.next_g += 1
        slot = g % self.nslots
        sem = f"L{self.name}{slot}"
        sw = self.src_wait_fn(g) if self.src_wait_fn else self.src_wait
        waits = list(sw) + [("ct", self.slot_free[slot])]
        cnt = self.plan.op(self.queue, self.load_fn(g, slot), waits=waits, inc=sem, amt=16)
        self.loaded[g] = (sem, cnt)

    def slot(self, g):
        return g % self.nslots

    def consume(self, g, ct_cnt):
        self.slot_free[g % self.nslots] = ct_cnt


def build_nc(ntiles=NTILES, debug=False):
    nc = bass.Bass("TRN2", target_bir_lowering=False)
    p = Plan()

    xin = nc.dram_tensor("x", [NTOK, D], F32, kind="ExternalInput").ap()
    wq = nc.dram_tensor("wq", [D, D], F32, kind="ExternalInput").ap()
    wk = nc.dram_tensor("wk", [D, D], F32, kind="ExternalInput").ap()
    wv = nc.dram_tensor("wv", [D, D], F32, kind="ExternalInput").ap()
    wgt = nc.dram_tensor("wgt", [D, D], F32, kind="ExternalInput").ap()
    wo = nc.dram_tensor("wo", [D, D], F32, kind="ExternalInput").ap()
    fg = nc.dram_tensor("fg", [H, D], I32, kind="ExternalInput").ap()
    fu = nc.dram_tensor("fu", [H, D], I32, kind="ExternalInput").ap()
    fd = nc.dram_tensor("fd", [D, H], I32, kind="ExternalInput").ap()
    ld0 = nc.dram_tensor("ld0", [1, 1], F32, kind="ExternalInput").ap()
    ident = nc.dram_tensor("ident", [128, 128], BF16, kind="ExternalInput").ap()
    tmod = nc.dram_tensor("tmod", [128, TT], F32, kind="ExternalInput").ap()
    maskt = nc.dram_tensor("maskt", [128, 128], BF16, kind="ExternalInput").ap()
    out = nc.dram_tensor("out", [NTOK, D], F32, kind="ExternalOutput").ap()

    xbf_scr = nc.dram_tensor("xbf_scr", [NTOK, D], BF16).ap()
    x1_scr = nc.dram_tensor("x1_scr", [NTOK, D], BF16).ap()
    t2 = {"q": nc.dram_tensor("t2_q", [DC * DC * 128 * 128], BF16),
          "k": nc.dram_tensor("t2_k", [DC * DC * 128 * 128], BF16),
          "g": nc.dram_tensor("t2_g", [DC * DC * 128 * 128], BF16),
          "v": nc.dram_tensor("t2_v", [DC * 128 * 1024], BF16),
          "o": nc.dram_tensor("t2_o", [DC * 128 * 1024], BF16),
          "fd": nc.dram_tensor("t2_fd", [HC * 128 * 1024], BF16),
          "fg": nc.dram_tensor("t2_fg", [HC * DC * 128 * 128], BF16),
          "fu": nc.dram_tensor("t2_fu", [HC * DC * 128 * 128], BF16)}

    st = ExitStack()
    with st:
        def sb(name, shape, dt, **kw):
            return st.enter_context(nc.sbuf_tensor(name, shape, dt, **kw))

        DFt = sb("DF", [128, TT], BF16)
        DFI = sb("DFI", [128, TT], BF16)
        MSK = sb("MSK", [128, 128], BF16)
        sld = sb("sld", [1, 1], F32)
        ones = sb("ones", [1, 128], F32)
        lvec = sb("lvec", [128, 1], F32)
        epsT = sb("epsT", [128, 1], F32)
        tmpd = sb("tmpd", [128, TT], F32)

        xbf = sb("xbf", [128, NS * D], BF16)
        xfm2 = [sb(f"xfm{i}", [128, DC * TT], BF16, is_transpose_dest=True) for i in range(2)]
        qp = sb("qp", [128, DC * TT], BF16)
        kp = sb("kp", [128, DC * TT], BF16)
        vtm = sb("vtm", [128, NS * D], BF16)
        gfm = sb("gfm", [128, DC * TT], BF16)
        gof = sb("gof", [128, NS * D], BF16)
        stm = sb("stm", [128, NS * 128], BF16)
        r1 = sb("r1", [128, NS * D], BF16)
        x1b = [sb(f"x1b{i}", [128, NS * D], BF16) for i in range(2)]
        x1fm2 = [sb(f"x1fm{i}", [128, DC * TT], BF16, is_transpose_dest=True) for i in range(2)]
        sgate = sb("sgate", [128, 2 * TT], BF16)
        sup = sb("sup", [128, 2 * TT], BF16)
        hb = sb("hb", [128, HC * TT], BF16, is_transpose_dest=True)
        r2 = sb("r2", [128, NS * D], BF16)
        outb = [sb(f"outb{i}", [128, D], F32) for i in range(2)]
        junk = sb("junk", [128, D], BF16)

        sum1 = sb("sum1", [128, NS], F32)
        sq1 = sb("sq1", [128, NS], F32)
        mst = sb("mst", [128, NS], F32)
        msq = sb("msq", [128, NS], F32)
        vart = sb("vart", [128, NS], F32)
        rstd = sb("rstd", [128, NS], F32)
        sum2 = sb("sum2", [128, NS], F32)
        sq2 = sb("sq2", [128, NS], F32)
        mst2 = sb("mst2", [128, NS], F32)
        msq2 = sb("msq2", [128, NS], F32)
        vart2 = sb("vart2", [128, NS], F32)
        rstd2 = sb("rstd2", [128, NS], F32)

        bufq = sb("bufq", [128, 2 * 1024], BF16)
        bufk = sb("bufk", [128, 2 * 1024], BF16)
        bufg = sb("bufg", [128, 2 * 1024], BF16)
        bufv = sb("bufv", [128, 2 * 1024], BF16, is_transpose_dest=True)
        stgi = [sb(f"stgi{i}", [128, 1024], I32) for i in range(2)]
        xcb = [sb("xcb0", [128, 1024], BF16)]
        identS = sb("identS", [128, 128], BF16)
        tev = [sb(f"tev{i}", [128, 1024], BF16) for i in range(2)]
        bufo = sb("bufo", [128, 3 * 1024], BF16, is_transpose_dest=True)
        buffg = sb("buffg", [128, 2 * 1024], BF16)
        buffu = sb("buffu", [128, 2 * 1024], BF16)
        buffd = sb("buffd", [128, 3 * 1024], BF16, is_transpose_dest=True)

        ps = [st.enter_context(nc.psum_tensor(f"ps{i}", [128, 1024], F32)) for i in range(4)]

        def bank_ap(b, cols=512):
            return ps[b // 2][:, (b % 2) * 512:(b % 2) * 512 + cols]

        def pair_ap(pi):
            return ps[pi]

        bank_free = {b: ("ct", 0) for b in range(8)}
        _bkcur = [0]

        def grab(n=1):
            if n == 2 and _bkcur[0] % 2:
                _bkcur[0] += 1
            ids = [(_bkcur[0] + k) % 8 for k in range(n)]
            _bkcur[0] += n
            return ids

        def take(b):
            return [bank_free[b]]

        def free(b, sem, cnt):
            bank_free[b] = (sem, cnt)

        cast_cnt = {}
        cast_x = [f"x{i}c" for i in range(ntiles)]

        p.op("gpsimd", lambda e: e.memset(ones[:], 1.0))
        p.op("gpsimd", lambda e: e.memset(epsT[:], EPS), inc="cp")

        conv_pl = [None, None]
        conv_hb = {}
        W_LANES = [(f"W{k}", hb[:, 6144 + k * 1024: 6144 + (k + 1) * 1024]) for k in range(4)]
        X_LANES = [("X0", xcb[0][:])]
        _st = dict(pl=0, xl=0, wl=0, tev=0)
        tev_free = [None, None]
        t2_done = {}

        def _load_chunk(src_ap, r, cc, is_f32, waits):
            slot = _st["pl"] % 2
            _st["pl"] += 1
            sstg = stgi[slot][:].bitcast(F32) if is_f32 else stgi[slot][:]
            lw = list(waits)
            if conv_pl[slot]:
                lw.append(conv_pl[slot])
            lc = p.op("sync",
                      lambda en, s_=src_ap, rr=r, c_=cc, d_=sstg:
                          en.dma_start(out=d_, in_=s_[rr * 128:(rr + 1) * 128,
                                                      c_ * 1024:(c_ + 1) * 1024]),
                      waits=lw, inc=f"PL{slot}", amt=16)
            return slot, sstg, lc

        def conv_x(i):
            for r in range(NS):
                slot, sstg, lc = _load_chunk(xin[i * TT:(i + 1) * TT, :], r, 0, True, [])
                lane, lap = X_LANES[_st["xl"] % len(X_LANES)]
                _st["xl"] += 1
                hw = [(f"PL{slot}", lc)]
                if conv_hb.get(lane):
                    hw.append(conv_hb[lane])
                cc_ = p.op("gpsimd",
                           lambda en, la=lap, s_=sstg: en.tensor_copy(la, s_),
                           waits=hw, inc="cp")
                conv_pl[slot] = ("cp", cc_)
                sc_ = p.op("scalar",
                           lambda en, ii=i, rr=r, la=lap:
                               en.dma_start(out=xbf_scr[ii * TT + rr * 128:
                                                        ii * TT + (rr + 1) * 128, :], in_=la),
                           waits=[("cp", cc_)], inc="S" + lane, amt=16)
                conv_hb[lane] = ("S" + lane, sc_)
            cast_cnt[f"x{i}c"] = [("SX0", p.cnt["SX0"])]

        wdone = {}

        def wconv_chunk(name, src_ap, is_f32, dst, style, r, cg, lanes=None,
                        first_gate=()):
                    lanes = lanes or W_LANES
                    slot, sstg, lc = _load_chunk(src_ap, r, cg, is_f32, [])
                    lane, lap = lanes[_st["wl"] % len(lanes)]
                    _st["wl"] += 1
                    hw = [(f"PL{slot}", lc)]
                    if conv_hb.get(lane):
                        hw.append(conv_hb[lane])
                    else:
                        hw.extend(first_gate)
                    cc_ = p.op("vector",
                               lambda en, la=lap, s_=sstg: en.tensor_copy(la, s_),
                               waits=hw, inc="cv")
                    conv_pl[slot] = ("cv", cc_)
                    b = grab()[0]
                    bw = take(b)
                    tc = None
                    for c in range(DC):
                        tc = p.op("tensor",
                                  lambda en, bb=b, la=lap, c_=c:
                                      en.transpose(
                                          bank_ap(bb).bitcast(BF16)[:, c_ * 128:(c_ + 1) * 128],
                                          la[:, c_ * 128:(c_ + 1) * 128], identS[:]),
                                  waits=([("cv", cc_), ("dS", c_idt)] + bw) if c == 0 else (),
                                  inc="ct" if c == DC - 1 else None)
                    conv_hb[lane] = ("ct", tc)
                    ts_ = _st["tev"] % 2
                    _st["tev"] += 1
                    ew = [("ct", tc)]
                    if tev_free[ts_]:
                        ew.append(tev_free[ts_])
                    ev = p.op("vector",
                              lambda en, bb=b, t_=ts_:
                                  en.tensor_copy(tev[t_][:], bank_ap(bb).bitcast(BF16)),
                              waits=ew, inc="cv")
                    free(b, "cv", ev)
                    if style == "lhsT":
                        dap = bass.AP(dst, r * DC * 16384, [[1024, 128], [1, 1024]])
                    else:
                        dap = bass.AP(dst, cg * DC * 131072 + r * 128,
                                      [[1024, 128], [131072, DC], [1, 128]])
                    ssem = f"TV{ts_}"
                    sc_ = p.op("scalar",
                               lambda en, da=dap, t_=ts_, st_=style:
                                   en.dma_start(out=da, in_=tev[t_][:] if st_ == "lhsT"
                                                else tev[t_][:].rearrange("p (c l) -> p c l",
                                                                          l=128)),
                               waits=[("cv", ev)], inc=ssem, amt=16)
                    tev_free[ts_] = (ssem, sc_)
                    wdone.setdefault(name, []).append((ssem, sc_))

        def wconv(name, src_ap, R, C, is_f32, dst, style, cg_outer=False):
            if cg_outer:
                for cg in range(C // 1024):
                    for r in range(R // 128):
                        wconv_chunk(name, src_ap, is_f32, dst, style, r, cg)
            else:
                for r in range(R // 128):
                    for cg in range(C // 1024):
                        wconv_chunk(name, src_ap, is_f32, dst, style, r, cg)
            t2_done[name] = [("TV0", p.cnt["TV0"]), ("TV1", p.cnt["TV1"])]
            cast_cnt[name] = t2_done[name]

        p.op("sync", lambda e: e.dma_start(out=sld[:], in_=ld0), inc="dS", amt=16)
        p.op("sync", lambda e: e.dma_start(out=tmpd[:], in_=tmod), inc="dS", amt=16)
        p.op("sync", lambda e: e.dma_start(out=MSK[:], in_=maskt), inc="dS", amt=16)
        c_idt = p.op("sync", lambda e: e.dma_start(out=identS[:], in_=ident), inc="dS", amt=16)
        c_con = c_idt
        c_sld = c_tmd = c_msk = c_con

        c_bc = p.op("tensor", lambda e: e.matmul(ps[0][:, 0:1], ones[:], sld[:],
                                                 start=True, stop=True),
                    waits=[("dS", c_sld), ("cp", 1)], inc="ct")
        c_sg = p.op("scalar", lambda e: e.activation(lvec[:], ps[0][:, 0:1], AF.Sigmoid),
                    waits=[("ct", c_bc)], inc="cs")
        c_ln = p.op("scalar", lambda e: e.activation(lvec[:], lvec[:], AF.Ln),
                    waits=[("cs", c_sg)], inc="cs")
        c_tm = p.op("vector", lambda e: e.tensor_scalar(tmpd[:], tmpd[:], lvec[:, 0:1], None,
                                                        ALU.mult),
                    waits=[("cs", c_ln), ("dS", c_tmd)], inc="cv")
        p.op("scalar", lambda e: e.activation(DFt[:], tmpd[:], AF.Exp),
             waits=[("cv", c_tm)], inc="cs")
        c_dfi = p.op("scalar", lambda e: e.activation(DFI[:], tmpd[:], AF.Exp, scale=-1.0),
                     inc="cs")
        free(0, "cs", c_sg)

        wconv("wv", wv, D, D, True, t2["v"], "rhs")
        wconv("wq", wq, D, D, True, t2["q"], "lhsT")
        conv_x(0)
        if ntiles > 1:
            conv_x(1)

        def mk_scr2_loader(t2t, npt, buf):
            def mk(g, slot):
                j = g % npt
                def fn(e):
                    src = bass.AP(t2t, j * DC * 16384, [[128, 128], [16384, DC], [1, 128]])
                    dst = buf[:, slot * 1024:(slot + 1) * 1024].rearrange(
                        "p (e l) -> p e l", l=128)
                    return e.dma_start(out=dst, in_=src)
                return fn
            return mk

        def mk_piece_loader(t2t, npt, buf):
            def mk(g, slot):
                j = g % npt
                def fn(e):
                    src = bass.AP(t2t, j * 131072, [[1024, 128], [1, 1024]])
                    return e.dma_start(out=buf[:, slot * 1024:(slot + 1) * 1024], in_=src)
                return fn
            return mk

        MQ = Stream(p, "q", bufq, 2, mk_piece_loader(t2["q"], DC, bufq))
        MQ.src_wait_fn = lambda g: [wdone["wq"][g % DC]]
        MK = Stream(p, "k", bufk, 2, mk_piece_loader(t2["k"], DC, bufk))
        MK.src_wait_fn = lambda g: [wdone["wk"][g % DC]]
        MG = Stream(p, "g", bufg, 2, mk_piece_loader(t2["g"], DC, bufg))
        MG.src_wait_fn = lambda g: [wdone["wg"][g % DC]]
        MV = Stream(p, "v", bufv, 2, mk_piece_loader(t2["v"], DC, bufv))
        MV.src_wait_fn = lambda g: [wdone["wv"][-1]]
        MO = Stream(p, "o", bufo, 3, mk_piece_loader(t2["o"], DC, bufo))
        MO.src_wait_fn = lambda g: [wdone["wo"][-1]]
        MFG = Stream(p, "G", buffg, 2, mk_piece_loader(t2["fg"], HC, buffg))
        MFG.src_wait_fn = lambda g: [wdone["fg"][g % HC]]
        MFU = Stream(p, "U", buffu, 2, mk_piece_loader(t2["fu"], HC, buffu))
        MFU.src_wait_fn = lambda g: [wdone["fu"][g % HC]]
        MFD = Stream(p, "D", buffd, 3, mk_piece_loader(t2["fd"], HC, buffd))
        MFD.src_wait_fn = lambda g: [wdone["fd"][((g % HC) // DC) * DC + DC - 1]]

        state = dict(xfm_free0=0, xfm_free1=0, x1fm_free0=0, x1fm_free1=0, xbf_free=0,
                     qp_free=0, kp_free=0, r1_free_cs=0)
        x1b_store = [None, None]
        out_store = [None, None]

        def emit_inputs(ti):
            par = ti % 2
            t0 = ti * TT
            xfm = xfm2[par]
            xfm_loads = None
            for e in range(DC):
                w = (list(cast_cnt[cast_x[ti]]) + [("ct", state[f"xfm_free{par}"])]) if e == 0 else ()
                cnt = p.op("scalar",
                           lambda en, ee=e, tt0=t0, xf=xfm: en.dma_start(
                               out=xf[:, ee * TT:(ee + 1) * TT],
                               in_=xbf_scr[tt0:tt0 + TT, ee * 128:(ee + 1) * 128],
                               transpose=True),
                           waits=w, inc=f"DXF{par}", amt=16)
                xfm_loads = (f"DXF{par}", cnt)
            cnt = p.op("scalar",
                       lambda en, tt0=t0: en.dma_start(
                           out=xbf[:].rearrange("p (s d) -> p s d", d=D),
                           in_=bass.AP(xbf_scr.tensor, tt0 * D,
                                       [[D, 128], [128 * D, NS], [1, D]])),
                       waits=list(cast_cnt[cast_x[ti]]) + [("cv", state["xbf_free"])],
                       inc="DXB", amt=16)
            state[f"inp_{ti}"] = (xfm_loads, ("DXB", cnt))

        def emit_attn_body(ti):
            par = ti % 2
            t0 = ti * TT
            xfm = xfm2[par]
            xfm_loads, xbf_load = state[f"inp_{ti}"]

            q_evac_last = k_evac_last = 0
            for which, M, dst, dfap, freekey in (("q", MQ, qp, DFt, "qp_free"),
                                                 ("k", MK, kp, DFI, "kp_free")):
                base_g = M.next_g
                M.load(base_g)
                M.load(base_g + 1)
                gc = None
                for j in range(DC):
                    g = base_g + j
                    b = grab()[0]
                    slot = M.slot(g)
                    waits = [M.loaded[g], xfm_loads] + take(b)
                    for e in range(DC):
                        gc = p.op("tensor",
                                  lambda en, bb=b, sl=slot, ee=e, Mb=M.buf, xf=xfm:
                                      en.matmul(bank_ap(bb),
                                                Mb[:, sl * 1024 + ee * 128:sl * 1024 + (ee + 1) * 128],
                                                xf[:, ee * TT:(ee + 1) * TT],
                                                start=(ee == 0), stop=(ee == DC - 1)),
                                  waits=waits if e == 0 else (),
                                  inc="ct" if e == DC - 1 else None)
                    M.consume(g, gc)
                    if g + 2 < base_g + DC:
                        M.load(g + 2)
                    if ti == 0:
                        if which == "q":
                            wconv_chunk("wk", wk, True, t2["k"], "lhsT", j, 0)
                        else:
                            wconv_chunk("wg", wgt, True, t2["g"], "lhsT", j, 0)
                    ew = [("ct", gc), ("cs", c_dfi)]
                    if j == 0:
                        ew.append(("ct", state[freekey]))
                    ec = p.op("vector",
                              lambda en, bb=b, jj=j, dd=dst, df=dfap:
                                  en.tensor_tensor(dd[:, jj * TT:(jj + 1) * TT],
                                                   bank_ap(bb), df[:], ALU.mult),
                              waits=ew, inc="cv")
                    free(b, "cv", ec)
                    if which == "q":
                        q_evac_last = ec
                    else:
                        k_evac_last = ec

            v_evac_last = 0
            for sub in range(1):
                groups = [(s, h) for s in range(NS) for h in range(2)]
                gbanks = [grab()[0] for _ in groups]
                base_g = MV.next_g
                MV.load(base_g)
                MV.load(base_g + 1)
                gct = None
                for e in range(DC):
                    g = base_g + e
                    slot = MV.slot(g)
                    for gi, (s, h) in enumerate(groups):
                        b = gbanks[gi]
                        if e == 0:
                            waits = [MV.loaded[g], xfm_loads] + take(b)
                        elif gi == 0:
                            waits = [MV.loaded[g]]
                        else:
                            waits = ()
                        gct = p.op("tensor",
                                   lambda en, bb=b, ss=s, hh=h, ee=e, sl=slot, xf=xfm:
                                       en.matmul(bank_ap(bb),
                                                 xf[:, ee * TT + ss * 128:ee * TT + (ss + 1) * 128],
                                                 bufv[:, sl * 1024 + hh * 512:sl * 1024 + (hh + 1) * 512],
                                                 start=(ee == 0), stop=(ee == DC - 1)),
                                   waits=waits,
                                   inc="ct" if gi == len(groups) - 1 else None)
                    MV.consume(g, gct)
                    if e + 2 < DC:
                        MV.load(g + 2)
                for gi, (s, h) in enumerate(groups):
                    b = gbanks[gi]
                    ec = p.op("vector",
                              lambda en, bb=b, ss=s, hh=h:
                                  en.tensor_copy(vtm[:, ss * D + hh * 512:ss * D + (hh + 1) * 512],
                                                 bank_ap(bb)),
                              waits=[("ct", gct)], inc="cv")
                    free(b, "cv", ec)
                    v_evac_last = ec

            g_evac_last = 0
            base_g = MG.next_g
            MG.load(base_g)
            MG.load(base_g + 1)
            gc = None
            for j in range(DC):
                g = base_g + j
                b = grab()[0]
                slot = MG.slot(g)
                waits = [MG.loaded[g], xfm_loads] + take(b)
                for e in range(DC):
                    gc = p.op("tensor",
                              lambda en, bb=b, sl=slot, ee=e, xf=xfm:
                                  en.matmul(bank_ap(bb),
                                            bufg[:, sl * 1024 + ee * 128:sl * 1024 + (ee + 1) * 128],
                                            xf[:, ee * TT:(ee + 1) * TT],
                                            start=(ee == 0), stop=(ee == DC - 1)),
                              waits=waits if e == 0 else (),
                              inc="ct" if e == DC - 1 else None)
                MG.consume(g, gc)
                if ti == 0:
                    wconv_chunk("wo", wo, True, t2["o"], "rhs", j, 0)
                if g + 2 < base_g + DC:
                    MG.load(g + 2)
                ec = p.op("scalar",
                          lambda en, bb=b, jj=j:
                              en.activation(gfm[:, jj * TT:(jj + 1) * TT], bank_ap(bb),
                                            AF.Sigmoid),
                          waits=[("ct", gc)], inc="cs")
                free(b, "cs", ec)
                g_evac_last = ec
            xfm_free_new = gc

            s_evac = []
            gc = None
            for s in range(NS):
                b = grab()[0]
                waits = [("cv", max(q_evac_last, k_evac_last))] + take(b)
                for e in range(DC):
                    gc = p.op("tensor",
                              lambda en, bb=b, ss=s, ee=e:
                                  en.matmul(bank_ap(bb, 128),
                                            kp[:, ee * TT + ss * 128:ee * TT + (ss + 1) * 128],
                                            qp[:, ee * TT + ss * 128:ee * TT + (ss + 1) * 128],
                                            start=(ee == 0), stop=(ee == DC - 1)),
                              waits=waits if e == 0 else (),
                              inc="ct" if e == DC - 1 else None)
                ec = p.op("vector",
                          lambda en, bb=b, ss=s:
                              en.tensor_tensor(stm[:, ss * 128:(ss + 1) * 128],
                                               bank_ap(bb, 128), MSK[:], ALU.mult),
                          waits=[("ct", gc), ("dS", c_msk)], inc="cv")
                free(b, "cv", ec)
                s_evac.append(ec)
            s_phase_end = gc

            gof_evac = []
            gc = None
            for s in range(NS):
                pids = grab(2)
                pi = pids[0] // 2
                waits = [("cv", v_evac_last), ("cv", s_evac[s])] + take(pids[0]) + take(pids[1])
                for j in range(DC):
                    gc = p.op("tensor",
                              lambda en, pp=pi, ss=s, jj=j:
                                  en.matmul(pair_ap(pp)[:, jj * 128:(jj + 1) * 128],
                                            vtm[:, ss * D + jj * 128:ss * D + (jj + 1) * 128],
                                            stm[:, ss * 128:(ss + 1) * 128],
                                            start=True, stop=True),
                              waits=waits if j == 0 else (),
                              inc="ct" if j == DC - 1 else None)
                ec = p.op("vector",
                          lambda en, pp=pi, ss=s:
                              en.tensor_tensor(
                                  gof[:, ss * D:(ss + 1) * D].rearrange("p (j l) -> p j l", l=128),
                                  pair_ap(pp).rearrange("p (j l) -> p j l", l=128),
                                  gfm.rearrange("p (j t) -> p j t", t=TT)[:, :, ss * 128:(ss + 1) * 128],
                                  ALU.mult),
                          waits=[("ct", gc), ("cs", g_evac_last)], inc="cv")
                free(pids[0], "cv", ec)
                free(pids[1], "cv", ec)
                gof_evac.append(ec)
            o_phase_end = gc

            r1_evac_last = 0
            wgroups = [(s, h) for s in range(NS) for h in range(2)]
            wbanks = [grab()[0] for _ in wgroups]
            base_g = MO.next_g
            for la in range(2):
                MO.load(base_g + la)
            gct = None
            for e in range(DC):
                g = base_g + e
                slot = MO.slot(g)
                for gi, (s, h) in enumerate(wgroups):
                    b = wbanks[gi]
                    if e == 0:
                        waits = [MO.loaded[g], ("cv", gof_evac[NS - 1])] + take(b)
                    elif gi == 0:
                        waits = [MO.loaded[g]]
                    else:
                        waits = ()
                    gct = p.op("tensor",
                               lambda en, bb=b, ss=s, ee=e, sl=slot, hh=h:
                                   en.matmul(bank_ap(bb),
                                             gof[:, (ss * DC + ee) * 128:(ss * DC + ee + 1) * 128],
                                             bufo[:, sl * 1024 + hh * 512:sl * 1024 + (hh + 1) * 512],
                                             start=(ee == 0), stop=(ee == DC - 1)),
                               waits=waits,
                               inc="ct" if gi == len(wgroups) - 1 else None)
                MO.consume(g, gct)
                if e + 2 < DC:
                    MO.load(g + 2)
            for gi, (s, h) in enumerate(wgroups):
                b = wbanks[gi]
                ec = p.op("vector",
                          lambda en, bb=b, ss=s, hh=h:
                              en.scalar_tensor_tensor(
                                  r1[:, ss * D + hh * 512:ss * D + (hh + 1) * 512],
                                  xbf[:, ss * D + hh * 512:ss * D + (hh + 1) * 512],
                                  2.0, bank_ap(bb), ALU.mult, ALU.add),
                          waits=[("ct", gct), xbf_load,
                                 ("cs", state["r1_free_cs"])],
                          inc="cv")
                free(b, "cv", ec)
                r1_evac_last = ec

            c_red = p.op("vector",
                         lambda en: en.reduce_sum(sum1[:, 0:NS],
                                                  r1[:].rearrange("p (s d) -> p s d", d=D), AX.X),
                         waits=[("cv", r1_evac_last)], inc="cv")
            sq_last = 0
            for s in range(NS):
                sq_last = p.op("vector",
                               lambda en, ss=s: en.scalar_tensor_tensor(
                                   junk[:], r1[:, ss * D:(ss + 1) * D], 1.0,
                                   r1[:, ss * D:(ss + 1) * D], ALU.mult, ALU.mult,
                                   accum_out=sq1[:, ss:ss + 1]),
                               waits=(), inc="cv")
            c_mean = p.op("vector", lambda en: en.tensor_scalar(mst[:], sum1[:], 1.0 / D, None,
                                                                ALU.mult),
                          waits=[("cv", c_red)], inc="cv")
            c_msq = p.op("vector", lambda en: en.tensor_tensor(msq[:], mst[:], mst[:], ALU.mult),
                         waits=[("cv", c_mean)], inc="cv")
            c_var = p.op("vector",
                         lambda en: en.scalar_tensor_tensor(vart[:], sq1[:], 1.0 / D, msq[:],
                                                            ALU.mult, ALU.subtract),
                         waits=[("cv", sq_last), ("cv", c_msq)], inc="cv")
            c_lnv = p.op("scalar", lambda en: en.activation(vart[:], vart[:], AF.Ln, bias=epsT[:]),
                         waits=[("cv", c_var)], inc="cs")
            c_rstd = p.op("scalar", lambda en: en.activation(rstd[:], vart[:], AF.Exp, scale=-0.5),
                          waits=[("cs", c_lnv)], inc="cs")
            x1r = 0
            for s in range(NS):
                w = [("cs", c_rstd)]
                if s == 0 and x1b_store[par] is not None:
                    w.append(x1b_store[par])
                x1r = p.op("vector",
                           lambda en, ss=s, pp=par:
                               en.tensor_scalar(x1b[pp][:, ss * D:(ss + 1) * D],
                                                r1[:, ss * D:(ss + 1) * D],
                                                mst[:, ss:ss + 1], rstd[:, ss:ss + 1],
                                                ALU.subtract, ALU.mult),
                           waits=w, inc="cv")

            x1b_store[par] = ("DX1S", p.op("scalar",
                                  lambda en, tt0=t0, pp=par: en.dma_start(
                                      out=bass.AP(x1_scr.tensor, tt0 * D,
                                                  [[D, 128], [128 * D, NS], [1, D]]),
                                      in_=x1b[pp][:].rearrange("p (s d) -> p s d", d=D)),
                                  waits=[("cv", x1r)], inc="DX1S", amt=16))

            state[f"xfm_free{par}"] = xfm_free_new
            state["xbf_free"] = r1_evac_last
            state["qp_free"] = s_phase_end
            state["kp_free"] = s_phase_end
            state["r1_free_cs"] = 0

        def emit_x1fm(ti):
            par = ti % 2
            t0 = ti * TT
            x1f = x1fm2[par]
            x1fm_loads = None
            for e in range(DC):
                w = [x1b_store[par], ("ct", state[f"x1fm_free{par}"])] if e == 0 else ()
                cnt = p.op("scalar",
                           lambda en, ee=e, tt0=t0, xf=x1f: en.dma_start(
                               out=xf[:, ee * TT:(ee + 1) * TT],
                               in_=x1_scr[tt0:tt0 + TT, ee * 128:(ee + 1) * 128],
                               transpose=True),
                           waits=w, inc=f"DX1F{par}", amt=16)
                x1fm_loads = (f"DX1F{par}", cnt)
            state[f"x1fm_loads_{ti}"] = x1fm_loads

        def emit_ffn_gateup(ti):
            par = ti % 2
            t0 = ti * TT
            x1fm = x1fm2[par]
            x1fm_loads = state[f"x1fm_loads_{ti}"]
            if ti == 0:
                state["prologue_store_done"] = [("TV0", p.cnt["TV0"]),
                                                ("TV1", p.cnt["TV1"])]

            h_last = 0
            LAG = 8
            if ti == 0:
                F_LANES = [("F0", qp[:, 0:1024]), ("F1", qp[:, 1024:2048]),
                           ("F2", kp[:, 0:1024]), ("F3", kp[:, 1024:2048])]
                fgate = [("ct", state["qp_free"])]
                for jj in range(LAG):
                    wconv_chunk("fg", fg, False, t2["fg"], "lhsT", jj, 0, F_LANES, fgate)
                    wconv_chunk("fu", fu, False, t2["fu"], "lhsT", jj, 0, F_LANES, fgate)
                    wconv_chunk("fd", fd, False, t2["fd"], "rhs", jj % DC, jj // DC,
                                F_LANES, fgate)
            base_fg = MFG.next_g
            base_fu = MFU.next_g
            MFG.load(base_fg)
            MFU.load(base_fu)
            MFG.load(base_fg + 1)
            MFU.load(base_fu + 1)
            sg_free = [0, 0]
            gcu = None
            for j in range(HC):
                if ti == 0 and j + LAG < HC:
                    jj = j + LAG
                    wconv_chunk("fg", fg, False, t2["fg"], "lhsT", jj, 0, F_LANES)
                    wconv_chunk("fu", fu, False, t2["fu"], "lhsT", jj, 0, F_LANES)
                    wconv_chunk("fd", fd, False, t2["fd"], "rhs", jj % DC, jj // DC,
                                F_LANES)
                gg = base_fg + j
                gu = base_fu + j
                bgate = grab()[0]
                bup = grab()[0]
                slotg = MFG.slot(gg)
                slotu = MFU.slot(gu)
                waits = [MFG.loaded[gg], x1fm_loads] + take(bgate)
                gcg = None
                for e in range(DC):
                    gcg = p.op("tensor",
                               lambda en, bb=bgate, sl=slotg, ee=e, x1f_=x1fm:
                                   en.matmul(bank_ap(bb),
                                             buffg[:, sl * 1024 + ee * 128:sl * 1024 + (ee + 1) * 128],
                                             x1f_[:, ee * TT:(ee + 1) * TT],
                                             start=(ee == 0), stop=(ee == DC - 1)),
                               waits=waits if e == 0 else (),
                               inc="ct" if e == DC - 1 else None)
                MFG.consume(gg, gcg)
                waits = [MFU.loaded[gu]] + take(bup)
                for e in range(DC):
                    gcu = p.op("tensor",
                               lambda en, bb=bup, sl=slotu, ee=e, x1f_=x1fm:
                                   en.matmul(bank_ap(bb),
                                             buffu[:, sl * 1024 + ee * 128:sl * 1024 + (ee + 1) * 128],
                                             x1f_[:, ee * TT:(ee + 1) * TT],
                                             start=(ee == 0), stop=(ee == DC - 1)),
                               waits=waits if e == 0 else (),
                               inc="ct" if e == DC - 1 else None)
                MFU.consume(gu, gcu)
                if j + 2 < HC:
                    MFG.load(gg + 2)
                    MFU.load(gu + 2)
                sl2 = j % 2
                e1 = p.op("scalar",
                          lambda en, bb=bgate, s2=sl2:
                              en.activation(sgate[:, s2 * TT:(s2 + 1) * TT], bank_ap(bb),
                                            AF.Silu, scale=QSC),
                          waits=[("ct", gcg), ("cv", sg_free[sl2])], inc="cs")
                free(bgate, "cs", e1)
                e2 = p.op("vector",
                          lambda en, bb=bup, s2=sl2:
                              en.tensor_scalar(sup[:, s2 * TT:(s2 + 1) * TT], bank_ap(bb),
                                               QSC * QSC, None, ALU.mult),
                          waits=[("ct", gcu)], inc="cv")
                free(bup, "cv", e2)
                hw = [("cs", e1), ("cv", e2)]
                if ti == 0 and j == 0:
                    hw.extend(state["prologue_store_done"])
                h_last = p.op("vector",
                              lambda en, jj=j, s2=sl2:
                                  en.tensor_tensor(hb[:, jj * TT:(jj + 1) * TT],
                                                   sgate[:, s2 * TT:(s2 + 1) * TT],
                                                   sup[:, s2 * TT:(s2 + 1) * TT], ALU.mult),
                              waits=hw, inc="cv")
                sg_free[sl2] = h_last

            state[f"hlast_{ti}"] = h_last
            state[f"x1fm_free{par}"] = gcu

        def emit_ffn_down(ti):
            par = ti % 2
            t0 = ti * TT
            h_last = state[f"hlast_{ti}"]
            r2_evac_last = 0
            dgroups = [(s, h) for s in range(NS) for h in range(2)]
            dbanks = [grab()[0] for _ in dgroups]
            base_g = MFD.next_g
            for la in range(2):
                MFD.load(base_g + la)
            gct = None
            for jh in range(HC):
                g = base_g + jh
                slot = MFD.slot(g)
                for gi, (s, h) in enumerate(dgroups):
                    b = dbanks[gi]
                    if jh == 0:
                        waits = [MFD.loaded[g], ("cv", h_last)] + take(b)
                    elif gi == 0:
                        waits = [MFD.loaded[g]]
                    else:
                        waits = ()
                    gct = p.op("tensor",
                               lambda en, bb=b, ss=s, jj=jh, sl=slot, hh=h:
                                   en.matmul(bank_ap(bb),
                                             hb[:, jj * TT + ss * 128:jj * TT + (ss + 1) * 128],
                                             buffd[:, sl * 1024 + hh * 512:sl * 1024 + (hh + 1) * 512],
                                             start=(jj == 0), stop=(jj == HC - 1)),
                               waits=waits,
                               inc="ct" if gi == len(dgroups) - 1 else None)
                MFD.consume(g, gct)
                if jh + 2 < HC:
                    MFD.load(g + 2)
            for gi, (s, h) in enumerate(dgroups):
                b = dbanks[gi]
                ec = p.op("vector",
                          lambda en, bb=b, ss=s, hh=h, pp=par:
                              en.scalar_tensor_tensor(
                                  r2[:, ss * D + hh * 512:ss * D + (hh + 1) * 512],
                                  x1b[pp][:, ss * D + hh * 512:ss * D + (hh + 1) * 512],
                                  1.0, bank_ap(bb), ALU.mult, ALU.add),
                          waits=[("ct", gct)], inc="cv")
                free(b, "cv", ec)
                r2_evac_last = ec

            c_red2 = p.op("vector",
                          lambda en: en.reduce_sum(sum2[:, 0:NS],
                                                   r2[:].rearrange("p (s d) -> p s d", d=D), AX.X),
                          waits=[("cv", r2_evac_last)], inc="cv")
            sq_last2 = 0
            for s in range(NS):
                sq_last2 = p.op("vector",
                                lambda en, ss=s: en.scalar_tensor_tensor(
                                    junk[:], r2[:, ss * D:(ss + 1) * D], 1.0,
                                    r2[:, ss * D:(ss + 1) * D], ALU.mult, ALU.mult,
                                    accum_out=sq2[:, ss:ss + 1]),
                                waits=(), inc="cv")
            c_mean2 = p.op("vector", lambda en: en.tensor_scalar(mst2[:], sum2[:], 1.0 / D, None,
                                                                 ALU.mult),
                           waits=[("cv", c_red2)], inc="cv")
            c_msq2 = p.op("vector", lambda en: en.tensor_tensor(msq2[:], mst2[:], mst2[:], ALU.mult),
                          waits=[("cv", c_mean2)], inc="cv")
            c_var2 = p.op("vector",
                          lambda en: en.scalar_tensor_tensor(vart2[:], sq2[:], 1.0 / D, msq2[:],
                                                             ALU.mult, ALU.subtract),
                          waits=[("cv", sq_last2), ("cv", c_msq2)], inc="cv")
            c_lnv2 = p.op("scalar", lambda en: en.activation(vart2[:], vart2[:], AF.Ln, bias=epsT[:]),
                          waits=[("cv", c_var2)], inc="cs")
            c_rstd2 = p.op("scalar", lambda en: en.activation(rstd2[:], vart2[:], AF.Exp, scale=-0.5),
                           waits=[("cs", c_lnv2)], inc="cs")
            for s in range(NS):
                op_ = s % 2
                w = [("cs", c_rstd2)]
                if out_store[op_] is not None:
                    w.append(out_store[op_])
                cnrm = p.op("vector",
                            lambda en, ss=s, oo=op_:
                                en.tensor_scalar(outb[oo][:], r2[:, ss * D:(ss + 1) * D],
                                                 mst2[:, ss:ss + 1], rstd2[:, ss:ss + 1],
                                                 ALU.subtract, ALU.mult),
                            waits=w, inc="cv")
                out_store[op_] = (f"DOS{op_}", p.op("scalar",
                                      lambda en, ss=s, oo=op_, tt0=t0: en.dma_start(
                                          out=out[tt0 + ss * 128:tt0 + (ss + 1) * 128, :],
                                          in_=outb[oo][:]),
                                      waits=[("cv", cnrm)], inc=f"DOS{op_}", amt=16))

        emit_inputs(0)
        emit_attn_body(0)
        emit_x1fm(0)
        if ntiles > 1:
            if ntiles > 2:
                conv_x(2)
            if ntiles > 3:
                conv_x(3)
            emit_inputs(1)
            emit_attn_body(1)
        for _i in range(1, ntiles):
            emit_ffn_gateup(_i - 1)
            if _i + 1 < ntiles:
                if _i + 3 < ntiles:
                    conv_x(_i + 3)
                emit_inputs(_i + 1)
            emit_x1fm(_i)
            emit_ffn_down(_i - 1)
            if _i + 1 < ntiles:
                emit_attn_body(_i + 1)
        emit_ffn_gateup(ntiles - 1)
        emit_ffn_down(ntiles - 1)

        if debug:
            fin00 = dict(p.cnt)
            for nm, srcap, shp in [("x1scr", x1_scr, [TT, D]), ("xbfscr", xbf_scr, [TT, D])]:
                dout = nc.dram_tensor("dbg_" + nm, shp, BF16, kind="ExternalOutput").ap()
                p.op("gpsimd",
                     lambda en, d=dout, s=srcap, sh=shp: en.dma_start(out=d, in_=s[0:sh[0], :]),
                     waits=[(s_, v_) for s_, v_ in fin00.items() if v_ > 0],
                     inc="dS", amt=16)
            dbg_specs = [("qp", qp, BF16), ("kp", kp, BF16), ("vtm", vtm, BF16),
                         ("gfm", gfm, BF16), ("stm", stm, BF16), ("gof", gof, BF16),
                         ("r1", r1, BF16), ("x1b0", x1b[0], BF16), ("hb", hb, BF16),
                         ("r2", r2, BF16), ("xfm", xfm2[0], BF16), ("x1fm", x1fm2[0], BF16),
                         ("dft", DFt, BF16), ("dfi", DFI, BF16), ("xbf", xbf, BF16),
                         ("mstd", mst, F32), ("rstdd", rstd, F32)]
            fin0 = dict(p.cnt)
            w0 = [("ct", fin0["ct"]), ("cv", fin0["cv"]), ("cs", fin0["cs"])]
            for name, buf, dt in dbg_specs:
                shape = [buf.shape[0] if hasattr(buf, "shape") else 128, buf[:].shape[1]]
                dout = nc.dram_tensor("dbg_" + name, list(buf[:].shape), dt,
                                      kind="ExternalOutput").ap()
                p.op("sync", lambda en, d=dout, b=buf: en.dma_start(out=d, in_=b[:]),
                     waits=w0, inc="dS", amt=16)
                w0 = []

        fin = dict(p.cnt)
        p.op("gpsimd", lambda en: en.nop(),
             waits=[(s_, v_) for s_, v_ in fin.items() if v_ > 0])

        for _e, _ops in p.ops.items():
            for _w, _f, _i, _a in _ops:
                for _s, _v in _w:
                    assert _v <= p.cnt.get(_s, 0), (_e, _s, _v, p.cnt.get(_s, 0))
        _pos = {e_: 0 for e_ in p.ops}
        _cnt = {s_: 0 for s_ in p.cnt}
        _prog = True
        while _prog:
            _prog = False
            for _e, _ops in p.ops.items():
                while _pos[_e] < len(_ops):
                    _w, _f, _i, _a = _ops[_pos[_e]]
                    if all(_cnt.get(s_, 0) >= v_ for s_, v_ in _w):
                        if _i:
                            _cnt[_i] = _cnt.get(_i, 0) + _a
                        _pos[_e] += 1
                        _prog = True
                    else:
                        break
        assert all(_pos[e_] == len(p.ops[e_]) for e_ in p.ops), \
            {e_: (_pos[e_], len(p.ops[e_])) for e_ in p.ops if _pos[e_] < len(p.ops[e_])}

        sems = {s_: st.enter_context(nc.semaphore(s_)) for s_ in p.cnt}
        block = st.enter_context(nc.Block())

        def runner(ename):
            def f(eng):
                for waits, fn, inc, amt in p.ops[ename]:
                    for s_name, v in waits:
                        eng.wait_ge(sems[s_name], v)
                    ins = fn(eng)
                    if inc is not None:
                        ins.then_inc(sems[inc], amt)
            return f

        block.tensor(runner("tensor"))
        block.vector(runner("vector"))
        block.scalar(runner("scalar"))
        block.sync(runner("sync"))
        block.gpsimd(runner("gpsimd"))

    return nc


_NC_CACHE = {}


def _get_nc(ntiles=NTILES):
    if ntiles not in _NC_CACHE:
        _NC_CACHE[ntiles] = build_nc(ntiles)
    return _NC_CACHE[ntiles]


def _consts():
    import ml_dtypes
    tmod = np.tile((np.arange(TT) % 32).astype(np.float32), (128, 1))
    k = np.arange(128)
    c = np.arange(128)
    maskt = ((k[:, None] // 32 == c[None, :] // 32) & (c[None, :] >= k[:, None])
             ).astype(ml_dtypes.bfloat16)
    ident = np.eye(128).astype(ml_dtypes.bfloat16)
    return np.ascontiguousarray(tmod), np.ascontiguousarray(maskt), np.ascontiguousarray(ident)


def kernel(L0_attn_x, L0_attn_W_q, L0_attn_W_k, L0_attn_W_v, L0_attn_W_g, L0_attn_W_o,
           L0_attn_log_decay, L0_ffn_w_gate, L0_ffn_w_up, L0_ffn_w_down):
    x = np.ascontiguousarray(np.asarray(L0_attn_x, dtype=np.float32)).reshape(-1, D)
    tmod, maskt, ident = _consts()
    ld0 = np.ascontiguousarray(np.asarray(L0_attn_log_decay, np.float32)[0:1].reshape(1, 1))
    common = dict(
        wq=np.ascontiguousarray(np.asarray(L0_attn_W_q, np.float32)),
        wk=np.ascontiguousarray(np.asarray(L0_attn_W_k, np.float32)),
        wv=np.ascontiguousarray(np.asarray(L0_attn_W_v, np.float32)),
        wgt=np.ascontiguousarray(np.asarray(L0_attn_W_g, np.float32)),
        wo=np.ascontiguousarray(np.asarray(L0_attn_W_o, np.float32)),
        fg=np.ascontiguousarray(np.asarray(L0_ffn_w_gate, np.int32)),
        fu=np.ascontiguousarray(np.asarray(L0_ffn_w_up, np.int32)),
        fd=np.ascontiguousarray(np.asarray(L0_ffn_w_down, np.int32)),
        ld0=ld0, tmod=tmod, maskt=maskt, ident=ident,
    )
    in_maps = []
    for c_ in range(NCORES):
        m = dict(common)
        m["x"] = np.ascontiguousarray(x[c_ * NTOK:(c_ + 1) * NTOK])
        in_maps.append(m)
    nc = _get_nc()
    res = run_bass_kernel_spmd(nc, in_maps, core_ids=list(range(NCORES)))
    outs = [res.results[c_]["out"] for c_ in range(NCORES)]
    full = np.concatenate(outs, axis=0).reshape(np.asarray(L0_attn_x).shape)
    return full.astype(np.float32)


# version 20
# speedup vs baseline: 1.0038x; 1.0025x over previous
import numpy as np
import concourse.bass as bass
import concourse.mybir as mybir
from concourse.bass_utils import run_bass_kernel_spmd
from contextlib import ExitStack

F32 = mybir.dt.float32
BF16 = mybir.dt.bfloat16
I32 = mybir.dt.int32
AF = mybir.ActivationFunctionType
ALU = mybir.AluOpType
AX = mybir.AxisListType

NCORES = 8
NTOK = 4096
TT = 512
NTILES = NTOK // TT
NS = TT // 128
D = 1024
DC = D // 128
H = 3072
HC = H // 128
EPS = 1e-5
QSC = 1.0 / 127.0

ENGINES = ("tensor", "vector", "scalar", "sync", "gpsimd")


class Plan:
    def __init__(self):
        from collections import defaultdict
        self.ops = {e: [] for e in ENGINES}
        self.cnt = defaultdict(int)

    def op(self, eng, fn, waits=(), inc=None, amt=1):
        w = {}
        for s, v in waits:
            if v:
                w[s] = max(w.get(s, 0), v)
        self.ops[eng].append((sorted(w.items()), fn, inc, amt))
        if inc is not None:
            self.cnt[inc] += amt
            return self.cnt[inc]
        return None


class Stream:

    def __init__(self, plan, name, buf, nslots, load_fn, queue="sync"):
        self.plan = plan
        self.queue = queue
        self.name = name
        self.buf = buf
        self.nslots = nslots
        self.load_fn = load_fn
        self.src_wait = []
        self.src_wait_fn = None
        self.slot_free = [0] * nslots
        self.loaded = {}
        self.next_g = 0

    def load(self, g):
        assert g == self.next_g, (g, self.next_g)
        self.next_g += 1
        slot = g % self.nslots
        sem = f"L{self.name}{slot}"
        sw = self.src_wait_fn(g) if self.src_wait_fn else self.src_wait
        waits = list(sw) + [("ct", self.slot_free[slot])]
        cnt = self.plan.op(self.queue, self.load_fn(g, slot), waits=waits, inc=sem, amt=16)
        self.loaded[g] = (sem, cnt)

    def slot(self, g):
        return g % self.nslots

    def consume(self, g, ct_cnt):
        self.slot_free[g % self.nslots] = ct_cnt


def build_nc(ntiles=NTILES, debug=False):
    nc = bass.Bass("TRN2", target_bir_lowering=False)
    p = Plan()

    xin = nc.dram_tensor("x", [NTOK, D], F32, kind="ExternalInput").ap()
    wq = nc.dram_tensor("wq", [D, D], F32, kind="ExternalInput").ap()
    wk = nc.dram_tensor("wk", [D, D], F32, kind="ExternalInput").ap()
    wv = nc.dram_tensor("wv", [D, D], F32, kind="ExternalInput").ap()
    wgt = nc.dram_tensor("wgt", [D, D], F32, kind="ExternalInput").ap()
    wo = nc.dram_tensor("wo", [D, D], F32, kind="ExternalInput").ap()
    fg = nc.dram_tensor("fg", [H, D], I32, kind="ExternalInput").ap()
    fu = nc.dram_tensor("fu", [H, D], I32, kind="ExternalInput").ap()
    fd = nc.dram_tensor("fd", [D, H], I32, kind="ExternalInput").ap()
    ld0 = nc.dram_tensor("ld0", [1, 1], F32, kind="ExternalInput").ap()
    ident = nc.dram_tensor("ident", [128, 128], BF16, kind="ExternalInput").ap()
    tmod = nc.dram_tensor("tmod", [128, TT], F32, kind="ExternalInput").ap()
    maskt = nc.dram_tensor("maskt", [128, 128], BF16, kind="ExternalInput").ap()
    out = nc.dram_tensor("out", [NTOK, D], F32, kind="ExternalOutput").ap()

    xbf_scr = nc.dram_tensor("xbf_scr", [NTOK, D], BF16).ap()
    x1_scr = nc.dram_tensor("x1_scr", [NTOK, D], BF16).ap()
    t2 = {"q": nc.dram_tensor("t2_q", [DC * DC * 128 * 128], BF16),
          "k": nc.dram_tensor("t2_k", [DC * DC * 128 * 128], BF16),
          "g": nc.dram_tensor("t2_g", [DC * DC * 128 * 128], BF16),
          "v": nc.dram_tensor("t2_v", [DC * 128 * 1024], BF16),
          "o": nc.dram_tensor("t2_o", [DC * 128 * 1024], BF16),
          "fd": nc.dram_tensor("t2_fd", [HC * 128 * 1024], BF16),
          "fg": nc.dram_tensor("t2_fg", [HC * DC * 128 * 128], BF16),
          "fu": nc.dram_tensor("t2_fu", [HC * DC * 128 * 128], BF16)}

    st = ExitStack()
    with st:
        def sb(name, shape, dt, **kw):
            return st.enter_context(nc.sbuf_tensor(name, shape, dt, **kw))

        DFt = sb("DF", [128, TT], BF16)
        DFI = sb("DFI", [128, TT], BF16)
        MSK = sb("MSK", [128, 128], BF16)
        sld = sb("sld", [1, 1], F32)
        ones = sb("ones", [1, 128], F32)
        lvec = sb("lvec", [128, 1], F32)
        epsT = sb("epsT", [128, 1], F32)
        tmpd = sb("tmpd", [128, TT], F32)

        xbf = sb("xbf", [128, NS * D], BF16)
        xfm2 = [sb(f"xfm{i}", [128, DC * TT], BF16, is_transpose_dest=True) for i in range(2)]
        qp = sb("qp", [128, DC * TT], BF16)
        kp = sb("kp", [128, DC * TT], BF16)
        vtm = sb("vtm", [128, NS * D], BF16)
        gfm = sb("gfm", [128, DC * TT], BF16)
        gof = sb("gof", [128, NS * D], BF16)
        stm = sb("stm", [128, NS * 128], BF16)
        r1 = sb("r1", [128, NS * D], BF16)
        x1b = [sb(f"x1b{i}", [128, NS * D], BF16) for i in range(2)]
        x1fm2 = [sb(f"x1fm{i}", [128, DC * TT], BF16, is_transpose_dest=True) for i in range(2)]
        sgate = sb("sgate", [128, 2 * TT], BF16)
        sup = sb("sup", [128, 2 * TT], BF16)
        hb = sb("hb", [128, HC * TT], BF16, is_transpose_dest=True)
        r2 = sb("r2", [128, NS * D], BF16)
        outb = [sb(f"outb{i}", [128, D], F32) for i in range(2)]
        junk = sb("junk", [128, D], BF16)

        sum1 = sb("sum1", [128, NS], F32)
        sq1 = sb("sq1", [128, NS], F32)
        mst = sb("mst", [128, NS], F32)
        msq = sb("msq", [128, NS], F32)
        vart = sb("vart", [128, NS], F32)
        rstd = sb("rstd", [128, NS], F32)
        sum2 = sb("sum2", [128, NS], F32)
        sq2 = sb("sq2", [128, NS], F32)
        mst2 = sb("mst2", [128, NS], F32)
        msq2 = sb("msq2", [128, NS], F32)
        vart2 = sb("vart2", [128, NS], F32)
        rstd2 = sb("rstd2", [128, NS], F32)

        bufq = sb("bufq", [128, 2 * 1024], BF16)
        bufk = sb("bufk", [128, 2 * 1024], BF16)
        bufg = sb("bufg", [128, 2 * 1024], BF16)
        bufv = sb("bufv", [128, 2 * 1024], BF16, is_transpose_dest=True)
        stgi = [sb(f"stgi{i}", [128, 1024], I32) for i in range(2)]
        xcb = [sb("xcb0", [128, 1024], BF16)]
        identS = sb("identS", [128, 128], BF16)
        tev = [sb(f"tev{i}", [128, 1024], BF16) for i in range(2)]
        bufo = sb("bufo", [128, 3 * 1024], BF16, is_transpose_dest=True)
        buffg = sb("buffg", [128, 2 * 1024], BF16)
        buffu = sb("buffu", [128, 2 * 1024], BF16)
        buffd = sb("buffd", [128, 3 * 1024], BF16, is_transpose_dest=True)

        ps = [st.enter_context(nc.psum_tensor(f"ps{i}", [128, 1024], F32)) for i in range(4)]

        def bank_ap(b, cols=512):
            return ps[b // 2][:, (b % 2) * 512:(b % 2) * 512 + cols]

        def pair_ap(pi):
            return ps[pi]

        bank_free = {b: ("ct", 0) for b in range(8)}
        _bkcur = [0]

        def grab(n=1):
            if n == 2 and _bkcur[0] % 2:
                _bkcur[0] += 1
            ids = [(_bkcur[0] + k) % 8 for k in range(n)]
            _bkcur[0] += n
            return ids

        def take(b):
            return [bank_free[b]]

        def free(b, sem, cnt):
            bank_free[b] = (sem, cnt)

        cast_cnt = {}
        cast_x = [f"x{i}c" for i in range(ntiles)]

        p.op("gpsimd", lambda e: e.memset(ones[:], 1.0))
        p.op("gpsimd", lambda e: e.memset(epsT[:], EPS), inc="cp")

        conv_pl = [None, None]
        conv_hb = {}
        W_LANES = [(f"W{k}", hb[:, 6144 + k * 1024: 6144 + (k + 1) * 1024]) for k in range(4)]
        X_LANES = [("X0", xcb[0][:])]
        _st = dict(pl=0, xl=0, wl=0, tev=0)
        tev_free = [None, None]
        t2_done = {}

        def _load_chunk(src_ap, r, cc, is_f32, waits):
            slot = _st["pl"] % 2
            _st["pl"] += 1
            sstg = stgi[slot][:].bitcast(F32) if is_f32 else stgi[slot][:]
            lw = list(waits)
            if conv_pl[slot]:
                lw.append(conv_pl[slot])
            lc = p.op("sync",
                      lambda en, s_=src_ap, rr=r, c_=cc, d_=sstg:
                          en.dma_start(out=d_, in_=s_[rr * 128:(rr + 1) * 128,
                                                      c_ * 1024:(c_ + 1) * 1024]),
                      waits=lw, inc=f"PL{slot}", amt=16)
            return slot, sstg, lc

        def conv_x(i):
            for r in range(NS):
                slot, sstg, lc = _load_chunk(xin[i * TT:(i + 1) * TT, :], r, 0, True, [])
                lane, lap = X_LANES[_st["xl"] % len(X_LANES)]
                _st["xl"] += 1
                hw = [(f"PL{slot}", lc)]
                if conv_hb.get(lane):
                    hw.append(conv_hb[lane])
                cc_ = p.op("gpsimd",
                           lambda en, la=lap, s_=sstg: en.tensor_copy(la, s_),
                           waits=hw, inc="cp")
                conv_pl[slot] = ("cp", cc_)
                sc_ = p.op("scalar",
                           lambda en, ii=i, rr=r, la=lap:
                               en.dma_start(out=xbf_scr[ii * TT + rr * 128:
                                                        ii * TT + (rr + 1) * 128, :], in_=la),
                           waits=[("cp", cc_)], inc="S" + lane, amt=16)
                conv_hb[lane] = ("S" + lane, sc_)
            cast_cnt[f"x{i}c"] = [("SX0", p.cnt["SX0"])]

        wdone = {}

        def wconv_chunk(name, src_ap, is_f32, dst, style, r, cg, lanes=None,
                        first_gate=()):
                    lanes = lanes or W_LANES
                    slot, sstg, lc = _load_chunk(src_ap, r, cg, is_f32, [])
                    lane, lap = lanes[_st["wl"] % len(lanes)]
                    _st["wl"] += 1
                    hw = [(f"PL{slot}", lc)]
                    if conv_hb.get(lane):
                        hw.append(conv_hb[lane])
                    else:
                        hw.extend(first_gate)
                    cc_ = p.op("vector",
                               lambda en, la=lap, s_=sstg: en.tensor_copy(la, s_),
                               waits=hw, inc="cv")
                    conv_pl[slot] = ("cv", cc_)
                    b = grab()[0]
                    bw = take(b)
                    tc = None
                    for c in range(DC):
                        tc = p.op("tensor",
                                  lambda en, bb=b, la=lap, c_=c:
                                      en.transpose(
                                          bank_ap(bb).bitcast(BF16)[:, c_ * 128:(c_ + 1) * 128],
                                          la[:, c_ * 128:(c_ + 1) * 128], identS[:]),
                                  waits=([("cv", cc_), ("dS", c_idt)] + bw) if c == 0 else (),
                                  inc="ct" if c == DC - 1 else None)
                    conv_hb[lane] = ("ct", tc)
                    ts_ = _st["tev"] % 2
                    _st["tev"] += 1
                    ew = [("ct", tc)]
                    if tev_free[ts_]:
                        ew.append(tev_free[ts_])
                    ev = p.op("vector",
                              lambda en, bb=b, t_=ts_:
                                  en.tensor_copy(tev[t_][:], bank_ap(bb).bitcast(BF16)),
                              waits=ew, inc="cv")
                    free(b, "cv", ev)
                    if style == "lhsT":
                        dap = bass.AP(dst, r * DC * 16384, [[1024, 128], [1, 1024]])
                    else:
                        dap = bass.AP(dst, cg * DC * 131072 + r * 128,
                                      [[1024, 128], [131072, DC], [1, 128]])
                    ssem = f"TV{ts_}"
                    sc_ = p.op("scalar",
                               lambda en, da=dap, t_=ts_, st_=style:
                                   en.dma_start(out=da, in_=tev[t_][:] if st_ == "lhsT"
                                                else tev[t_][:].rearrange("p (c l) -> p c l",
                                                                          l=128)),
                               waits=[("cv", ev)], inc=ssem, amt=16)
                    tev_free[ts_] = (ssem, sc_)
                    wdone.setdefault(name, []).append((ssem, sc_))

        def wconv(name, src_ap, R, C, is_f32, dst, style, cg_outer=False):
            if cg_outer:
                for cg in range(C // 1024):
                    for r in range(R // 128):
                        wconv_chunk(name, src_ap, is_f32, dst, style, r, cg)
            else:
                for r in range(R // 128):
                    for cg in range(C // 1024):
                        wconv_chunk(name, src_ap, is_f32, dst, style, r, cg)
            t2_done[name] = [("TV0", p.cnt["TV0"]), ("TV1", p.cnt["TV1"])]
            cast_cnt[name] = t2_done[name]

        p.op("sync", lambda e: e.dma_start(out=sld[:], in_=ld0), inc="dS", amt=16)
        p.op("sync", lambda e: e.dma_start(out=tmpd[:], in_=tmod), inc="dS", amt=16)
        p.op("sync", lambda e: e.dma_start(out=MSK[:], in_=maskt), inc="dS", amt=16)
        c_idt = p.op("sync", lambda e: e.dma_start(out=identS[:], in_=ident), inc="dS", amt=16)
        c_con = c_idt
        c_sld = c_tmd = c_msk = c_con

        c_bc = p.op("tensor", lambda e: e.matmul(ps[0][:, 0:1], ones[:], sld[:],
                                                 start=True, stop=True),
                    waits=[("dS", c_sld), ("cp", 1)], inc="ct")
        c_sg = p.op("scalar", lambda e: e.activation(lvec[:], ps[0][:, 0:1], AF.Sigmoid),
                    waits=[("ct", c_bc)], inc="cs")
        c_ln = p.op("scalar", lambda e: e.activation(lvec[:], lvec[:], AF.Ln),
                    waits=[("cs", c_sg)], inc="cs")
        c_tm = p.op("vector", lambda e: e.tensor_scalar(tmpd[:], tmpd[:], lvec[:, 0:1], None,
                                                        ALU.mult),
                    waits=[("cs", c_ln), ("dS", c_tmd)], inc="cv")
        p.op("scalar", lambda e: e.activation(DFt[:], tmpd[:], AF.Exp),
             waits=[("cv", c_tm)], inc="cs")
        c_dfi = p.op("scalar", lambda e: e.activation(DFI[:], tmpd[:], AF.Exp, scale=-1.0),
                     inc="cs")
        free(0, "cs", c_sg)

        wconv("wv", wv, D, D, True, t2["v"], "rhs")
        wconv("wq", wq, D, D, True, t2["q"], "lhsT")
        conv_x(0)
        if ntiles > 1:
            conv_x(1)

        def mk_scr2_loader(t2t, npt, buf):
            def mk(g, slot):
                j = g % npt
                def fn(e):
                    src = bass.AP(t2t, j * DC * 16384, [[128, 128], [16384, DC], [1, 128]])
                    dst = buf[:, slot * 1024:(slot + 1) * 1024].rearrange(
                        "p (e l) -> p e l", l=128)
                    return e.dma_start(out=dst, in_=src)
                return fn
            return mk

        def mk_piece_loader(t2t, npt, buf):
            def mk(g, slot):
                j = g % npt
                def fn(e):
                    src = bass.AP(t2t, j * 131072, [[1024, 128], [1, 1024]])
                    return e.dma_start(out=buf[:, slot * 1024:(slot + 1) * 1024], in_=src)
                return fn
            return mk

        MQ = Stream(p, "q", bufq, 2, mk_piece_loader(t2["q"], DC, bufq))
        MQ.src_wait_fn = lambda g: [wdone["wq"][g % DC]]
        MK = Stream(p, "k", bufk, 2, mk_piece_loader(t2["k"], DC, bufk))
        MK.src_wait_fn = lambda g: [wdone["wk"][g % DC]]
        MG = Stream(p, "g", bufg, 2, mk_piece_loader(t2["g"], DC, bufg))
        MG.src_wait_fn = lambda g: [wdone["wg"][g % DC]]
        MV = Stream(p, "v", bufv, 2, mk_piece_loader(t2["v"], DC, bufv))
        MV.src_wait_fn = lambda g: [wdone["wv"][-1]]
        MO = Stream(p, "o", bufo, 3, mk_piece_loader(t2["o"], DC, bufo))
        MO.src_wait_fn = lambda g: [wdone["wo"][-1]]
        MFG = Stream(p, "G", buffg, 2, mk_piece_loader(t2["fg"], HC, buffg))
        MFG.src_wait_fn = lambda g: [wdone["fg"][g % HC]]
        MFU = Stream(p, "U", buffu, 2, mk_piece_loader(t2["fu"], HC, buffu))
        MFU.src_wait_fn = lambda g: [wdone["fu"][g % HC]]
        MFD = Stream(p, "D", buffd, 3, mk_piece_loader(t2["fd"], HC, buffd))
        MFD.src_wait_fn = lambda g: [wdone["fd"][((g % HC) // DC) * DC + DC - 1]]

        state = dict(xfm_free0=0, xfm_free1=0, x1fm_free0=0, x1fm_free1=0, xbf_free=0,
                     qp_free=0, kp_free=0, r1_free_cs=0)
        x1b_store = [None, None]
        out_store = [None, None]

        def emit_inputs(ti):
            par = ti % 2
            t0 = ti * TT
            xfm = xfm2[par]
            xfm_loads = None
            for e in range(DC):
                w = (list(cast_cnt[cast_x[ti]]) + [("ct", state[f"xfm_free{par}"])]) if e == 0 else ()
                cnt = p.op("scalar",
                           lambda en, ee=e, tt0=t0, xf=xfm: en.dma_start(
                               out=xf[:, ee * TT:(ee + 1) * TT],
                               in_=xbf_scr[tt0:tt0 + TT, ee * 128:(ee + 1) * 128],
                               transpose=True),
                           waits=w, inc=f"DXF{par}", amt=16)
                xfm_loads = (f"DXF{par}", cnt)
            cnt = p.op("scalar",
                       lambda en, tt0=t0: en.dma_start(
                           out=xbf[:].rearrange("p (s d) -> p s d", d=D),
                           in_=bass.AP(xbf_scr.tensor, tt0 * D,
                                       [[D, 128], [128 * D, NS], [1, D]])),
                       waits=list(cast_cnt[cast_x[ti]]) + [("cv", state["xbf_free"])],
                       inc="DXB", amt=16)
            state[f"inp_{ti}"] = (xfm_loads, ("DXB", cnt))

        def emit_attn_body(ti):
            par = ti % 2
            t0 = ti * TT
            xfm = xfm2[par]
            xfm_loads, xbf_load = state[f"inp_{ti}"]

            q_evac_last = k_evac_last = 0
            for which, M, dst, dfap, freekey in (("q", MQ, qp, DFt, "qp_free"),
                                                 ("k", MK, kp, DFI, "kp_free")):
                if ti == 0 and which == "k":
                    wconv("wk", wk, D, D, True, t2["k"], "lhsT")
                base_g = M.next_g
                M.load(base_g)
                M.load(base_g + 1)
                gc = None
                for j in range(DC):
                    g = base_g + j
                    b = grab()[0]
                    slot = M.slot(g)
                    waits = [M.loaded[g], xfm_loads] + take(b)
                    for e in range(DC):
                        gc = p.op("tensor",
                                  lambda en, bb=b, sl=slot, ee=e, Mb=M.buf, xf=xfm:
                                      en.matmul(bank_ap(bb),
                                                Mb[:, sl * 1024 + ee * 128:sl * 1024 + (ee + 1) * 128],
                                                xf[:, ee * TT:(ee + 1) * TT],
                                                start=(ee == 0), stop=(ee == DC - 1)),
                                  waits=waits if e == 0 else (),
                                  inc="ct" if e == DC - 1 else None)
                    M.consume(g, gc)
                    if g + 2 < base_g + DC:
                        M.load(g + 2)
                    ew = [("ct", gc), ("cs", c_dfi)]
                    if j == 0:
                        ew.append(("ct", state[freekey]))
                    ec = p.op("vector",
                              lambda en, bb=b, jj=j, dd=dst, df=dfap:
                                  en.tensor_tensor(dd[:, jj * TT:(jj + 1) * TT],
                                                   bank_ap(bb), df[:], ALU.mult),
                              waits=ew, inc="cv")
                    free(b, "cv", ec)
                    if which == "q":
                        q_evac_last = ec
                    else:
                        k_evac_last = ec

            v_evac_last = 0
            for sub in range(1):
                groups = [(s, h) for s in range(NS) for h in range(2)]
                gbanks = [grab()[0] for _ in groups]
                base_g = MV.next_g
                MV.load(base_g)
                MV.load(base_g + 1)
                gct = None
                for e in range(DC):
                    g = base_g + e
                    slot = MV.slot(g)
                    for gi, (s, h) in enumerate(groups):
                        b = gbanks[gi]
                        if e == 0:
                            waits = [MV.loaded[g], xfm_loads] + take(b)
                        elif gi == 0:
                            waits = [MV.loaded[g]]
                        else:
                            waits = ()
                        gct = p.op("tensor",
                                   lambda en, bb=b, ss=s, hh=h, ee=e, sl=slot, xf=xfm:
                                       en.matmul(bank_ap(bb),
                                                 xf[:, ee * TT + ss * 128:ee * TT + (ss + 1) * 128],
                                                 bufv[:, sl * 1024 + hh * 512:sl * 1024 + (hh + 1) * 512],
                                                 start=(ee == 0), stop=(ee == DC - 1)),
                                   waits=waits,
                                   inc="ct" if gi == len(groups) - 1 else None)
                    MV.consume(g, gct)
                    if e + 2 < DC:
                        MV.load(g + 2)
                for gi, (s, h) in enumerate(groups):
                    b = gbanks[gi]
                    ec = p.op("vector",
                              lambda en, bb=b, ss=s, hh=h:
                                  en.tensor_copy(vtm[:, ss * D + hh * 512:ss * D + (hh + 1) * 512],
                                                 bank_ap(bb)),
                              waits=[("ct", gct)], inc="cv")
                    free(b, "cv", ec)
                    v_evac_last = ec

            if ti == 0:
                wconv("wg", wgt, D, D, True, t2["g"], "lhsT")
                wconv("wo", wo, D, D, True, t2["o"], "rhs")
            g_evac_last = 0
            base_g = MG.next_g
            MG.load(base_g)
            MG.load(base_g + 1)
            gc = None
            for j in range(DC):
                g = base_g + j
                b = grab()[0]
                slot = MG.slot(g)
                waits = [MG.loaded[g], xfm_loads] + take(b)
                for e in range(DC):
                    gc = p.op("tensor",
                              lambda en, bb=b, sl=slot, ee=e, xf=xfm:
                                  en.matmul(bank_ap(bb),
                                            bufg[:, sl * 1024 + ee * 128:sl * 1024 + (ee + 1) * 128],
                                            xf[:, ee * TT:(ee + 1) * TT],
                                            start=(ee == 0), stop=(ee == DC - 1)),
                              waits=waits if e == 0 else (),
                              inc="ct" if e == DC - 1 else None)
                MG.consume(g, gc)
                if g + 2 < base_g + DC:
                    MG.load(g + 2)
                ec = p.op("scalar",
                          lambda en, bb=b, jj=j:
                              en.activation(gfm[:, jj * TT:(jj + 1) * TT], bank_ap(bb),
                                            AF.Sigmoid),
                          waits=[("ct", gc)], inc="cs")
                free(b, "cs", ec)
                g_evac_last = ec
            xfm_free_new = gc

            s_evac = []
            gc = None
            for s in range(NS):
                b = grab()[0]
                waits = [("cv", max(q_evac_last, k_evac_last))] + take(b)
                for e in range(DC):
                    gc = p.op("tensor",
                              lambda en, bb=b, ss=s, ee=e:
                                  en.matmul(bank_ap(bb, 128),
                                            kp[:, ee * TT + ss * 128:ee * TT + (ss + 1) * 128],
                                            qp[:, ee * TT + ss * 128:ee * TT + (ss + 1) * 128],
                                            start=(ee == 0), stop=(ee == DC - 1)),
                              waits=waits if e == 0 else (),
                              inc="ct" if e == DC - 1 else None)
                ec = p.op("vector",
                          lambda en, bb=b, ss=s:
                              en.tensor_tensor(stm[:, ss * 128:(ss + 1) * 128],
                                               bank_ap(bb, 128), MSK[:], ALU.mult),
                          waits=[("ct", gc), ("dS", c_msk)], inc="cv")
                free(b, "cv", ec)
                s_evac.append(ec)
            s_phase_end = gc

            gof_evac = []
            gc = None
            for s in range(NS):
                pids = grab(2)
                pi = pids[0] // 2
                waits = [("cv", v_evac_last), ("cv", s_evac[s])] + take(pids[0]) + take(pids[1])
                for j in range(DC):
                    gc = p.op("tensor",
                              lambda en, pp=pi, ss=s, jj=j:
                                  en.matmul(pair_ap(pp)[:, jj * 128:(jj + 1) * 128],
                                            vtm[:, ss * D + jj * 128:ss * D + (jj + 1) * 128],
                                            stm[:, ss * 128:(ss + 1) * 128],
                                            start=True, stop=True),
                              waits=waits if j == 0 else (),
                              inc="ct" if j == DC - 1 else None)
                ec = p.op("vector",
                          lambda en, pp=pi, ss=s:
                              en.tensor_tensor(
                                  gof[:, ss * D:(ss + 1) * D].rearrange("p (j l) -> p j l", l=128),
                                  pair_ap(pp).rearrange("p (j l) -> p j l", l=128),
                                  gfm.rearrange("p (j t) -> p j t", t=TT)[:, :, ss * 128:(ss + 1) * 128],
                                  ALU.mult),
                          waits=[("ct", gc), ("cs", g_evac_last)], inc="cv")
                free(pids[0], "cv", ec)
                free(pids[1], "cv", ec)
                gof_evac.append(ec)
            o_phase_end = gc

            r1_evac_last = 0
            wgroups = [(s, h) for s in range(NS) for h in range(2)]
            wbanks = [grab()[0] for _ in wgroups]
            base_g = MO.next_g
            for la in range(2):
                MO.load(base_g + la)
            gct = None
            for e in range(DC):
                g = base_g + e
                slot = MO.slot(g)
                for gi, (s, h) in enumerate(wgroups):
                    b = wbanks[gi]
                    if e == 0:
                        waits = [MO.loaded[g], ("cv", gof_evac[NS - 1])] + take(b)
                    elif gi == 0:
                        waits = [MO.loaded[g]]
                    else:
                        waits = ()
                    gct = p.op("tensor",
                               lambda en, bb=b, ss=s, ee=e, sl=slot, hh=h:
                                   en.matmul(bank_ap(bb),
                                             gof[:, (ss * DC + ee) * 128:(ss * DC + ee + 1) * 128],
                                             bufo[:, sl * 1024 + hh * 512:sl * 1024 + (hh + 1) * 512],
                                             start=(ee == 0), stop=(ee == DC - 1)),
                               waits=waits,
                               inc="ct" if gi == len(wgroups) - 1 else None)
                MO.consume(g, gct)
                if e + 2 < DC:
                    MO.load(g + 2)
            for gi, (s, h) in enumerate(wgroups):
                b = wbanks[gi]
                ec = p.op("vector",
                          lambda en, bb=b, ss=s, hh=h:
                              en.scalar_tensor_tensor(
                                  r1[:, ss * D + hh * 512:ss * D + (hh + 1) * 512],
                                  xbf[:, ss * D + hh * 512:ss * D + (hh + 1) * 512],
                                  2.0, bank_ap(bb), ALU.mult, ALU.add),
                          waits=[("ct", gct), xbf_load,
                                 ("cs", state["r1_free_cs"])],
                          inc="cv")
                free(b, "cv", ec)
                r1_evac_last = ec

            c_red = p.op("vector",
                         lambda en: en.reduce_sum(sum1[:, 0:NS],
                                                  r1[:].rearrange("p (s d) -> p s d", d=D), AX.X),
                         waits=[("cv", r1_evac_last)], inc="cv")
            sq_last = 0
            for s in range(NS):
                sq_last = p.op("vector",
                               lambda en, ss=s: en.scalar_tensor_tensor(
                                   junk[:], r1[:, ss * D:(ss + 1) * D], 1.0,
                                   r1[:, ss * D:(ss + 1) * D], ALU.mult, ALU.mult,
                                   accum_out=sq1[:, ss:ss + 1]),
                               waits=(), inc="cv")
            c_mean = p.op("vector", lambda en: en.tensor_scalar(mst[:], sum1[:], 1.0 / D, None,
                                                                ALU.mult),
                          waits=[("cv", c_red)], inc="cv")
            c_msq = p.op("vector", lambda en: en.tensor_tensor(msq[:], mst[:], mst[:], ALU.mult),
                         waits=[("cv", c_mean)], inc="cv")
            c_var = p.op("vector",
                         lambda en: en.scalar_tensor_tensor(vart[:], sq1[:], 1.0 / D, msq[:],
                                                            ALU.mult, ALU.subtract),
                         waits=[("cv", sq_last), ("cv", c_msq)], inc="cv")
            c_lnv = p.op("scalar", lambda en: en.activation(vart[:], vart[:], AF.Ln, bias=epsT[:]),
                         waits=[("cv", c_var)], inc="cs")
            c_rstd = p.op("scalar", lambda en: en.activation(rstd[:], vart[:], AF.Exp, scale=-0.5),
                          waits=[("cs", c_lnv)], inc="cs")
            x1r = 0
            for s in range(NS):
                w = [("cs", c_rstd)]
                if s == 0 and x1b_store[par] is not None:
                    w.append(x1b_store[par])
                x1r = p.op("vector",
                           lambda en, ss=s, pp=par:
                               en.tensor_scalar(x1b[pp][:, ss * D:(ss + 1) * D],
                                                r1[:, ss * D:(ss + 1) * D],
                                                mst[:, ss:ss + 1], rstd[:, ss:ss + 1],
                                                ALU.subtract, ALU.mult),
                           waits=w, inc="cv")

            x1b_store[par] = ("DX1S", p.op("scalar",
                                  lambda en, tt0=t0, pp=par: en.dma_start(
                                      out=bass.AP(x1_scr.tensor, tt0 * D,
                                                  [[D, 128], [128 * D, NS], [1, D]]),
                                      in_=x1b[pp][:].rearrange("p (s d) -> p s d", d=D)),
                                  waits=[("cv", x1r)], inc="DX1S", amt=16))

            state[f"xfm_free{par}"] = xfm_free_new
            state["xbf_free"] = r1_evac_last
            state["qp_free"] = s_phase_end
            state["kp_free"] = s_phase_end
            state["r1_free_cs"] = 0

        def emit_x1fm(ti):
            par = ti % 2
            t0 = ti * TT
            x1f = x1fm2[par]
            x1fm_loads = None
            for e in range(DC):
                w = [x1b_store[par], ("ct", state[f"x1fm_free{par}"])] if e == 0 else ()
                cnt = p.op("scalar",
                           lambda en, ee=e, tt0=t0, xf=x1f: en.dma_start(
                               out=xf[:, ee * TT:(ee + 1) * TT],
                               in_=x1_scr[tt0:tt0 + TT, ee * 128:(ee + 1) * 128],
                               transpose=True),
                           waits=w, inc=f"DX1F{par}", amt=16)
                x1fm_loads = (f"DX1F{par}", cnt)
            state[f"x1fm_loads_{ti}"] = x1fm_loads

        def emit_ffn_gateup(ti):
            par = ti % 2
            t0 = ti * TT
            x1fm = x1fm2[par]
            x1fm_loads = state[f"x1fm_loads_{ti}"]
            if ti == 0:
                state["prologue_store_done"] = [("TV0", p.cnt["TV0"]),
                                                ("TV1", p.cnt["TV1"])]

            h_last = 0
            LAG = 8
            if ti == 0:
                F_LANES = [("F0", qp[:, 0:1024]), ("F1", qp[:, 1024:2048]),
                           ("F2", kp[:, 0:1024]), ("F3", kp[:, 1024:2048])]
                fgate = [("ct", state["qp_free"])]
                for jj in range(LAG):
                    wconv_chunk("fg", fg, False, t2["fg"], "lhsT", jj, 0, F_LANES, fgate)
                    wconv_chunk("fu", fu, False, t2["fu"], "lhsT", jj, 0, F_LANES, fgate)
                    wconv_chunk("fd", fd, False, t2["fd"], "rhs", jj % DC, jj // DC,
                                F_LANES, fgate)
            base_fg = MFG.next_g
            base_fu = MFU.next_g
            MFG.load(base_fg)
            MFU.load(base_fu)
            MFG.load(base_fg + 1)
            MFU.load(base_fu + 1)
            sg_free = [0, 0]
            gcu = None
            for j in range(HC):
                if ti == 0 and j + LAG < HC:
                    jj = j + LAG
                    wconv_chunk("fg", fg, False, t2["fg"], "lhsT", jj, 0, F_LANES)
                    wconv_chunk("fu", fu, False, t2["fu"], "lhsT", jj, 0, F_LANES)
                    wconv_chunk("fd", fd, False, t2["fd"], "rhs", jj % DC, jj // DC,
                                F_LANES)
                gg = base_fg + j
                gu = base_fu + j
                bgate = grab()[0]
                bup = grab()[0]
                slotg = MFG.slot(gg)
                slotu = MFU.slot(gu)
                waits = [MFG.loaded[gg], x1fm_loads] + take(bgate)
                gcg = None
                for e in range(DC):
                    gcg = p.op("tensor",
                               lambda en, bb=bgate, sl=slotg, ee=e, x1f_=x1fm:
                                   en.matmul(bank_ap(bb),
                                             buffg[:, sl * 1024 + ee * 128:sl * 1024 + (ee + 1) * 128],
                                             x1f_[:, ee * TT:(ee + 1) * TT],
                                             start=(ee == 0), stop=(ee == DC - 1)),
                               waits=waits if e == 0 else (),
                               inc="ct" if e == DC - 1 else None)
                MFG.consume(gg, gcg)
                waits = [MFU.loaded[gu]] + take(bup)
                for e in range(DC):
                    gcu = p.op("tensor",
                               lambda en, bb=bup, sl=slotu, ee=e, x1f_=x1fm:
                                   en.matmul(bank_ap(bb),
                                             buffu[:, sl * 1024 + ee * 128:sl * 1024 + (ee + 1) * 128],
                                             x1f_[:, ee * TT:(ee + 1) * TT],
                                             start=(ee == 0), stop=(ee == DC - 1)),
                               waits=waits if e == 0 else (),
                               inc="ct" if e == DC - 1 else None)
                MFU.consume(gu, gcu)
                if j + 2 < HC:
                    MFG.load(gg + 2)
                    MFU.load(gu + 2)
                sl2 = j % 2
                e1 = p.op("scalar",
                          lambda en, bb=bgate, s2=sl2:
                              en.activation(sgate[:, s2 * TT:(s2 + 1) * TT], bank_ap(bb),
                                            AF.Silu, scale=QSC),
                          waits=[("ct", gcg), ("cv", sg_free[sl2])], inc="cs")
                free(bgate, "cs", e1)
                e2 = p.op("vector",
                          lambda en, bb=bup, s2=sl2:
                              en.tensor_scalar(sup[:, s2 * TT:(s2 + 1) * TT], bank_ap(bb),
                                               QSC * QSC, None, ALU.mult),
                          waits=[("ct", gcu)], inc="cv")
                free(bup, "cv", e2)
                hw = [("cs", e1), ("cv", e2)]
                if ti == 0 and j == 0:
                    hw.extend(state["prologue_store_done"])
                h_last = p.op("vector",
                              lambda en, jj=j, s2=sl2:
                                  en.tensor_tensor(hb[:, jj * TT:(jj + 1) * TT],
                                                   sgate[:, s2 * TT:(s2 + 1) * TT],
                                                   sup[:, s2 * TT:(s2 + 1) * TT], ALU.mult),
                              waits=hw, inc="cv")
                sg_free[sl2] = h_last

            state[f"hlast_{ti}"] = h_last
            state[f"x1fm_free{par}"] = gcu

        def emit_ffn_down(ti):
            par = ti % 2
            t0 = ti * TT
            h_last = state[f"hlast_{ti}"]
            r2_evac_last = 0
            dgroups = [(s, h) for s in range(NS) for h in range(2)]
            dbanks = [grab()[0] for _ in dgroups]
            base_g = MFD.next_g
            for la in range(2):
                MFD.load(base_g + la)
            gct = None
            for jh in range(HC):
                g = base_g + jh
                slot = MFD.slot(g)
                for gi, (s, h) in enumerate(dgroups):
                    b = dbanks[gi]
                    if jh == 0:
                        waits = [MFD.loaded[g], ("cv", h_last)] + take(b)
                    elif gi == 0:
                        waits = [MFD.loaded[g]]
                    else:
                        waits = ()
                    gct = p.op("tensor",
                               lambda en, bb=b, ss=s, jj=jh, sl=slot, hh=h:
                                   en.matmul(bank_ap(bb),
                                             hb[:, jj * TT + ss * 128:jj * TT + (ss + 1) * 128],
                                             buffd[:, sl * 1024 + hh * 512:sl * 1024 + (hh + 1) * 512],
                                             start=(jj == 0), stop=(jj == HC - 1)),
                               waits=waits,
                               inc="ct" if gi == len(dgroups) - 1 else None)
                MFD.consume(g, gct)
                if jh + 2 < HC:
                    MFD.load(g + 2)
            for gi, (s, h) in enumerate(dgroups):
                b = dbanks[gi]
                ec = p.op("vector",
                          lambda en, bb=b, ss=s, hh=h, pp=par:
                              en.scalar_tensor_tensor(
                                  r2[:, ss * D + hh * 512:ss * D + (hh + 1) * 512],
                                  x1b[pp][:, ss * D + hh * 512:ss * D + (hh + 1) * 512],
                                  1.0, bank_ap(bb), ALU.mult, ALU.add),
                          waits=[("ct", gct)], inc="cv")
                free(b, "cv", ec)
                r2_evac_last = ec

            c_red2 = p.op("vector",
                          lambda en: en.reduce_sum(sum2[:, 0:NS],
                                                   r2[:].rearrange("p (s d) -> p s d", d=D), AX.X),
                          waits=[("cv", r2_evac_last)], inc="cv")
            sq_last2 = 0
            for s in range(NS):
                sq_last2 = p.op("vector",
                                lambda en, ss=s: en.scalar_tensor_tensor(
                                    junk[:], r2[:, ss * D:(ss + 1) * D], 1.0,
                                    r2[:, ss * D:(ss + 1) * D], ALU.mult, ALU.mult,
                                    accum_out=sq2[:, ss:ss + 1]),
                                waits=(), inc="cv")
            c_mean2 = p.op("vector", lambda en: en.tensor_scalar(mst2[:], sum2[:], 1.0 / D, None,
                                                                 ALU.mult),
                           waits=[("cv", c_red2)], inc="cv")
            c_msq2 = p.op("vector", lambda en: en.tensor_tensor(msq2[:], mst2[:], mst2[:], ALU.mult),
                          waits=[("cv", c_mean2)], inc="cv")
            c_var2 = p.op("vector",
                          lambda en: en.scalar_tensor_tensor(vart2[:], sq2[:], 1.0 / D, msq2[:],
                                                             ALU.mult, ALU.subtract),
                          waits=[("cv", sq_last2), ("cv", c_msq2)], inc="cv")
            c_lnv2 = p.op("scalar", lambda en: en.activation(vart2[:], vart2[:], AF.Ln, bias=epsT[:]),
                          waits=[("cv", c_var2)], inc="cs")
            c_rstd2 = p.op("scalar", lambda en: en.activation(rstd2[:], vart2[:], AF.Exp, scale=-0.5),
                           waits=[("cs", c_lnv2)], inc="cs")
            for s in range(NS):
                op_ = s % 2
                w = [("cs", c_rstd2)]
                if out_store[op_] is not None:
                    w.append(out_store[op_])
                cnrm = p.op("vector",
                            lambda en, ss=s, oo=op_:
                                en.tensor_scalar(outb[oo][:], r2[:, ss * D:(ss + 1) * D],
                                                 mst2[:, ss:ss + 1], rstd2[:, ss:ss + 1],
                                                 ALU.subtract, ALU.mult),
                            waits=w, inc="cv")
                out_store[op_] = (f"DOS{op_}", p.op("scalar",
                                      lambda en, ss=s, oo=op_, tt0=t0: en.dma_start(
                                          out=out[tt0 + ss * 128:tt0 + (ss + 1) * 128, :],
                                          in_=outb[oo][:]),
                                      waits=[("cv", cnrm)], inc=f"DOS{op_}", amt=16))

        emit_inputs(0)
        emit_attn_body(0)
        emit_x1fm(0)
        if ntiles > 1:
            if ntiles > 2:
                conv_x(2)
            if ntiles > 3:
                conv_x(3)
            emit_inputs(1)
            emit_attn_body(1)
        for _i in range(1, ntiles):
            emit_ffn_gateup(_i - 1)
            if _i + 1 < ntiles:
                if _i + 3 < ntiles:
                    conv_x(_i + 3)
                emit_inputs(_i + 1)
            emit_x1fm(_i)
            emit_ffn_down(_i - 1)
            if _i + 1 < ntiles:
                emit_attn_body(_i + 1)
        emit_ffn_gateup(ntiles - 1)
        emit_ffn_down(ntiles - 1)

        if debug:
            fin00 = dict(p.cnt)
            for nm, srcap, shp in [("x1scr", x1_scr, [TT, D]), ("xbfscr", xbf_scr, [TT, D])]:
                dout = nc.dram_tensor("dbg_" + nm, shp, BF16, kind="ExternalOutput").ap()
                p.op("gpsimd",
                     lambda en, d=dout, s=srcap, sh=shp: en.dma_start(out=d, in_=s[0:sh[0], :]),
                     waits=[(s_, v_) for s_, v_ in fin00.items() if v_ > 0],
                     inc="dS", amt=16)
            dbg_specs = [("qp", qp, BF16), ("kp", kp, BF16), ("vtm", vtm, BF16),
                         ("gfm", gfm, BF16), ("stm", stm, BF16), ("gof", gof, BF16),
                         ("r1", r1, BF16), ("x1b0", x1b[0], BF16), ("hb", hb, BF16),
                         ("r2", r2, BF16), ("xfm", xfm2[0], BF16), ("x1fm", x1fm2[0], BF16),
                         ("dft", DFt, BF16), ("dfi", DFI, BF16), ("xbf", xbf, BF16),
                         ("mstd", mst, F32), ("rstdd", rstd, F32)]
            fin0 = dict(p.cnt)
            w0 = [("ct", fin0["ct"]), ("cv", fin0["cv"]), ("cs", fin0["cs"])]
            for name, buf, dt in dbg_specs:
                shape = [buf.shape[0] if hasattr(buf, "shape") else 128, buf[:].shape[1]]
                dout = nc.dram_tensor("dbg_" + name, list(buf[:].shape), dt,
                                      kind="ExternalOutput").ap()
                p.op("sync", lambda en, d=dout, b=buf: en.dma_start(out=d, in_=b[:]),
                     waits=w0, inc="dS", amt=16)
                w0 = []

        fin = dict(p.cnt)
        p.op("gpsimd", lambda en: en.nop(),
             waits=[(s_, v_) for s_, v_ in fin.items() if v_ > 0])

        for _e, _ops in p.ops.items():
            for _w, _f, _i, _a in _ops:
                for _s, _v in _w:
                    assert _v <= p.cnt.get(_s, 0), (_e, _s, _v, p.cnt.get(_s, 0))
        _pos = {e_: 0 for e_ in p.ops}
        _cnt = {s_: 0 for s_ in p.cnt}
        _prog = True
        while _prog:
            _prog = False
            for _e, _ops in p.ops.items():
                while _pos[_e] < len(_ops):
                    _w, _f, _i, _a = _ops[_pos[_e]]
                    if all(_cnt.get(s_, 0) >= v_ for s_, v_ in _w):
                        if _i:
                            _cnt[_i] = _cnt.get(_i, 0) + _a
                        _pos[_e] += 1
                        _prog = True
                    else:
                        break
        assert all(_pos[e_] == len(p.ops[e_]) for e_ in p.ops), \
            {e_: (_pos[e_], len(p.ops[e_])) for e_ in p.ops if _pos[e_] < len(p.ops[e_])}

        sems = {s_: st.enter_context(nc.semaphore(s_)) for s_ in p.cnt}
        block = st.enter_context(nc.Block())

        def runner(ename):
            def f(eng):
                for waits, fn, inc, amt in p.ops[ename]:
                    for s_name, v in waits:
                        eng.wait_ge(sems[s_name], v)
                    ins = fn(eng)
                    if inc is not None:
                        ins.then_inc(sems[inc], amt)
            return f

        block.tensor(runner("tensor"))
        block.vector(runner("vector"))
        block.scalar(runner("scalar"))
        block.sync(runner("sync"))
        block.gpsimd(runner("gpsimd"))

    return nc


_NC_CACHE = {}


def _get_nc(ntiles=NTILES):
    if ntiles not in _NC_CACHE:
        _NC_CACHE[ntiles] = build_nc(ntiles)
    return _NC_CACHE[ntiles]


def _consts():
    import ml_dtypes
    tmod = np.tile((np.arange(TT) % 32).astype(np.float32), (128, 1))
    k = np.arange(128)
    c = np.arange(128)
    maskt = ((k[:, None] // 32 == c[None, :] // 32) & (c[None, :] >= k[:, None])
             ).astype(ml_dtypes.bfloat16)
    ident = np.eye(128).astype(ml_dtypes.bfloat16)
    return np.ascontiguousarray(tmod), np.ascontiguousarray(maskt), np.ascontiguousarray(ident)


def kernel(L0_attn_x, L0_attn_W_q, L0_attn_W_k, L0_attn_W_v, L0_attn_W_g, L0_attn_W_o,
           L0_attn_log_decay, L0_ffn_w_gate, L0_ffn_w_up, L0_ffn_w_down):
    x = np.ascontiguousarray(np.asarray(L0_attn_x, dtype=np.float32)).reshape(-1, D)
    tmod, maskt, ident = _consts()
    ld0 = np.ascontiguousarray(np.asarray(L0_attn_log_decay, np.float32)[0:1].reshape(1, 1))
    common = dict(
        wq=np.ascontiguousarray(np.asarray(L0_attn_W_q, np.float32)),
        wk=np.ascontiguousarray(np.asarray(L0_attn_W_k, np.float32)),
        wv=np.ascontiguousarray(np.asarray(L0_attn_W_v, np.float32)),
        wgt=np.ascontiguousarray(np.asarray(L0_attn_W_g, np.float32)),
        wo=np.ascontiguousarray(np.asarray(L0_attn_W_o, np.float32)),
        fg=np.ascontiguousarray(np.asarray(L0_ffn_w_gate, np.int32)),
        fu=np.ascontiguousarray(np.asarray(L0_ffn_w_up, np.int32)),
        fd=np.ascontiguousarray(np.asarray(L0_ffn_w_down, np.int32)),
        ld0=ld0, tmod=tmod, maskt=maskt, ident=ident,
    )
    in_maps = []
    for c_ in range(NCORES):
        m = dict(common)
        m["x"] = np.ascontiguousarray(x[c_ * NTOK:(c_ + 1) * NTOK])
        in_maps.append(m)
    nc = _get_nc()
    res = run_bass_kernel_spmd(nc, in_maps, core_ids=list(range(NCORES)))
    outs = [res.results[c_]["out"] for c_ in range(NCORES)]
    full = np.concatenate(outs, axis=0).reshape(np.asarray(L0_attn_x).shape)
    return full.astype(np.float32)


# version 21
# speedup vs baseline: 1.0081x; 1.0043x over previous
import numpy as np
import concourse.bass as bass
import concourse.mybir as mybir
from concourse.bass_utils import run_bass_kernel_spmd
from contextlib import ExitStack

F32 = mybir.dt.float32
BF16 = mybir.dt.bfloat16
I32 = mybir.dt.int32
AF = mybir.ActivationFunctionType
ALU = mybir.AluOpType
AX = mybir.AxisListType

NCORES = 8
NTOK = 4096
TT = 512
NTILES = NTOK // TT
NS = TT // 128
D = 1024
DC = D // 128
H = 3072
HC = H // 128
EPS = 1e-5
QSC = 1.0 / 127.0

ENGINES = ("tensor", "vector", "scalar", "sync", "gpsimd")


class Plan:
    def __init__(self):
        from collections import defaultdict
        self.ops = {e: [] for e in ENGINES}
        self.cnt = defaultdict(int)

    def op(self, eng, fn, waits=(), inc=None, amt=1):
        w = {}
        for s, v in waits:
            if v:
                w[s] = max(w.get(s, 0), v)
        self.ops[eng].append((sorted(w.items()), fn, inc, amt))
        if inc is not None:
            self.cnt[inc] += amt
            return self.cnt[inc]
        return None


class Stream:

    def __init__(self, plan, name, buf, nslots, load_fn, queue="sync"):
        self.plan = plan
        self.queue = queue
        self.name = name
        self.buf = buf
        self.nslots = nslots
        self.load_fn = load_fn
        self.src_wait = []
        self.src_wait_fn = None
        self.slot_free = [0] * nslots
        self.loaded = {}
        self.next_g = 0

    def load(self, g):
        assert g == self.next_g, (g, self.next_g)
        self.next_g += 1
        slot = g % self.nslots
        sem = f"L{self.name}{slot}"
        sw = self.src_wait_fn(g) if self.src_wait_fn else self.src_wait
        waits = list(sw) + [("ct", self.slot_free[slot])]
        cnt = self.plan.op(self.queue, self.load_fn(g, slot), waits=waits, inc=sem, amt=16)
        self.loaded[g] = (sem, cnt)

    def slot(self, g):
        return g % self.nslots

    def consume(self, g, ct_cnt):
        self.slot_free[g % self.nslots] = ct_cnt


def build_nc(ntiles=NTILES, debug=False):
    nc = bass.Bass("TRN2", target_bir_lowering=False)
    p = Plan()

    xin = nc.dram_tensor("x", [NTOK, D], F32, kind="ExternalInput").ap()
    wq = nc.dram_tensor("wq", [D, D], F32, kind="ExternalInput").ap()
    wk = nc.dram_tensor("wk", [D, D], F32, kind="ExternalInput").ap()
    wv = nc.dram_tensor("wv", [D, D], F32, kind="ExternalInput").ap()
    wgt = nc.dram_tensor("wgt", [D, D], F32, kind="ExternalInput").ap()
    wo = nc.dram_tensor("wo", [D, D], F32, kind="ExternalInput").ap()
    fg = nc.dram_tensor("fg", [H, D], I32, kind="ExternalInput").ap()
    fu = nc.dram_tensor("fu", [H, D], I32, kind="ExternalInput").ap()
    fd = nc.dram_tensor("fd", [D, H], I32, kind="ExternalInput").ap()
    ld0 = nc.dram_tensor("ld0", [1, 1], F32, kind="ExternalInput").ap()
    ident = nc.dram_tensor("ident", [128, 128], BF16, kind="ExternalInput").ap()
    tmod = nc.dram_tensor("tmod", [128, TT], F32, kind="ExternalInput").ap()
    maskt = nc.dram_tensor("maskt", [128, 128], BF16, kind="ExternalInput").ap()
    out = nc.dram_tensor("out", [NTOK, D], F32, kind="ExternalOutput").ap()

    xbf_scr = nc.dram_tensor("xbf_scr", [NTOK, D], BF16).ap()
    x1_scr = nc.dram_tensor("x1_scr", [NTOK, D], BF16).ap()
    t2 = {"q": nc.dram_tensor("t2_q", [DC * DC * 128 * 128], BF16),
          "k": nc.dram_tensor("t2_k", [DC * DC * 128 * 128], BF16),
          "g": nc.dram_tensor("t2_g", [DC * DC * 128 * 128], BF16),
          "v": nc.dram_tensor("t2_v", [DC * 128 * 1024], BF16),
          "o": nc.dram_tensor("t2_o", [DC * 128 * 1024], BF16),
          "fd": nc.dram_tensor("t2_fd", [HC * 128 * 1024], BF16),
          "fg": nc.dram_tensor("t2_fg", [HC * DC * 128 * 128], BF16),
          "fu": nc.dram_tensor("t2_fu", [HC * DC * 128 * 128], BF16)}

    st = ExitStack()
    with st:
        def sb(name, shape, dt, **kw):
            return st.enter_context(nc.sbuf_tensor(name, shape, dt, **kw))

        DFt = sb("DF", [128, TT], BF16)
        DFI = sb("DFI", [128, TT], BF16)
        MSK = sb("MSK", [128, 128], BF16)
        sld = sb("sld", [1, 1], F32)
        ones = sb("ones", [1, 128], F32)
        lvec = sb("lvec", [128, 1], F32)
        epsT = sb("epsT", [128, 1], F32)
        tmpd = sb("tmpd", [128, TT], F32)

        xbf = sb("xbf", [128, NS * D], BF16)
        xfm2 = [sb(f"xfm{i}", [128, DC * TT], BF16, is_transpose_dest=True) for i in range(2)]
        qp = sb("qp", [128, DC * TT], BF16)
        kp = sb("kp", [128, DC * TT], BF16)
        vtm = sb("vtm", [128, NS * D], BF16)
        gfm = sb("gfm", [128, DC * TT], BF16)
        gof = sb("gof", [128, NS * D], BF16)
        stm = sb("stm", [128, NS * 128], BF16)
        r1 = sb("r1", [128, NS * D], BF16)
        x1b = [sb(f"x1b{i}", [128, NS * D], BF16) for i in range(2)]
        x1fm2 = [sb(f"x1fm{i}", [128, DC * TT], BF16, is_transpose_dest=True) for i in range(2)]
        sgate = sb("sgate", [128, 2 * TT], BF16)
        sup = sb("sup", [128, 2 * TT], BF16)
        hb = sb("hb", [128, HC * TT], BF16, is_transpose_dest=True)
        r2 = sb("r2", [128, NS * D], BF16)
        outb = [sb(f"outb{i}", [128, D], F32) for i in range(2)]
        junk = sb("junk", [128, D], BF16)

        sum1 = sb("sum1", [128, NS], F32)
        sq1 = sb("sq1", [128, NS], F32)
        mst = sb("mst", [128, NS], F32)
        msq = sb("msq", [128, NS], F32)
        vart = sb("vart", [128, NS], F32)
        rstd = sb("rstd", [128, NS], F32)
        sum2 = sb("sum2", [128, NS], F32)
        sq2 = sb("sq2", [128, NS], F32)
        mst2 = sb("mst2", [128, NS], F32)
        msq2 = sb("msq2", [128, NS], F32)
        vart2 = sb("vart2", [128, NS], F32)
        rstd2 = sb("rstd2", [128, NS], F32)

        bufq = sb("bufq", [128, 2 * 1024], BF16)
        bufk = sb("bufk", [128, 2 * 1024], BF16)
        bufg = sb("bufg", [128, 2 * 1024], BF16)
        bufv = sb("bufv", [128, 2 * 1024], BF16, is_transpose_dest=True)
        stgi = [sb(f"stgi{i}", [128, 1024], I32) for i in range(2)]
        xcb = [sb("xcb0", [128, 1024], BF16)]
        identS = sb("identS", [128, 128], BF16)
        tev = [sb(f"tev{i}", [128, 1024], BF16) for i in range(2)]
        bufo = sb("bufo", [128, 3 * 1024], BF16, is_transpose_dest=True)
        buffg = sb("buffg", [128, 2 * 1024], BF16)
        buffu = sb("buffu", [128, 2 * 1024], BF16)
        buffd = sb("buffd", [128, 3 * 1024], BF16, is_transpose_dest=True)

        ps = [st.enter_context(nc.psum_tensor(f"ps{i}", [128, 1024], F32)) for i in range(4)]

        def bank_ap(b, cols=512):
            return ps[b // 2][:, (b % 2) * 512:(b % 2) * 512 + cols]

        def pair_ap(pi):
            return ps[pi]

        bank_free = {b: ("ct", 0) for b in range(8)}
        _bkcur = [0]

        def grab(n=1):
            if n == 2 and _bkcur[0] % 2:
                _bkcur[0] += 1
            ids = [(_bkcur[0] + k) % 8 for k in range(n)]
            _bkcur[0] += n
            return ids

        def take(b):
            return [bank_free[b]]

        def free(b, sem, cnt):
            bank_free[b] = (sem, cnt)

        cast_cnt = {}
        cast_x = [f"x{i}c" for i in range(ntiles)]

        p.op("gpsimd", lambda e: e.memset(ones[:], 1.0))
        p.op("gpsimd", lambda e: e.memset(epsT[:], EPS), inc="cp")

        conv_pl = [None, None]
        conv_hb = {}
        W_LANES = [(f"W{k}", hb[:, 6144 + k * 1024: 6144 + (k + 1) * 1024]) for k in range(4)]
        X_LANES = [("X0", xcb[0][:])]
        _st = dict(pl=0, xl=0, wl=0, tev=0)
        tev_free = [None, None]
        t2_done = {}

        def _load_chunk(src_ap, r, cc, is_f32, waits):
            slot = _st["pl"] % 2
            _st["pl"] += 1
            sstg = stgi[slot][:].bitcast(F32) if is_f32 else stgi[slot][:]
            lw = list(waits)
            if conv_pl[slot]:
                lw.append(conv_pl[slot])
            lc = p.op("sync",
                      lambda en, s_=src_ap, rr=r, c_=cc, d_=sstg:
                          en.dma_start(out=d_, in_=s_[rr * 128:(rr + 1) * 128,
                                                      c_ * 1024:(c_ + 1) * 1024]),
                      waits=lw, inc=f"PL{slot}", amt=16)
            return slot, sstg, lc

        def conv_x(i):
            for r in range(NS):
                slot, sstg, lc = _load_chunk(xin[i * TT:(i + 1) * TT, :], r, 0, True, [])
                lane, lap = X_LANES[_st["xl"] % len(X_LANES)]
                _st["xl"] += 1
                hw = [(f"PL{slot}", lc)]
                if conv_hb.get(lane):
                    hw.append(conv_hb[lane])
                cc_ = p.op("gpsimd",
                           lambda en, la=lap, s_=sstg: en.tensor_copy(la, s_),
                           waits=hw, inc="cp")
                conv_pl[slot] = ("cp", cc_)
                sc_ = p.op("scalar",
                           lambda en, ii=i, rr=r, la=lap:
                               en.dma_start(out=xbf_scr[ii * TT + rr * 128:
                                                        ii * TT + (rr + 1) * 128, :], in_=la),
                           waits=[("cp", cc_)], inc="S" + lane, amt=16)
                conv_hb[lane] = ("S" + lane, sc_)
            cast_cnt[f"x{i}c"] = [("SX0", p.cnt["SX0"])]

        wdone = {}

        def wconv_chunk(name, src_ap, is_f32, dst, style, r, cg, lanes=None,
                        first_gate=()):
                    lanes = lanes or W_LANES
                    slot, sstg, lc = _load_chunk(src_ap, r, cg, is_f32, [])
                    lane, lap = lanes[_st["wl"] % len(lanes)]
                    _st["wl"] += 1
                    hw = [(f"PL{slot}", lc)]
                    if conv_hb.get(lane):
                        hw.append(conv_hb[lane])
                    else:
                        hw.extend(first_gate)
                    cc_ = p.op("vector",
                               lambda en, la=lap, s_=sstg: en.tensor_copy(la, s_),
                               waits=hw, inc="cv")
                    conv_pl[slot] = ("cv", cc_)
                    b = grab()[0]
                    bw = take(b)
                    tc = None
                    for c in range(DC):
                        tc = p.op("tensor",
                                  lambda en, bb=b, la=lap, c_=c:
                                      en.transpose(
                                          bank_ap(bb).bitcast(BF16)[:, c_ * 128:(c_ + 1) * 128],
                                          la[:, c_ * 128:(c_ + 1) * 128], identS[:]),
                                  waits=([("cv", cc_), ("dS", c_idt)] + bw) if c == 0 else (),
                                  inc="ct" if c == DC - 1 else None)
                    conv_hb[lane] = ("ct", tc)
                    ts_ = _st["tev"] % 2
                    _st["tev"] += 1
                    ew = [("ct", tc)]
                    if tev_free[ts_]:
                        ew.append(tev_free[ts_])
                    ev = p.op("scalar",
                              lambda en, bb=b, t_=ts_:
                                  en.copy(tev[t_][:], bank_ap(bb).bitcast(BF16)),
                              waits=ew, inc="cs")
                    free(b, "cs", ev)
                    if style == "lhsT":
                        dap = bass.AP(dst, r * DC * 16384, [[1024, 128], [1, 1024]])
                    else:
                        dap = bass.AP(dst, cg * DC * 131072 + r * 128,
                                      [[1024, 128], [131072, DC], [1, 128]])
                    ssem = f"TV{ts_}"
                    sc_ = p.op("scalar",
                               lambda en, da=dap, t_=ts_, st_=style:
                                   en.dma_start(out=da, in_=tev[t_][:] if st_ == "lhsT"
                                                else tev[t_][:].rearrange("p (c l) -> p c l",
                                                                          l=128)),
                               waits=[("cs", ev)], inc=ssem, amt=16)
                    tev_free[ts_] = (ssem, sc_)
                    wdone.setdefault(name, []).append((ssem, sc_))

        def wconv(name, src_ap, R, C, is_f32, dst, style, cg_outer=False):
            if cg_outer:
                for cg in range(C // 1024):
                    for r in range(R // 128):
                        wconv_chunk(name, src_ap, is_f32, dst, style, r, cg)
            else:
                for r in range(R // 128):
                    for cg in range(C // 1024):
                        wconv_chunk(name, src_ap, is_f32, dst, style, r, cg)
            t2_done[name] = [("TV0", p.cnt["TV0"]), ("TV1", p.cnt["TV1"])]
            cast_cnt[name] = t2_done[name]

        p.op("sync", lambda e: e.dma_start(out=sld[:], in_=ld0), inc="dS", amt=16)
        p.op("sync", lambda e: e.dma_start(out=tmpd[:], in_=tmod), inc="dS", amt=16)
        p.op("sync", lambda e: e.dma_start(out=MSK[:], in_=maskt), inc="dS", amt=16)
        c_idt = p.op("sync", lambda e: e.dma_start(out=identS[:], in_=ident), inc="dS", amt=16)
        c_con = c_idt
        c_sld = c_tmd = c_msk = c_con

        c_bc = p.op("tensor", lambda e: e.matmul(ps[0][:, 0:1], ones[:], sld[:],
                                                 start=True, stop=True),
                    waits=[("dS", c_sld), ("cp", 1)], inc="ct")
        c_sg = p.op("scalar", lambda e: e.activation(lvec[:], ps[0][:, 0:1], AF.Sigmoid),
                    waits=[("ct", c_bc)], inc="cs")
        c_ln = p.op("scalar", lambda e: e.activation(lvec[:], lvec[:], AF.Ln),
                    waits=[("cs", c_sg)], inc="cs")
        c_tm = p.op("vector", lambda e: e.tensor_scalar(tmpd[:], tmpd[:], lvec[:, 0:1], None,
                                                        ALU.mult),
                    waits=[("cs", c_ln), ("dS", c_tmd)], inc="cv")
        p.op("scalar", lambda e: e.activation(DFt[:], tmpd[:], AF.Exp),
             waits=[("cv", c_tm)], inc="cs")
        c_dfi = p.op("scalar", lambda e: e.activation(DFI[:], tmpd[:], AF.Exp, scale=-1.0),
                     inc="cs")
        free(0, "cs", c_sg)

        wconv("wv", wv, D, D, True, t2["v"], "rhs")
        wconv("wq", wq, D, D, True, t2["q"], "lhsT")
        conv_x(0)
        if ntiles > 1:
            conv_x(1)

        def mk_scr2_loader(t2t, npt, buf):
            def mk(g, slot):
                j = g % npt
                def fn(e):
                    src = bass.AP(t2t, j * DC * 16384, [[128, 128], [16384, DC], [1, 128]])
                    dst = buf[:, slot * 1024:(slot + 1) * 1024].rearrange(
                        "p (e l) -> p e l", l=128)
                    return e.dma_start(out=dst, in_=src)
                return fn
            return mk

        def mk_piece_loader(t2t, npt, buf):
            def mk(g, slot):
                j = g % npt
                def fn(e):
                    src = bass.AP(t2t, j * 131072, [[1024, 128], [1, 1024]])
                    return e.dma_start(out=buf[:, slot * 1024:(slot + 1) * 1024], in_=src)
                return fn
            return mk

        MQ = Stream(p, "q", bufq, 2, mk_piece_loader(t2["q"], DC, bufq))
        MQ.src_wait_fn = lambda g: [wdone["wq"][g % DC]]
        MK = Stream(p, "k", bufk, 2, mk_piece_loader(t2["k"], DC, bufk))
        MK.src_wait_fn = lambda g: [wdone["wk"][g % DC]]
        MG = Stream(p, "g", bufg, 2, mk_piece_loader(t2["g"], DC, bufg))
        MG.src_wait_fn = lambda g: [wdone["wg"][g % DC]]
        MV = Stream(p, "v", bufv, 2, mk_piece_loader(t2["v"], DC, bufv))
        MV.src_wait_fn = lambda g: [wdone["wv"][-1]]
        MO = Stream(p, "o", bufo, 3, mk_piece_loader(t2["o"], DC, bufo))
        MO.src_wait_fn = lambda g: [wdone["wo"][-1]]
        MFG = Stream(p, "G", buffg, 2, mk_piece_loader(t2["fg"], HC, buffg))
        MFG.src_wait_fn = lambda g: [wdone["fg"][g % HC]]
        MFU = Stream(p, "U", buffu, 2, mk_piece_loader(t2["fu"], HC, buffu))
        MFU.src_wait_fn = lambda g: [wdone["fu"][g % HC]]
        MFD = Stream(p, "D", buffd, 3, mk_piece_loader(t2["fd"], HC, buffd))
        MFD.src_wait_fn = lambda g: [wdone["fd"][((g % HC) // DC) * DC + DC - 1]]

        state = dict(xfm_free0=0, xfm_free1=0, x1fm_free0=0, x1fm_free1=0, xbf_free=0,
                     qp_free=0, kp_free=0, r1_free_cs=0)
        x1b_store = [None, None]
        out_store = [None, None]

        def emit_inputs(ti):
            par = ti % 2
            t0 = ti * TT
            xfm = xfm2[par]
            xfm_loads = None
            for e in range(DC):
                w = (list(cast_cnt[cast_x[ti]]) + [("ct", state[f"xfm_free{par}"])]) if e == 0 else ()
                cnt = p.op("scalar",
                           lambda en, ee=e, tt0=t0, xf=xfm: en.dma_start(
                               out=xf[:, ee * TT:(ee + 1) * TT],
                               in_=xbf_scr[tt0:tt0 + TT, ee * 128:(ee + 1) * 128],
                               transpose=True),
                           waits=w, inc=f"DXF{par}", amt=16)
                xfm_loads = (f"DXF{par}", cnt)
            cnt = p.op("scalar",
                       lambda en, tt0=t0: en.dma_start(
                           out=xbf[:].rearrange("p (s d) -> p s d", d=D),
                           in_=bass.AP(xbf_scr.tensor, tt0 * D,
                                       [[D, 128], [128 * D, NS], [1, D]])),
                       waits=list(cast_cnt[cast_x[ti]]) + [("cv", state["xbf_free"])],
                       inc="DXB", amt=16)
            state[f"inp_{ti}"] = (xfm_loads, ("DXB", cnt))

        def emit_attn_body(ti):
            par = ti % 2
            t0 = ti * TT
            xfm = xfm2[par]
            xfm_loads, xbf_load = state[f"inp_{ti}"]

            q_evac_last = k_evac_last = 0
            for which, M, dst, dfap, freekey in (("q", MQ, qp, DFt, "qp_free"),
                                                 ("k", MK, kp, DFI, "kp_free")):
                if ti == 0 and which == "k":
                    wconv("wk", wk, D, D, True, t2["k"], "lhsT")
                base_g = M.next_g
                M.load(base_g)
                M.load(base_g + 1)
                gc = None
                for j in range(DC):
                    g = base_g + j
                    b = grab()[0]
                    slot = M.slot(g)
                    waits = [M.loaded[g], xfm_loads] + take(b)
                    for e in range(DC):
                        gc = p.op("tensor",
                                  lambda en, bb=b, sl=slot, ee=e, Mb=M.buf, xf=xfm:
                                      en.matmul(bank_ap(bb),
                                                Mb[:, sl * 1024 + ee * 128:sl * 1024 + (ee + 1) * 128],
                                                xf[:, ee * TT:(ee + 1) * TT],
                                                start=(ee == 0), stop=(ee == DC - 1)),
                                  waits=waits if e == 0 else (),
                                  inc="ct" if e == DC - 1 else None)
                    M.consume(g, gc)
                    if g + 2 < base_g + DC:
                        M.load(g + 2)
                    ew = [("ct", gc), ("cs", c_dfi)]
                    if j == 0:
                        ew.append(("ct", state[freekey]))
                    ec = p.op("vector",
                              lambda en, bb=b, jj=j, dd=dst, df=dfap:
                                  en.tensor_tensor(dd[:, jj * TT:(jj + 1) * TT],
                                                   bank_ap(bb), df[:], ALU.mult),
                              waits=ew, inc="cv")
                    free(b, "cv", ec)
                    if which == "q":
                        q_evac_last = ec
                    else:
                        k_evac_last = ec

            v_evac_last = 0
            for sub in range(1):
                groups = [(s, h) for s in range(NS) for h in range(2)]
                gbanks = [grab()[0] for _ in groups]
                base_g = MV.next_g
                MV.load(base_g)
                MV.load(base_g + 1)
                gct = None
                for e in range(DC):
                    g = base_g + e
                    slot = MV.slot(g)
                    for gi, (s, h) in enumerate(groups):
                        b = gbanks[gi]
                        if e == 0:
                            waits = [MV.loaded[g], xfm_loads] + take(b)
                        elif gi == 0:
                            waits = [MV.loaded[g]]
                        else:
                            waits = ()
                        gct = p.op("tensor",
                                   lambda en, bb=b, ss=s, hh=h, ee=e, sl=slot, xf=xfm:
                                       en.matmul(bank_ap(bb),
                                                 xf[:, ee * TT + ss * 128:ee * TT + (ss + 1) * 128],
                                                 bufv[:, sl * 1024 + hh * 512:sl * 1024 + (hh + 1) * 512],
                                                 start=(ee == 0), stop=(ee == DC - 1)),
                                   waits=waits,
                                   inc="ct" if gi == len(groups) - 1 else None)
                    MV.consume(g, gct)
                    if e + 2 < DC:
                        MV.load(g + 2)
                for gi, (s, h) in enumerate(groups):
                    b = gbanks[gi]
                    ec = p.op("vector",
                              lambda en, bb=b, ss=s, hh=h:
                                  en.tensor_copy(vtm[:, ss * D + hh * 512:ss * D + (hh + 1) * 512],
                                                 bank_ap(bb)),
                              waits=[("ct", gct)], inc="cv")
                    free(b, "cv", ec)
                    v_evac_last = ec

            if ti == 0:
                wconv("wg", wgt, D, D, True, t2["g"], "lhsT")
                wconv("wo", wo, D, D, True, t2["o"], "rhs")
            g_evac_last = 0
            base_g = MG.next_g
            MG.load(base_g)
            MG.load(base_g + 1)
            gc = None
            for j in range(DC):
                g = base_g + j
                b = grab()[0]
                slot = MG.slot(g)
                waits = [MG.loaded[g], xfm_loads] + take(b)
                for e in range(DC):
                    gc = p.op("tensor",
                              lambda en, bb=b, sl=slot, ee=e, xf=xfm:
                                  en.matmul(bank_ap(bb),
                                            bufg[:, sl * 1024 + ee * 128:sl * 1024 + (ee + 1) * 128],
                                            xf[:, ee * TT:(ee + 1) * TT],
                                            start=(ee == 0), stop=(ee == DC - 1)),
                              waits=waits if e == 0 else (),
                              inc="ct" if e == DC - 1 else None)
                MG.consume(g, gc)
                if g + 2 < base_g + DC:
                    MG.load(g + 2)
                ec = p.op("scalar",
                          lambda en, bb=b, jj=j:
                              en.activation(gfm[:, jj * TT:(jj + 1) * TT], bank_ap(bb),
                                            AF.Sigmoid),
                          waits=[("ct", gc)], inc="cs")
                free(b, "cs", ec)
                g_evac_last = ec
            xfm_free_new = gc

            s_evac = []
            gc = None
            for s in range(NS):
                b = grab()[0]
                waits = [("cv", max(q_evac_last, k_evac_last))] + take(b)
                for e in range(DC):
                    gc = p.op("tensor",
                              lambda en, bb=b, ss=s, ee=e:
                                  en.matmul(bank_ap(bb, 128),
                                            kp[:, ee * TT + ss * 128:ee * TT + (ss + 1) * 128],
                                            qp[:, ee * TT + ss * 128:ee * TT + (ss + 1) * 128],
                                            start=(ee == 0), stop=(ee == DC - 1)),
                              waits=waits if e == 0 else (),
                              inc="ct" if e == DC - 1 else None)
                ec = p.op("vector",
                          lambda en, bb=b, ss=s:
                              en.tensor_tensor(stm[:, ss * 128:(ss + 1) * 128],
                                               bank_ap(bb, 128), MSK[:], ALU.mult),
                          waits=[("ct", gc), ("dS", c_msk)], inc="cv")
                free(b, "cv", ec)
                s_evac.append(ec)
            s_phase_end = gc

            gof_evac = []
            gc = None
            for s in range(NS):
                pids = grab(2)
                pi = pids[0] // 2
                waits = [("cv", v_evac_last), ("cv", s_evac[s])] + take(pids[0]) + take(pids[1])
                for j in range(DC):
                    gc = p.op("tensor",
                              lambda en, pp=pi, ss=s, jj=j:
                                  en.matmul(pair_ap(pp)[:, jj * 128:(jj + 1) * 128],
                                            vtm[:, ss * D + jj * 128:ss * D + (jj + 1) * 128],
                                            stm[:, ss * 128:(ss + 1) * 128],
                                            start=True, stop=True),
                              waits=waits if j == 0 else (),
                              inc="ct" if j == DC - 1 else None)
                ec = p.op("vector",
                          lambda en, pp=pi, ss=s:
                              en.tensor_tensor(
                                  gof[:, ss * D:(ss + 1) * D].rearrange("p (j l) -> p j l", l=128),
                                  pair_ap(pp).rearrange("p (j l) -> p j l", l=128),
                                  gfm.rearrange("p (j t) -> p j t", t=TT)[:, :, ss * 128:(ss + 1) * 128],
                                  ALU.mult),
                          waits=[("ct", gc), ("cs", g_evac_last)], inc="cv")
                free(pids[0], "cv", ec)
                free(pids[1], "cv", ec)
                gof_evac.append(ec)
            o_phase_end = gc

            r1_evac_last = 0
            wgroups = [(s, h) for s in range(NS) for h in range(2)]
            wbanks = [grab()[0] for _ in wgroups]
            base_g = MO.next_g
            for la in range(2):
                MO.load(base_g + la)
            gct = None
            for e in range(DC):
                g = base_g + e
                slot = MO.slot(g)
                for gi, (s, h) in enumerate(wgroups):
                    b = wbanks[gi]
                    if e == 0:
                        waits = [MO.loaded[g], ("cv", gof_evac[NS - 1])] + take(b)
                    elif gi == 0:
                        waits = [MO.loaded[g]]
                    else:
                        waits = ()
                    gct = p.op("tensor",
                               lambda en, bb=b, ss=s, ee=e, sl=slot, hh=h:
                                   en.matmul(bank_ap(bb),
                                             gof[:, (ss * DC + ee) * 128:(ss * DC + ee + 1) * 128],
                                             bufo[:, sl * 1024 + hh * 512:sl * 1024 + (hh + 1) * 512],
                                             start=(ee == 0), stop=(ee == DC - 1)),
                               waits=waits,
                               inc="ct" if gi == len(wgroups) - 1 else None)
                MO.consume(g, gct)
                if e + 2 < DC:
                    MO.load(g + 2)
            for gi, (s, h) in enumerate(wgroups):
                b = wbanks[gi]
                ec = p.op("vector",
                          lambda en, bb=b, ss=s, hh=h:
                              en.scalar_tensor_tensor(
                                  r1[:, ss * D + hh * 512:ss * D + (hh + 1) * 512],
                                  xbf[:, ss * D + hh * 512:ss * D + (hh + 1) * 512],
                                  2.0, bank_ap(bb), ALU.mult, ALU.add),
                          waits=[("ct", gct), xbf_load,
                                 ("cs", state["r1_free_cs"])],
                          inc="cv")
                free(b, "cv", ec)
                r1_evac_last = ec

            c_red = p.op("vector",
                         lambda en: en.reduce_sum(sum1[:, 0:NS],
                                                  r1[:].rearrange("p (s d) -> p s d", d=D), AX.X),
                         waits=[("cv", r1_evac_last)], inc="cv")
            sq_last = 0
            for s in range(NS):
                sq_last = p.op("vector",
                               lambda en, ss=s: en.scalar_tensor_tensor(
                                   junk[:], r1[:, ss * D:(ss + 1) * D], 1.0,
                                   r1[:, ss * D:(ss + 1) * D], ALU.mult, ALU.mult,
                                   accum_out=sq1[:, ss:ss + 1]),
                               waits=(), inc="cv")
            c_mean = p.op("vector", lambda en: en.tensor_scalar(mst[:], sum1[:], 1.0 / D, None,
                                                                ALU.mult),
                          waits=[("cv", c_red)], inc="cv")
            c_msq = p.op("vector", lambda en: en.tensor_tensor(msq[:], mst[:], mst[:], ALU.mult),
                         waits=[("cv", c_mean)], inc="cv")
            c_var = p.op("vector",
                         lambda en: en.scalar_tensor_tensor(vart[:], sq1[:], 1.0 / D, msq[:],
                                                            ALU.mult, ALU.subtract),
                         waits=[("cv", sq_last), ("cv", c_msq)], inc="cv")
            c_lnv = p.op("scalar", lambda en: en.activation(vart[:], vart[:], AF.Ln, bias=epsT[:]),
                         waits=[("cv", c_var)], inc="cs")
            c_rstd = p.op("scalar", lambda en: en.activation(rstd[:], vart[:], AF.Exp, scale=-0.5),
                          waits=[("cs", c_lnv)], inc="cs")
            x1r = 0
            for s in range(NS):
                w = [("cs", c_rstd)]
                if s == 0 and x1b_store[par] is not None:
                    w.append(x1b_store[par])
                x1r = p.op("vector",
                           lambda en, ss=s, pp=par:
                               en.tensor_scalar(x1b[pp][:, ss * D:(ss + 1) * D],
                                                r1[:, ss * D:(ss + 1) * D],
                                                mst[:, ss:ss + 1], rstd[:, ss:ss + 1],
                                                ALU.subtract, ALU.mult),
                           waits=w, inc="cv")

            x1b_store[par] = ("DX1S", p.op("scalar",
                                  lambda en, tt0=t0, pp=par: en.dma_start(
                                      out=bass.AP(x1_scr.tensor, tt0 * D,
                                                  [[D, 128], [128 * D, NS], [1, D]]),
                                      in_=x1b[pp][:].rearrange("p (s d) -> p s d", d=D)),
                                  waits=[("cv", x1r)], inc="DX1S", amt=16))

            state[f"xfm_free{par}"] = xfm_free_new
            state["xbf_free"] = r1_evac_last
            state["qp_free"] = s_phase_end
            state["kp_free"] = s_phase_end
            state["r1_free_cs"] = 0

        def emit_x1fm(ti):
            par = ti % 2
            t0 = ti * TT
            x1f = x1fm2[par]
            x1fm_loads = None
            for e in range(DC):
                w = [x1b_store[par], ("ct", state[f"x1fm_free{par}"])] if e == 0 else ()
                cnt = p.op("scalar",
                           lambda en, ee=e, tt0=t0, xf=x1f: en.dma_start(
                               out=xf[:, ee * TT:(ee + 1) * TT],
                               in_=x1_scr[tt0:tt0 + TT, ee * 128:(ee + 1) * 128],
                               transpose=True),
                           waits=w, inc=f"DX1F{par}", amt=16)
                x1fm_loads = (f"DX1F{par}", cnt)
            state[f"x1fm_loads_{ti}"] = x1fm_loads

        def emit_ffn_gateup(ti):
            par = ti % 2
            t0 = ti * TT
            x1fm = x1fm2[par]
            x1fm_loads = state[f"x1fm_loads_{ti}"]
            if ti == 0:
                state["prologue_store_done"] = [("TV0", p.cnt["TV0"]),
                                                ("TV1", p.cnt["TV1"])]

            h_last = 0
            LAG = 8
            if ti == 0:
                F_LANES = [("F0", qp[:, 0:1024]), ("F1", qp[:, 1024:2048]),
                           ("F2", kp[:, 0:1024]), ("F3", kp[:, 1024:2048])]
                fgate = [("ct", state["qp_free"])]
                for jj in range(LAG):
                    wconv_chunk("fg", fg, False, t2["fg"], "lhsT", jj, 0, F_LANES, fgate)
                    wconv_chunk("fu", fu, False, t2["fu"], "lhsT", jj, 0, F_LANES, fgate)
                    wconv_chunk("fd", fd, False, t2["fd"], "rhs", jj % DC, jj // DC,
                                F_LANES, fgate)
            base_fg = MFG.next_g
            base_fu = MFU.next_g
            MFG.load(base_fg)
            MFU.load(base_fu)
            MFG.load(base_fg + 1)
            MFU.load(base_fu + 1)
            sg_free = [0, 0]
            gcu = None
            for j in range(HC):
                if ti == 0 and j + LAG < HC:
                    jj = j + LAG
                    wconv_chunk("fg", fg, False, t2["fg"], "lhsT", jj, 0, F_LANES)
                    wconv_chunk("fu", fu, False, t2["fu"], "lhsT", jj, 0, F_LANES)
                    wconv_chunk("fd", fd, False, t2["fd"], "rhs", jj % DC, jj // DC,
                                F_LANES)
                gg = base_fg + j
                gu = base_fu + j
                bgate = grab()[0]
                bup = grab()[0]
                slotg = MFG.slot(gg)
                slotu = MFU.slot(gu)
                waits = [MFG.loaded[gg], x1fm_loads] + take(bgate)
                gcg = None
                for e in range(DC):
                    gcg = p.op("tensor",
                               lambda en, bb=bgate, sl=slotg, ee=e, x1f_=x1fm:
                                   en.matmul(bank_ap(bb),
                                             buffg[:, sl * 1024 + ee * 128:sl * 1024 + (ee + 1) * 128],
                                             x1f_[:, ee * TT:(ee + 1) * TT],
                                             start=(ee == 0), stop=(ee == DC - 1)),
                               waits=waits if e == 0 else (),
                               inc="ct" if e == DC - 1 else None)
                MFG.consume(gg, gcg)
                waits = [MFU.loaded[gu]] + take(bup)
                for e in range(DC):
                    gcu = p.op("tensor",
                               lambda en, bb=bup, sl=slotu, ee=e, x1f_=x1fm:
                                   en.matmul(bank_ap(bb),
                                             buffu[:, sl * 1024 + ee * 128:sl * 1024 + (ee + 1) * 128],
                                             x1f_[:, ee * TT:(ee + 1) * TT],
                                             start=(ee == 0), stop=(ee == DC - 1)),
                               waits=waits if e == 0 else (),
                               inc="ct" if e == DC - 1 else None)
                MFU.consume(gu, gcu)
                if j + 2 < HC:
                    MFG.load(gg + 2)
                    MFU.load(gu + 2)
                sl2 = j % 2
                e1 = p.op("scalar",
                          lambda en, bb=bgate, s2=sl2:
                              en.activation(sgate[:, s2 * TT:(s2 + 1) * TT], bank_ap(bb),
                                            AF.Silu, scale=QSC),
                          waits=[("ct", gcg), ("cv", sg_free[sl2])], inc="cs")
                free(bgate, "cs", e1)
                e2 = p.op("vector",
                          lambda en, bb=bup, s2=sl2:
                              en.tensor_scalar(sup[:, s2 * TT:(s2 + 1) * TT], bank_ap(bb),
                                               QSC * QSC, None, ALU.mult),
                          waits=[("ct", gcu)], inc="cv")
                free(bup, "cv", e2)
                hw = [("cs", e1), ("cv", e2)]
                if ti == 0 and j == 0:
                    hw.extend(state["prologue_store_done"])
                h_last = p.op("vector",
                              lambda en, jj=j, s2=sl2:
                                  en.tensor_tensor(hb[:, jj * TT:(jj + 1) * TT],
                                                   sgate[:, s2 * TT:(s2 + 1) * TT],
                                                   sup[:, s2 * TT:(s2 + 1) * TT], ALU.mult),
                              waits=hw, inc="cv")
                sg_free[sl2] = h_last

            state[f"hlast_{ti}"] = h_last
            state[f"x1fm_free{par}"] = gcu

        def emit_ffn_down(ti):
            par = ti % 2
            t0 = ti * TT
            h_last = state[f"hlast_{ti}"]
            r2_evac_last = 0
            dgroups = [(s, h) for s in range(NS) for h in range(2)]
            dbanks = [grab()[0] for _ in dgroups]
            base_g = MFD.next_g
            for la in range(2):
                MFD.load(base_g + la)
            gct = None
            for jh in range(HC):
                g = base_g + jh
                slot = MFD.slot(g)
                for gi, (s, h) in enumerate(dgroups):
                    b = dbanks[gi]
                    if jh == 0:
                        waits = [MFD.loaded[g], ("cv", h_last)] + take(b)
                    elif gi == 0:
                        waits = [MFD.loaded[g]]
                    else:
                        waits = ()
                    gct = p.op("tensor",
                               lambda en, bb=b, ss=s, jj=jh, sl=slot, hh=h:
                                   en.matmul(bank_ap(bb),
                                             hb[:, jj * TT + ss * 128:jj * TT + (ss + 1) * 128],
                                             buffd[:, sl * 1024 + hh * 512:sl * 1024 + (hh + 1) * 512],
                                             start=(jj == 0), stop=(jj == HC - 1)),
                               waits=waits,
                               inc="ct" if gi == len(dgroups) - 1 else None)
                MFD.consume(g, gct)
                if jh + 2 < HC:
                    MFD.load(g + 2)
            for gi, (s, h) in enumerate(dgroups):
                b = dbanks[gi]
                ec = p.op("vector",
                          lambda en, bb=b, ss=s, hh=h, pp=par:
                              en.scalar_tensor_tensor(
                                  r2[:, ss * D + hh * 512:ss * D + (hh + 1) * 512],
                                  x1b[pp][:, ss * D + hh * 512:ss * D + (hh + 1) * 512],
                                  1.0, bank_ap(bb), ALU.mult, ALU.add),
                          waits=[("ct", gct)], inc="cv")
                free(b, "cv", ec)
                r2_evac_last = ec

            c_red2 = p.op("vector",
                          lambda en: en.reduce_sum(sum2[:, 0:NS],
                                                   r2[:].rearrange("p (s d) -> p s d", d=D), AX.X),
                          waits=[("cv", r2_evac_last)], inc="cv")
            sq_last2 = 0
            for s in range(NS):
                sq_last2 = p.op("vector",
                                lambda en, ss=s: en.scalar_tensor_tensor(
                                    junk[:], r2[:, ss * D:(ss + 1) * D], 1.0,
                                    r2[:, ss * D:(ss + 1) * D], ALU.mult, ALU.mult,
                                    accum_out=sq2[:, ss:ss + 1]),
                                waits=(), inc="cv")
            c_mean2 = p.op("vector", lambda en: en.tensor_scalar(mst2[:], sum2[:], 1.0 / D, None,
                                                                 ALU.mult),
                           waits=[("cv", c_red2)], inc="cv")
            c_msq2 = p.op("vector", lambda en: en.tensor_tensor(msq2[:], mst2[:], mst2[:], ALU.mult),
                          waits=[("cv", c_mean2)], inc="cv")
            c_var2 = p.op("vector",
                          lambda en: en.scalar_tensor_tensor(vart2[:], sq2[:], 1.0 / D, msq2[:],
                                                             ALU.mult, ALU.subtract),
                          waits=[("cv", sq_last2), ("cv", c_msq2)], inc="cv")
            c_lnv2 = p.op("scalar", lambda en: en.activation(vart2[:], vart2[:], AF.Ln, bias=epsT[:]),
                          waits=[("cv", c_var2)], inc="cs")
            c_rstd2 = p.op("scalar", lambda en: en.activation(rstd2[:], vart2[:], AF.Exp, scale=-0.5),
                           waits=[("cs", c_lnv2)], inc="cs")
            for s in range(NS):
                op_ = s % 2
                w = [("cs", c_rstd2)]
                if out_store[op_] is not None:
                    w.append(out_store[op_])
                cnrm = p.op("vector",
                            lambda en, ss=s, oo=op_:
                                en.tensor_scalar(outb[oo][:], r2[:, ss * D:(ss + 1) * D],
                                                 mst2[:, ss:ss + 1], rstd2[:, ss:ss + 1],
                                                 ALU.subtract, ALU.mult),
                            waits=w, inc="cv")
                out_store[op_] = (f"DOS{op_}", p.op("scalar",
                                      lambda en, ss=s, oo=op_, tt0=t0: en.dma_start(
                                          out=out[tt0 + ss * 128:tt0 + (ss + 1) * 128, :],
                                          in_=outb[oo][:]),
                                      waits=[("cv", cnrm)], inc=f"DOS{op_}", amt=16))

        emit_inputs(0)
        emit_attn_body(0)
        emit_x1fm(0)
        if ntiles > 1:
            if ntiles > 2:
                conv_x(2)
            if ntiles > 3:
                conv_x(3)
            emit_inputs(1)
            emit_attn_body(1)
        for _i in range(1, ntiles):
            emit_ffn_gateup(_i - 1)
            if _i + 1 < ntiles:
                if _i + 3 < ntiles:
                    conv_x(_i + 3)
                emit_inputs(_i + 1)
            emit_x1fm(_i)
            emit_ffn_down(_i - 1)
            if _i + 1 < ntiles:
                emit_attn_body(_i + 1)
        emit_ffn_gateup(ntiles - 1)
        emit_ffn_down(ntiles - 1)

        if debug:
            fin00 = dict(p.cnt)
            for nm, srcap, shp in [("x1scr", x1_scr, [TT, D]), ("xbfscr", xbf_scr, [TT, D])]:
                dout = nc.dram_tensor("dbg_" + nm, shp, BF16, kind="ExternalOutput").ap()
                p.op("gpsimd",
                     lambda en, d=dout, s=srcap, sh=shp: en.dma_start(out=d, in_=s[0:sh[0], :]),
                     waits=[(s_, v_) for s_, v_ in fin00.items() if v_ > 0],
                     inc="dS", amt=16)
            dbg_specs = [("qp", qp, BF16), ("kp", kp, BF16), ("vtm", vtm, BF16),
                         ("gfm", gfm, BF16), ("stm", stm, BF16), ("gof", gof, BF16),
                         ("r1", r1, BF16), ("x1b0", x1b[0], BF16), ("hb", hb, BF16),
                         ("r2", r2, BF16), ("xfm", xfm2[0], BF16), ("x1fm", x1fm2[0], BF16),
                         ("dft", DFt, BF16), ("dfi", DFI, BF16), ("xbf", xbf, BF16),
                         ("mstd", mst, F32), ("rstdd", rstd, F32)]
            fin0 = dict(p.cnt)
            w0 = [("ct", fin0["ct"]), ("cv", fin0["cv"]), ("cs", fin0["cs"])]
            for name, buf, dt in dbg_specs:
                shape = [buf.shape[0] if hasattr(buf, "shape") else 128, buf[:].shape[1]]
                dout = nc.dram_tensor("dbg_" + name, list(buf[:].shape), dt,
                                      kind="ExternalOutput").ap()
                p.op("sync", lambda en, d=dout, b=buf: en.dma_start(out=d, in_=b[:]),
                     waits=w0, inc="dS", amt=16)
                w0 = []

        fin = dict(p.cnt)
        p.op("gpsimd", lambda en: en.nop(),
             waits=[(s_, v_) for s_, v_ in fin.items() if v_ > 0])

        for _e, _ops in p.ops.items():
            for _w, _f, _i, _a in _ops:
                for _s, _v in _w:
                    assert _v <= p.cnt.get(_s, 0), (_e, _s, _v, p.cnt.get(_s, 0))
        _pos = {e_: 0 for e_ in p.ops}
        _cnt = {s_: 0 for s_ in p.cnt}
        _prog = True
        while _prog:
            _prog = False
            for _e, _ops in p.ops.items():
                while _pos[_e] < len(_ops):
                    _w, _f, _i, _a = _ops[_pos[_e]]
                    if all(_cnt.get(s_, 0) >= v_ for s_, v_ in _w):
                        if _i:
                            _cnt[_i] = _cnt.get(_i, 0) + _a
                        _pos[_e] += 1
                        _prog = True
                    else:
                        break
        assert all(_pos[e_] == len(p.ops[e_]) for e_ in p.ops), \
            {e_: (_pos[e_], len(p.ops[e_])) for e_ in p.ops if _pos[e_] < len(p.ops[e_])}

        sems = {s_: st.enter_context(nc.semaphore(s_)) for s_ in p.cnt}
        block = st.enter_context(nc.Block())

        def runner(ename):
            def f(eng):
                for waits, fn, inc, amt in p.ops[ename]:
                    for s_name, v in waits:
                        eng.wait_ge(sems[s_name], v)
                    ins = fn(eng)
                    if inc is not None:
                        ins.then_inc(sems[inc], amt)
            return f

        block.tensor(runner("tensor"))
        block.vector(runner("vector"))
        block.scalar(runner("scalar"))
        block.sync(runner("sync"))
        block.gpsimd(runner("gpsimd"))

    return nc


_NC_CACHE = {}


def _get_nc(ntiles=NTILES):
    if ntiles not in _NC_CACHE:
        _NC_CACHE[ntiles] = build_nc(ntiles)
    return _NC_CACHE[ntiles]


def _consts():
    import ml_dtypes
    tmod = np.tile((np.arange(TT) % 32).astype(np.float32), (128, 1))
    k = np.arange(128)
    c = np.arange(128)
    maskt = ((k[:, None] // 32 == c[None, :] // 32) & (c[None, :] >= k[:, None])
             ).astype(ml_dtypes.bfloat16)
    ident = np.eye(128).astype(ml_dtypes.bfloat16)
    return np.ascontiguousarray(tmod), np.ascontiguousarray(maskt), np.ascontiguousarray(ident)


def kernel(L0_attn_x, L0_attn_W_q, L0_attn_W_k, L0_attn_W_v, L0_attn_W_g, L0_attn_W_o,
           L0_attn_log_decay, L0_ffn_w_gate, L0_ffn_w_up, L0_ffn_w_down):
    x = np.ascontiguousarray(np.asarray(L0_attn_x, dtype=np.float32)).reshape(-1, D)
    tmod, maskt, ident = _consts()
    ld0 = np.ascontiguousarray(np.asarray(L0_attn_log_decay, np.float32)[0:1].reshape(1, 1))
    common = dict(
        wq=np.ascontiguousarray(np.asarray(L0_attn_W_q, np.float32)),
        wk=np.ascontiguousarray(np.asarray(L0_attn_W_k, np.float32)),
        wv=np.ascontiguousarray(np.asarray(L0_attn_W_v, np.float32)),
        wgt=np.ascontiguousarray(np.asarray(L0_attn_W_g, np.float32)),
        wo=np.ascontiguousarray(np.asarray(L0_attn_W_o, np.float32)),
        fg=np.ascontiguousarray(np.asarray(L0_ffn_w_gate, np.int32)),
        fu=np.ascontiguousarray(np.asarray(L0_ffn_w_up, np.int32)),
        fd=np.ascontiguousarray(np.asarray(L0_ffn_w_down, np.int32)),
        ld0=ld0, tmod=tmod, maskt=maskt, ident=ident,
    )
    in_maps = []
    for c_ in range(NCORES):
        m = dict(common)
        m["x"] = np.ascontiguousarray(x[c_ * NTOK:(c_ + 1) * NTOK])
        in_maps.append(m)
    nc = _get_nc()
    res = run_bass_kernel_spmd(nc, in_maps, core_ids=list(range(NCORES)))
    outs = [res.results[c_]["out"] for c_ in range(NCORES)]
    full = np.concatenate(outs, axis=0).reshape(np.asarray(L0_attn_x).shape)
    return full.astype(np.float32)


# version 23
# speedup vs baseline: 1.0111x; 1.0030x over previous
import numpy as np
import concourse.bass as bass
import concourse.mybir as mybir
from concourse.bass_utils import run_bass_kernel_spmd
from contextlib import ExitStack

F32 = mybir.dt.float32
BF16 = mybir.dt.bfloat16
I32 = mybir.dt.int32
AF = mybir.ActivationFunctionType
ALU = mybir.AluOpType
AX = mybir.AxisListType

NCORES = 8
NTOK = 4096
TT = 512
NTILES = NTOK // TT
NS = TT // 128
D = 1024
DC = D // 128
H = 3072
HC = H // 128
EPS = 1e-5
QSC = 1.0 / 127.0

ENGINES = ("tensor", "vector", "scalar", "sync", "gpsimd")


class Plan:
    def __init__(self):
        from collections import defaultdict
        self.ops = {e: [] for e in ENGINES}
        self.cnt = defaultdict(int)

    def op(self, eng, fn, waits=(), inc=None, amt=1):
        w = {}
        for s, v in waits:
            if v:
                w[s] = max(w.get(s, 0), v)
        self.ops[eng].append((sorted(w.items()), fn, inc, amt))
        if inc is not None:
            self.cnt[inc] += amt
            return self.cnt[inc]
        return None


class Stream:

    def __init__(self, plan, name, buf, nslots, load_fn, queue="sync"):
        self.plan = plan
        self.queue = queue
        self.name = name
        self.buf = buf
        self.nslots = nslots
        self.load_fn = load_fn
        self.src_wait = []
        self.src_wait_fn = None
        self.slot_free = [0] * nslots
        self.loaded = {}
        self.next_g = 0

    def load(self, g):
        assert g == self.next_g, (g, self.next_g)
        self.next_g += 1
        slot = g % self.nslots
        sem = f"L{self.name}{slot}"
        sw = self.src_wait_fn(g) if self.src_wait_fn else self.src_wait
        waits = list(sw) + [("ct", self.slot_free[slot])]
        cnt = self.plan.op(self.queue, self.load_fn(g, slot), waits=waits, inc=sem, amt=16)
        self.loaded[g] = (sem, cnt)

    def slot(self, g):
        return g % self.nslots

    def consume(self, g, ct_cnt):
        self.slot_free[g % self.nslots] = ct_cnt


def build_nc(ntiles=NTILES, debug=False):
    nc = bass.Bass("TRN2", target_bir_lowering=False)
    p = Plan()

    xin = nc.dram_tensor("x", [NTOK, D], F32, kind="ExternalInput").ap()
    wq = nc.dram_tensor("wq", [D, D], F32, kind="ExternalInput").ap()
    wk = nc.dram_tensor("wk", [D, D], F32, kind="ExternalInput").ap()
    wv = nc.dram_tensor("wv", [D, D], F32, kind="ExternalInput").ap()
    wgt = nc.dram_tensor("wgt", [D, D], F32, kind="ExternalInput").ap()
    wo = nc.dram_tensor("wo", [D, D], F32, kind="ExternalInput").ap()
    fg = nc.dram_tensor("fg", [H, D], I32, kind="ExternalInput").ap()
    fu = nc.dram_tensor("fu", [H, D], I32, kind="ExternalInput").ap()
    fd = nc.dram_tensor("fd", [D, H], I32, kind="ExternalInput").ap()
    ld0 = nc.dram_tensor("ld0", [1, 1], F32, kind="ExternalInput").ap()
    ident = nc.dram_tensor("ident", [128, 128], BF16, kind="ExternalInput").ap()
    tmod = nc.dram_tensor("tmod", [128, TT], F32, kind="ExternalInput").ap()
    maskt = nc.dram_tensor("maskt", [128, 128], BF16, kind="ExternalInput").ap()
    out = nc.dram_tensor("out", [NTOK, D], F32, kind="ExternalOutput").ap()

    xbf_scr = nc.dram_tensor("xbf_scr", [NTOK, D], BF16).ap()
    x1_scr = nc.dram_tensor("x1_scr", [NTOK, D], BF16).ap()
    t2 = {"q": nc.dram_tensor("t2_q", [DC * DC * 128 * 128], BF16),
          "k": nc.dram_tensor("t2_k", [DC * DC * 128 * 128], BF16),
          "g": nc.dram_tensor("t2_g", [DC * DC * 128 * 128], BF16),
          "v": nc.dram_tensor("t2_v", [DC * 128 * 1024], BF16),
          "o": nc.dram_tensor("t2_o", [DC * 128 * 1024], BF16),
          "fd": nc.dram_tensor("t2_fd", [HC * 128 * 1024], BF16),
          "fg": nc.dram_tensor("t2_fg", [HC * DC * 128 * 128], BF16),
          "fu": nc.dram_tensor("t2_fu", [HC * DC * 128 * 128], BF16)}

    st = ExitStack()
    with st:
        def sb(name, shape, dt, **kw):
            return st.enter_context(nc.sbuf_tensor(name, shape, dt, **kw))

        DFt = sb("DF", [128, TT], BF16)
        DFI = sb("DFI", [128, TT], BF16)
        MSK = sb("MSK", [128, 128], BF16)
        sld = sb("sld", [1, 1], F32)
        ones = sb("ones", [1, 128], F32)
        lvec = sb("lvec", [128, 1], F32)
        epsT = sb("epsT", [128, 1], F32)
        tmpd = sb("tmpd", [128, TT], F32)

        xbf = sb("xbf", [128, NS * D], BF16)
        xfm2 = [sb(f"xfm{i}", [128, DC * TT], BF16, is_transpose_dest=True) for i in range(2)]
        qp = sb("qp", [128, DC * TT], BF16)
        kp = sb("kp", [128, DC * TT], BF16)
        vtm = sb("vtm", [128, NS * D], BF16)
        gfm = sb("gfm", [128, DC * TT], BF16)
        gof = sb("gof", [128, NS * D], BF16)
        stm = sb("stm", [128, NS * 128], BF16)
        r1 = sb("r1", [128, NS * D], BF16)
        x1b = [sb(f"x1b{i}", [128, NS * D], BF16) for i in range(2)]
        x1fm2 = [sb(f"x1fm{i}", [128, DC * TT], BF16, is_transpose_dest=True) for i in range(2)]
        sgate = sb("sgate", [128, 2 * TT], BF16)
        sup = sb("sup", [128, 2 * TT], BF16)
        hb = sb("hb", [128, HC * TT], BF16, is_transpose_dest=True)
        r2 = sb("r2", [128, NS * D], BF16)
        outb = [sb(f"outb{i}", [128, D], F32) for i in range(2)]
        junk = sb("junk", [128, D], BF16)

        sum1 = sb("sum1", [128, NS], F32)
        sq1 = sb("sq1", [128, NS], F32)
        mst = sb("mst", [128, NS], F32)
        msq = sb("msq", [128, NS], F32)
        vart = sb("vart", [128, NS], F32)
        rstd = sb("rstd", [128, NS], F32)
        sum2 = sb("sum2", [128, NS], F32)
        sq2 = sb("sq2", [128, NS], F32)
        mst2 = sb("mst2", [128, NS], F32)
        msq2 = sb("msq2", [128, NS], F32)
        vart2 = sb("vart2", [128, NS], F32)
        rstd2 = sb("rstd2", [128, NS], F32)

        bufq = sb("bufq", [128, 2 * 1024], BF16)
        bufk = sb("bufk", [128, 2 * 1024], BF16)
        bufg = sb("bufg", [128, 2 * 1024], BF16)
        bufv = sb("bufv", [128, 2 * 1024], BF16, is_transpose_dest=True)
        stgi = [sb(f"stgi{i}", [128, 1024], I32) for i in range(2)]
        xcb = [sb("xcb0", [128, 1024], BF16)]
        identS = sb("identS", [128, 128], BF16)
        tev = [sb(f"tev{i}", [128, 1024], BF16) for i in range(2)]
        bufo = sb("bufo", [128, 3 * 1024], BF16, is_transpose_dest=True)
        buffg = sb("buffg", [128, 2 * 1024], BF16)
        buffu = sb("buffu", [128, 2 * 1024], BF16)
        buffd = sb("buffd", [128, 3 * 1024], BF16, is_transpose_dest=True)

        ps = [st.enter_context(nc.psum_tensor(f"ps{i}", [128, 1024], F32)) for i in range(4)]

        def bank_ap(b, cols=512):
            return ps[b // 2][:, (b % 2) * 512:(b % 2) * 512 + cols]

        def pair_ap(pi):
            return ps[pi]

        bank_free = {b: ("ct", 0) for b in range(8)}
        _bkcur = [0]

        def grab(n=1):
            if n == 2 and _bkcur[0] % 2:
                _bkcur[0] += 1
            ids = [(_bkcur[0] + k) % 8 for k in range(n)]
            _bkcur[0] += n
            return ids

        def take(b):
            return [bank_free[b]]

        def free(b, sem, cnt):
            bank_free[b] = (sem, cnt)

        cast_cnt = {}
        cast_x = [f"x{i}c" for i in range(ntiles)]

        p.op("gpsimd", lambda e: e.memset(ones[:], 1.0))
        p.op("gpsimd", lambda e: e.memset(epsT[:], EPS), inc="cp")

        conv_pl = [None, None]
        conv_hb = {}
        W_LANES = [(f"W{k}", hb[:, 6144 + k * 1024: 6144 + (k + 1) * 1024]) for k in range(4)]
        X_LANES = [("X0", xcb[0][:])]
        _st = dict(pl=0, xl=0, wl=0, tev=0)
        tev_free = [None, None]
        t2_done = {}

        def _load_chunk(src_ap, r, cc, is_f32, waits):
            slot = _st["pl"] % 2
            _st["pl"] += 1
            sstg = stgi[slot][:].bitcast(F32) if is_f32 else stgi[slot][:]
            lw = list(waits)
            if conv_pl[slot]:
                lw.append(conv_pl[slot])
            lc = p.op("sync",
                      lambda en, s_=src_ap, rr=r, c_=cc, d_=sstg:
                          en.dma_start(out=d_, in_=s_[rr * 128:(rr + 1) * 128,
                                                      c_ * 1024:(c_ + 1) * 1024]),
                      waits=lw, inc=f"PL{slot}", amt=16)
            return slot, sstg, lc

        def conv_x(i):
            for r in range(NS):
                slot, sstg, lc = _load_chunk(xin[i * TT:(i + 1) * TT, :], r, 0, True, [])
                lane, lap = X_LANES[_st["xl"] % len(X_LANES)]
                _st["xl"] += 1
                hw = [(f"PL{slot}", lc)]
                if conv_hb.get(lane):
                    hw.append(conv_hb[lane])
                cc_ = p.op("gpsimd",
                           lambda en, la=lap, s_=sstg: en.tensor_copy(la, s_),
                           waits=hw, inc="cp")
                conv_pl[slot] = ("cp", cc_)
                sc_ = p.op("scalar",
                           lambda en, ii=i, rr=r, la=lap:
                               en.dma_start(out=xbf_scr[ii * TT + rr * 128:
                                                        ii * TT + (rr + 1) * 128, :], in_=la),
                           waits=[("cp", cc_)], inc="S" + lane, amt=16)
                conv_hb[lane] = ("S" + lane, sc_)
            cast_cnt[f"x{i}c"] = [("SX0", p.cnt["SX0"])]

        wdone = {}

        def wconv_chunk(name, src_ap, is_f32, dst, style, r, cg, lanes=None,
                        first_gate=()):
                    lanes = lanes or W_LANES
                    slot, sstg, lc = _load_chunk(src_ap, r, cg, is_f32, [])
                    lane, lap = lanes[_st["wl"] % len(lanes)]
                    _st["wl"] += 1
                    hw = [(f"PL{slot}", lc)]
                    if conv_hb.get(lane):
                        hw.append(conv_hb[lane])
                    else:
                        hw.extend(first_gate)
                    cc_ = p.op("vector",
                               lambda en, la=lap, s_=sstg: en.tensor_copy(la, s_),
                               waits=hw, inc="cv")
                    conv_pl[slot] = ("cv", cc_)
                    b = grab()[0]
                    bw = take(b)
                    tc = None
                    for c in range(DC):
                        tc = p.op("tensor",
                                  lambda en, bb=b, la=lap, c_=c:
                                      en.transpose(
                                          bank_ap(bb).bitcast(BF16)[:, c_ * 128:(c_ + 1) * 128],
                                          la[:, c_ * 128:(c_ + 1) * 128], identS[:]),
                                  waits=([("cv", cc_), ("dS", c_idt)] + bw) if c == 0 else (),
                                  inc="ct" if c == DC - 1 else None)
                    conv_hb[lane] = ("ct", tc)
                    ts_ = _st["tev"] % 2
                    _st["tev"] += 1
                    ew = [("ct", tc)]
                    if tev_free[ts_]:
                        ew.append(tev_free[ts_])
                    ev = p.op("scalar",
                              lambda en, bb=b, t_=ts_:
                                  en.copy(tev[t_][:], bank_ap(bb).bitcast(BF16)),
                              waits=ew, inc="cs")
                    free(b, "cs", ev)
                    if style == "lhsT":
                        dap = bass.AP(dst, r * DC * 16384, [[1024, 128], [1, 1024]])
                    else:
                        dap = bass.AP(dst, cg * DC * 131072 + r * 128,
                                      [[1024, 128], [131072, DC], [1, 128]])
                    ssem = f"TV{ts_}"
                    sc_ = p.op("scalar",
                               lambda en, da=dap, t_=ts_, st_=style:
                                   en.dma_start(out=da, in_=tev[t_][:] if st_ == "lhsT"
                                                else tev[t_][:].rearrange("p (c l) -> p c l",
                                                                          l=128)),
                               waits=[("cs", ev)], inc=ssem, amt=16)
                    tev_free[ts_] = (ssem, sc_)
                    wdone.setdefault(name, []).append((ssem, sc_))

        def wconv(name, src_ap, R, C, is_f32, dst, style, cg_outer=False):
            if cg_outer:
                for cg in range(C // 1024):
                    for r in range(R // 128):
                        wconv_chunk(name, src_ap, is_f32, dst, style, r, cg)
            else:
                for r in range(R // 128):
                    for cg in range(C // 1024):
                        wconv_chunk(name, src_ap, is_f32, dst, style, r, cg)
            t2_done[name] = [("TV0", p.cnt["TV0"]), ("TV1", p.cnt["TV1"])]
            cast_cnt[name] = t2_done[name]

        p.op("sync", lambda e: e.dma_start(out=sld[:], in_=ld0), inc="dS", amt=16)
        p.op("sync", lambda e: e.dma_start(out=tmpd[:], in_=tmod), inc="dS", amt=16)
        p.op("sync", lambda e: e.dma_start(out=MSK[:], in_=maskt), inc="dS", amt=16)
        c_idt = p.op("sync", lambda e: e.dma_start(out=identS[:], in_=ident), inc="dS", amt=16)
        c_con = c_idt
        c_sld = c_tmd = c_msk = c_con

        c_bc = p.op("tensor", lambda e: e.matmul(ps[0][:, 0:1], ones[:], sld[:],
                                                 start=True, stop=True),
                    waits=[("dS", c_sld), ("cp", 1)], inc="ct")
        c_sg = p.op("scalar", lambda e: e.activation(lvec[:], ps[0][:, 0:1], AF.Sigmoid),
                    waits=[("ct", c_bc)], inc="cs")
        c_ln = p.op("scalar", lambda e: e.activation(lvec[:], lvec[:], AF.Ln),
                    waits=[("cs", c_sg)], inc="cs")
        c_tm = p.op("vector", lambda e: e.tensor_scalar(tmpd[:], tmpd[:], lvec[:, 0:1], None,
                                                        ALU.mult),
                    waits=[("cs", c_ln), ("dS", c_tmd)], inc="cv")
        p.op("scalar", lambda e: e.activation(DFt[:], tmpd[:], AF.Exp),
             waits=[("cv", c_tm)], inc="cs")
        c_dfi = p.op("scalar", lambda e: e.activation(DFI[:], tmpd[:], AF.Exp, scale=-1.0),
                     inc="cs")
        free(0, "cs", c_sg)

        wconv("wv", wv, D, D, True, t2["v"], "rhs")
        wconv("wq", wq, D, D, True, t2["q"], "lhsT")
        conv_x(0)
        if ntiles > 1:
            conv_x(1)

        def mk_scr2_loader(t2t, npt, buf):
            def mk(g, slot):
                j = g % npt
                def fn(e):
                    src = bass.AP(t2t, j * DC * 16384, [[128, 128], [16384, DC], [1, 128]])
                    dst = buf[:, slot * 1024:(slot + 1) * 1024].rearrange(
                        "p (e l) -> p e l", l=128)
                    return e.dma_start(out=dst, in_=src)
                return fn
            return mk

        def mk_piece_loader(t2t, npt, buf):
            def mk(g, slot):
                j = g % npt
                def fn(e):
                    src = bass.AP(t2t, j * 131072, [[1024, 128], [1, 1024]])
                    return e.dma_start(out=buf[:, slot * 1024:(slot + 1) * 1024], in_=src)
                return fn
            return mk

        MQ = Stream(p, "q", bufq, 2, mk_piece_loader(t2["q"], DC, bufq))
        MQ.src_wait_fn = lambda g: [wdone["wq"][g % DC]]
        MK = Stream(p, "k", bufk, 2, mk_piece_loader(t2["k"], DC, bufk))
        MK.src_wait_fn = lambda g: [wdone["wk"][g % DC]]
        MG = Stream(p, "g", bufg, 2, mk_piece_loader(t2["g"], DC, bufg))
        MG.src_wait_fn = lambda g: [wdone["wg"][g % DC]]
        MV = Stream(p, "v", bufv, 2, mk_piece_loader(t2["v"], DC, bufv))
        MV.src_wait_fn = lambda g: [wdone["wv"][-1]]
        MO = Stream(p, "o", bufo, 3, mk_piece_loader(t2["o"], DC, bufo))
        MO.src_wait_fn = lambda g: [wdone["wo"][-1]]
        MFG = Stream(p, "G", buffg, 2, mk_piece_loader(t2["fg"], HC, buffg))
        MFG.src_wait_fn = lambda g: [wdone["fg"][g % HC]]
        MFU = Stream(p, "U", buffu, 2, mk_piece_loader(t2["fu"], HC, buffu))
        MFU.src_wait_fn = lambda g: [wdone["fu"][g % HC]]
        MFD = Stream(p, "D", buffd, 3, mk_piece_loader(t2["fd"], HC, buffd))
        MFD.src_wait_fn = lambda g: [wdone["fd"][((g % HC) // DC) * DC + DC - 1]]

        state = dict(xfm_free0=0, xfm_free1=0, x1fm_free0=0, x1fm_free1=0, xbf_free=0,
                     qp_free=0, kp_free=0, r1_free=("cv", 0), ln2_free=0)
        x1b_store = [None, None]
        out_store = [None, None]

        def emit_inputs(ti):
            par = ti % 2
            t0 = ti * TT
            xfm = xfm2[par]
            xfm_loads = None
            for e in range(DC):
                w = (list(cast_cnt[cast_x[ti]]) + [("ct", state[f"xfm_free{par}"])]) if e == 0 else ()
                cnt = p.op("scalar",
                           lambda en, ee=e, tt0=t0, xf=xfm: en.dma_start(
                               out=xf[:, ee * TT:(ee + 1) * TT],
                               in_=xbf_scr[tt0:tt0 + TT, ee * 128:(ee + 1) * 128],
                               transpose=True),
                           waits=w, inc=f"DXF{par}", amt=16)
                xfm_loads = (f"DXF{par}", cnt)
            cnt = p.op("scalar",
                       lambda en, tt0=t0: en.dma_start(
                           out=xbf[:].rearrange("p (s d) -> p s d", d=D),
                           in_=bass.AP(xbf_scr.tensor, tt0 * D,
                                       [[D, 128], [128 * D, NS], [1, D]])),
                       waits=list(cast_cnt[cast_x[ti]]) + [("cv", state["xbf_free"])],
                       inc="DXB", amt=16)
            state[f"inp_{ti}"] = (xfm_loads, ("DXB", cnt))

        def emit_attn_body(ti):
            par = ti % 2
            t0 = ti * TT
            xfm = xfm2[par]
            xfm_loads, xbf_load = state[f"inp_{ti}"]

            q_evac_last = k_evac_last = 0
            for which, M, dst, dfap, freekey in (("q", MQ, qp, DFt, "qp_free"),
                                                 ("k", MK, kp, DFI, "kp_free")):
                if ti == 0 and which == "k":
                    wconv("wk", wk, D, D, True, t2["k"], "lhsT")
                base_g = M.next_g
                M.load(base_g)
                M.load(base_g + 1)
                gc = None
                for j in range(DC):
                    g = base_g + j
                    b = grab()[0]
                    slot = M.slot(g)
                    waits = [M.loaded[g], xfm_loads] + take(b)
                    for e in range(DC):
                        gc = p.op("tensor",
                                  lambda en, bb=b, sl=slot, ee=e, Mb=M.buf, xf=xfm:
                                      en.matmul(bank_ap(bb),
                                                Mb[:, sl * 1024 + ee * 128:sl * 1024 + (ee + 1) * 128],
                                                xf[:, ee * TT:(ee + 1) * TT],
                                                start=(ee == 0), stop=(ee == DC - 1)),
                                  waits=waits if e == 0 else (),
                                  inc="ct" if e == DC - 1 else None)
                    M.consume(g, gc)
                    if g + 2 < base_g + DC:
                        M.load(g + 2)
                    ew = [("ct", gc), ("cs", c_dfi)]
                    if j == 0:
                        ew.append(("ct", state[freekey]))
                    ec = p.op("vector",
                              lambda en, bb=b, jj=j, dd=dst, df=dfap:
                                  en.tensor_tensor(dd[:, jj * TT:(jj + 1) * TT],
                                                   bank_ap(bb), df[:], ALU.mult),
                              waits=ew, inc="cv")
                    free(b, "cv", ec)
                    if which == "q":
                        q_evac_last = ec
                    else:
                        k_evac_last = ec

            v_evac_last = 0
            for sub in range(1):
                groups = [(s, h) for s in range(NS) for h in range(2)]
                gbanks = [grab()[0] for _ in groups]
                base_g = MV.next_g
                MV.load(base_g)
                MV.load(base_g + 1)
                gct = None
                for e in range(DC):
                    g = base_g + e
                    slot = MV.slot(g)
                    for gi, (s, h) in enumerate(groups):
                        b = gbanks[gi]
                        if e == 0:
                            waits = [MV.loaded[g], xfm_loads] + take(b)
                        elif gi == 0:
                            waits = [MV.loaded[g]]
                        else:
                            waits = ()
                        gct = p.op("tensor",
                                   lambda en, bb=b, ss=s, hh=h, ee=e, sl=slot, xf=xfm:
                                       en.matmul(bank_ap(bb),
                                                 xf[:, ee * TT + ss * 128:ee * TT + (ss + 1) * 128],
                                                 bufv[:, sl * 1024 + hh * 512:sl * 1024 + (hh + 1) * 512],
                                                 start=(ee == 0), stop=(ee == DC - 1)),
                                   waits=waits,
                                   inc="ct" if gi == len(groups) - 1 else None)
                    MV.consume(g, gct)
                    if e + 2 < DC:
                        MV.load(g + 2)
                for gi, (s, h) in enumerate(groups):
                    b = gbanks[gi]
                    ec = p.op("vector",
                              lambda en, bb=b, ss=s, hh=h:
                                  en.tensor_copy(vtm[:, ss * D + hh * 512:ss * D + (hh + 1) * 512],
                                                 bank_ap(bb)),
                              waits=[("ct", gct)], inc="cv")
                    free(b, "cv", ec)
                    v_evac_last = ec

            if ti == 0:
                wconv("wg", wgt, D, D, True, t2["g"], "lhsT")
                wconv("wo", wo, D, D, True, t2["o"], "rhs")
            g_evac_last = 0
            base_g = MG.next_g
            MG.load(base_g)
            MG.load(base_g + 1)
            gc = None
            for j in range(DC):
                g = base_g + j
                b = grab()[0]
                slot = MG.slot(g)
                waits = [MG.loaded[g], xfm_loads] + take(b)
                for e in range(DC):
                    gc = p.op("tensor",
                              lambda en, bb=b, sl=slot, ee=e, xf=xfm:
                                  en.matmul(bank_ap(bb),
                                            bufg[:, sl * 1024 + ee * 128:sl * 1024 + (ee + 1) * 128],
                                            xf[:, ee * TT:(ee + 1) * TT],
                                            start=(ee == 0), stop=(ee == DC - 1)),
                              waits=waits if e == 0 else (),
                              inc="ct" if e == DC - 1 else None)
                MG.consume(g, gc)
                if g + 2 < base_g + DC:
                    MG.load(g + 2)
                ec = p.op("scalar",
                          lambda en, bb=b, jj=j:
                              en.activation(gfm[:, jj * TT:(jj + 1) * TT], bank_ap(bb),
                                            AF.Sigmoid),
                          waits=[("ct", gc)], inc="cs")
                free(b, "cs", ec)
                g_evac_last = ec
            xfm_free_new = gc

            s_evac = []
            gc = None
            for s in range(NS):
                b = grab()[0]
                waits = [("cv", max(q_evac_last, k_evac_last))] + take(b)
                for e in range(DC):
                    gc = p.op("tensor",
                              lambda en, bb=b, ss=s, ee=e:
                                  en.matmul(bank_ap(bb, 128),
                                            kp[:, ee * TT + ss * 128:ee * TT + (ss + 1) * 128],
                                            qp[:, ee * TT + ss * 128:ee * TT + (ss + 1) * 128],
                                            start=(ee == 0), stop=(ee == DC - 1)),
                              waits=waits if e == 0 else (),
                              inc="ct" if e == DC - 1 else None)
                ec = p.op("vector",
                          lambda en, bb=b, ss=s:
                              en.tensor_tensor(stm[:, ss * 128:(ss + 1) * 128],
                                               bank_ap(bb, 128), MSK[:], ALU.mult),
                          waits=[("ct", gc), ("dS", c_msk)], inc="cv")
                free(b, "cv", ec)
                s_evac.append(ec)
            s_phase_end = gc

            gof_evac = []
            gc = None
            for s in range(NS):
                pids = grab(2)
                pi = pids[0] // 2
                waits = [("cv", v_evac_last), ("cv", s_evac[s])] + take(pids[0]) + take(pids[1])
                for j in range(DC):
                    gc = p.op("tensor",
                              lambda en, pp=pi, ss=s, jj=j:
                                  en.matmul(pair_ap(pp)[:, jj * 128:(jj + 1) * 128],
                                            vtm[:, ss * D + jj * 128:ss * D + (jj + 1) * 128],
                                            stm[:, ss * 128:(ss + 1) * 128],
                                            start=True, stop=True),
                              waits=waits if j == 0 else (),
                              inc="ct" if j == DC - 1 else None)
                ec = p.op("vector",
                          lambda en, pp=pi, ss=s:
                              en.tensor_tensor(
                                  gof[:, ss * D:(ss + 1) * D].rearrange("p (j l) -> p j l", l=128),
                                  pair_ap(pp).rearrange("p (j l) -> p j l", l=128),
                                  gfm.rearrange("p (j t) -> p j t", t=TT)[:, :, ss * 128:(ss + 1) * 128],
                                  ALU.mult),
                          waits=[("ct", gc), ("cs", g_evac_last)], inc="cv")
                free(pids[0], "cv", ec)
                free(pids[1], "cv", ec)
                gof_evac.append(ec)
            o_phase_end = gc

            r1_evac_last = 0
            wgroups = [(s, h) for s in range(NS) for h in range(2)]
            wbanks = [grab()[0] for _ in wgroups]
            base_g = MO.next_g
            for la in range(2):
                MO.load(base_g + la)
            gct = None
            for e in range(DC):
                g = base_g + e
                slot = MO.slot(g)
                for gi, (s, h) in enumerate(wgroups):
                    b = wbanks[gi]
                    if e == 0:
                        waits = [MO.loaded[g], ("cv", gof_evac[NS - 1])] + take(b)
                    elif gi == 0:
                        waits = [MO.loaded[g]]
                    else:
                        waits = ()
                    gct = p.op("tensor",
                               lambda en, bb=b, ss=s, ee=e, sl=slot, hh=h:
                                   en.matmul(bank_ap(bb),
                                             gof[:, (ss * DC + ee) * 128:(ss * DC + ee + 1) * 128],
                                             bufo[:, sl * 1024 + hh * 512:sl * 1024 + (hh + 1) * 512],
                                             start=(ee == 0), stop=(ee == DC - 1)),
                               waits=waits,
                               inc="ct" if gi == len(wgroups) - 1 else None)
                MO.consume(g, gct)
                if e + 2 < DC:
                    MO.load(g + 2)
            for gi, (s, h) in enumerate(wgroups):
                b = wbanks[gi]
                ec = p.op("vector",
                          lambda en, bb=b, ss=s, hh=h:
                              en.scalar_tensor_tensor(
                                  r1[:, ss * D + hh * 512:ss * D + (hh + 1) * 512],
                                  xbf[:, ss * D + hh * 512:ss * D + (hh + 1) * 512],
                                  2.0, bank_ap(bb), ALU.mult, ALU.add),
                          waits=[("ct", gct), xbf_load,
                                 state["r1_free"]],
                          inc="cv")
                free(b, "cv", ec)
                r1_evac_last = ec

            sq_last = 0
            for s in range(NS):
                p.op("scalar",
                     lambda en, ss=s: en.activation(junk[:], r1[:, ss * D:(ss + 1) * D],
                                                    AF.Copy, accum_out=sum1[:, ss:ss + 1]),
                     waits=[("cv", r1_evac_last)] if s == 0 else (), inc="cs")
                sq_last = p.op("scalar",
                               lambda en, ss=s: en.activation(junk[:], r1[:, ss * D:(ss + 1) * D],
                                                              AF.Square,
                                                              accum_out=sq1[:, ss:ss + 1]),
                               waits=(), inc="cs")
            c_mean = p.op("vector", lambda en: en.tensor_scalar(mst[:], sum1[:], 1.0 / D, None,
                                                                ALU.mult),
                          waits=[("cs", sq_last)], inc="cv")
            c_msq = p.op("vector", lambda en: en.tensor_tensor(msq[:], mst[:], mst[:], ALU.mult),
                         waits=[("cv", c_mean)], inc="cv")
            c_var = p.op("vector",
                         lambda en: en.scalar_tensor_tensor(vart[:], sq1[:], 1.0 / D, msq[:],
                                                            ALU.mult, ALU.subtract),
                         waits=[("cv", c_msq)], inc="cv")
            c_lnv = p.op("scalar", lambda en: en.activation(vart[:], vart[:], AF.Ln, bias=epsT[:]),
                         waits=[("cv", c_var)], inc="cs")
            c_rstd = p.op("scalar", lambda en: en.activation(rstd[:], vart[:], AF.Exp, scale=-0.5),
                          waits=[("cs", c_lnv)], inc="cs")
            x1r = 0
            for s in range(NS):
                w = [("cs", c_rstd)]
                if s == 0 and x1b_store[par] is not None:
                    w.append(x1b_store[par])
                x1r = p.op("vector",
                           lambda en, ss=s, pp=par:
                               en.tensor_scalar(x1b[pp][:, ss * D:(ss + 1) * D],
                                                r1[:, ss * D:(ss + 1) * D],
                                                mst[:, ss:ss + 1], rstd[:, ss:ss + 1],
                                                ALU.subtract, ALU.mult),
                           waits=w, inc="cv")

            state[f"x1r_{ti}"] = x1r
            x1b_store[par] = ("DX1S", p.op("scalar",
                                  lambda en, tt0=t0, pp=par: en.dma_start(
                                      out=bass.AP(x1_scr.tensor, tt0 * D,
                                                  [[D, 128], [128 * D, NS], [1, D]]),
                                      in_=x1b[pp][:].rearrange("p (s d) -> p s d", d=D)),
                                  waits=[("cv", x1r)], inc="DX1S", amt=16))

            state[f"xfm_free{par}"] = xfm_free_new
            state["xbf_free"] = r1_evac_last
            state["qp_free"] = s_phase_end
            state["kp_free"] = s_phase_end
            state["r1_free"] = ("cv", 0)

        def emit_x1fm(ti):
            par = ti % 2
            t0 = ti * TT
            x1f = x1fm2[par]
            x1fm_loads = None
            for e in range(DC):
                w = [x1b_store[par], ("ct", state[f"x1fm_free{par}"])] if e == 0 else ()
                cnt = p.op("scalar",
                           lambda en, ee=e, tt0=t0, xf=x1f: en.dma_start(
                               out=xf[:, ee * TT:(ee + 1) * TT],
                               in_=x1_scr[tt0:tt0 + TT, ee * 128:(ee + 1) * 128],
                               transpose=True),
                           waits=w, inc=f"DX1F{par}", amt=16)
                x1fm_loads = (f"DX1F{par}", cnt)
            state[f"x1fm_loads_{ti}"] = x1fm_loads

        def emit_ffn_gateup(ti):
            par = ti % 2
            t0 = ti * TT
            x1fm = x1fm2[par]
            x1fm_loads = state[f"x1fm_loads_{ti}"]
            if ti == 0:
                state["prologue_store_done"] = [("TV0", p.cnt["TV0"]),
                                                ("TV1", p.cnt["TV1"])]

            h_last = 0
            LAG = 8
            if ti == 0:
                F_LANES = [("F0", qp[:, 0:1024]), ("F1", qp[:, 1024:2048]),
                           ("F2", kp[:, 0:1024]), ("F3", kp[:, 1024:2048])]
                fgate = [("ct", state["qp_free"])]
                for jj in range(LAG):
                    wconv_chunk("fg", fg, False, t2["fg"], "lhsT", jj, 0, F_LANES, fgate)
                    wconv_chunk("fu", fu, False, t2["fu"], "lhsT", jj, 0, F_LANES, fgate)
                    wconv_chunk("fd", fd, False, t2["fd"], "rhs", jj % DC, jj // DC,
                                F_LANES, fgate)
            base_fg = MFG.next_g
            base_fu = MFU.next_g
            MFG.load(base_fg)
            MFU.load(base_fu)
            MFG.load(base_fg + 1)
            MFU.load(base_fu + 1)
            sg_free = [0, 0]
            gcu = None
            for j in range(HC):
                if ti == 0 and j + LAG < HC:
                    jj = j + LAG
                    wconv_chunk("fg", fg, False, t2["fg"], "lhsT", jj, 0, F_LANES)
                    wconv_chunk("fu", fu, False, t2["fu"], "lhsT", jj, 0, F_LANES)
                    wconv_chunk("fd", fd, False, t2["fd"], "rhs", jj % DC, jj // DC,
                                F_LANES)
                gg = base_fg + j
                gu = base_fu + j
                bgate = grab()[0]
                bup = grab()[0]
                slotg = MFG.slot(gg)
                slotu = MFU.slot(gu)
                waits = [MFG.loaded[gg], x1fm_loads] + take(bgate)
                gcg = None
                for e in range(DC):
                    gcg = p.op("tensor",
                               lambda en, bb=bgate, sl=slotg, ee=e, x1f_=x1fm:
                                   en.matmul(bank_ap(bb),
                                             buffg[:, sl * 1024 + ee * 128:sl * 1024 + (ee + 1) * 128],
                                             x1f_[:, ee * TT:(ee + 1) * TT],
                                             start=(ee == 0), stop=(ee == DC - 1)),
                               waits=waits if e == 0 else (),
                               inc="ct" if e == DC - 1 else None)
                MFG.consume(gg, gcg)
                waits = [MFU.loaded[gu]] + take(bup)
                for e in range(DC):
                    gcu = p.op("tensor",
                               lambda en, bb=bup, sl=slotu, ee=e, x1f_=x1fm:
                                   en.matmul(bank_ap(bb),
                                             buffu[:, sl * 1024 + ee * 128:sl * 1024 + (ee + 1) * 128],
                                             x1f_[:, ee * TT:(ee + 1) * TT],
                                             start=(ee == 0), stop=(ee == DC - 1)),
                               waits=waits if e == 0 else (),
                               inc="ct" if e == DC - 1 else None)
                MFU.consume(gu, gcu)
                if j + 2 < HC:
                    MFG.load(gg + 2)
                    MFU.load(gu + 2)
                sl2 = j % 2
                e1 = p.op("scalar",
                          lambda en, bb=bgate, s2=sl2:
                              en.activation(sgate[:, s2 * TT:(s2 + 1) * TT], bank_ap(bb),
                                            AF.Silu, scale=QSC),
                          waits=[("ct", gcg), ("cv", sg_free[sl2])], inc="cs")
                free(bgate, "cs", e1)
                e2 = p.op("vector",
                          lambda en, bb=bup, s2=sl2:
                              en.tensor_scalar(sup[:, s2 * TT:(s2 + 1) * TT], bank_ap(bb),
                                               QSC * QSC, None, ALU.mult),
                          waits=[("ct", gcu)], inc="cv")
                free(bup, "cv", e2)
                hw = [("cs", e1), ("cv", e2)]
                if ti == 0 and j == 0:
                    hw.extend(state["prologue_store_done"])
                h_last = p.op("vector",
                              lambda en, jj=j, s2=sl2:
                                  en.tensor_tensor(hb[:, jj * TT:(jj + 1) * TT],
                                                   sgate[:, s2 * TT:(s2 + 1) * TT],
                                                   sup[:, s2 * TT:(s2 + 1) * TT], ALU.mult),
                              waits=hw, inc="cv")
                sg_free[sl2] = h_last

            state[f"hlast_{ti}"] = h_last
            state[f"x1fm_free{par}"] = gcu

        def emit_ffn_down(ti):
            par = ti % 2
            t0 = ti * TT
            h_last = state[f"hlast_{ti}"]
            r2_evac_last = 0
            dgroups = [(s, h) for s in range(NS) for h in range(2)]
            dbanks = [grab()[0] for _ in dgroups]
            base_g = MFD.next_g
            for la in range(2):
                MFD.load(base_g + la)
            gct = None
            for jh in range(HC):
                g = base_g + jh
                slot = MFD.slot(g)
                for gi, (s, h) in enumerate(dgroups):
                    b = dbanks[gi]
                    if jh == 0:
                        waits = [MFD.loaded[g], ("cv", h_last)] + take(b)
                    elif gi == 0:
                        waits = [MFD.loaded[g]]
                    else:
                        waits = ()
                    gct = p.op("tensor",
                               lambda en, bb=b, ss=s, jj=jh, sl=slot, hh=h:
                                   en.matmul(bank_ap(bb),
                                             hb[:, jj * TT + ss * 128:jj * TT + (ss + 1) * 128],
                                             buffd[:, sl * 1024 + hh * 512:sl * 1024 + (hh + 1) * 512],
                                             start=(jj == 0), stop=(jj == HC - 1)),
                               waits=waits,
                               inc="ct" if gi == len(dgroups) - 1 else None)
                MFD.consume(g, gct)
                if jh + 2 < HC:
                    MFD.load(g + 2)
            for gi, (s, h) in enumerate(dgroups):
                b = dbanks[gi]
                ec = p.op("vector",
                          lambda en, bb=b, ss=s, hh=h, pp=par:
                              en.scalar_tensor_tensor(
                                  r2[:, ss * D + hh * 512:ss * D + (hh + 1) * 512],
                                  x1b[pp][:, ss * D + hh * 512:ss * D + (hh + 1) * 512],
                                  1.0, bank_ap(bb), ALU.mult, ALU.add),
                          waits=[("ct", gct)], inc="cv")
                free(b, "cv", ec)
                r2_evac_last = ec

            sq_last2 = 0
            for s in range(NS):
                p.op("scalar",
                     lambda en, ss=s: en.activation(junk[:], r2[:, ss * D:(ss + 1) * D],
                                                    AF.Copy, accum_out=sum2[:, ss:ss + 1]),
                     waits=[("cv", r2_evac_last)] if s == 0 else (), inc="cs")
                sq_last2 = p.op("scalar",
                                lambda en, ss=s: en.activation(junk[:], r2[:, ss * D:(ss + 1) * D],
                                                               AF.Square,
                                                               accum_out=sq2[:, ss:ss + 1]),
                                waits=(), inc="cs")
            c_mean2 = p.op("vector", lambda en: en.tensor_scalar(mst2[:], sum2[:], 1.0 / D, None,
                                                                 ALU.mult),
                           waits=[("cs", sq_last2)], inc="cv")
            c_msq2 = p.op("vector", lambda en: en.tensor_tensor(msq2[:], mst2[:], mst2[:], ALU.mult),
                          waits=[("cv", c_mean2)], inc="cv")
            c_var2 = p.op("vector",
                          lambda en: en.scalar_tensor_tensor(vart2[:], sq2[:], 1.0 / D, msq2[:],
                                                             ALU.mult, ALU.subtract),
                          waits=[("cv", c_msq2)], inc="cv")
            c_lnv2 = p.op("scalar", lambda en: en.activation(vart2[:], vart2[:], AF.Ln, bias=epsT[:]),
                          waits=[("cv", c_var2)], inc="cs")
            c_rstd2 = p.op("scalar", lambda en: en.activation(rstd2[:], vart2[:], AF.Exp, scale=-0.5),
                           waits=[("cs", c_lnv2)], inc="cs")
            for s in range(NS):
                op_ = s % 2
                w = [("cs", c_rstd2)]
                if out_store[op_] is not None:
                    w.append(out_store[op_])
                cnrm = p.op("vector",
                            lambda en, ss=s, oo=op_:
                                en.tensor_scalar(outb[oo][:], r2[:, ss * D:(ss + 1) * D],
                                                 mst2[:, ss:ss + 1], rstd2[:, ss:ss + 1],
                                                 ALU.subtract, ALU.mult),
                            waits=w, inc="cv")
                out_store[op_] = (f"DOS{op_}", p.op("scalar",
                                      lambda en, ss=s, oo=op_, tt0=t0: en.dma_start(
                                          out=out[tt0 + ss * 128:tt0 + (ss + 1) * 128, :],
                                          in_=outb[oo][:]),
                                      waits=[("cv", cnrm)], inc=f"DOS{op_}", amt=16))

        emit_inputs(0)
        emit_attn_body(0)
        emit_x1fm(0)
        if ntiles > 1:
            if ntiles > 2:
                conv_x(2)
            if ntiles > 3:
                conv_x(3)
            emit_inputs(1)
            emit_attn_body(1)
        for _i in range(1, ntiles):
            emit_ffn_gateup(_i - 1)
            if _i + 1 < ntiles:
                if _i + 3 < ntiles:
                    conv_x(_i + 3)
                emit_inputs(_i + 1)
            emit_x1fm(_i)
            emit_ffn_down(_i - 1)
            if _i + 1 < ntiles:
                emit_attn_body(_i + 1)
        emit_ffn_gateup(ntiles - 1)
        emit_ffn_down(ntiles - 1)

        if debug:
            fin00 = dict(p.cnt)
            for nm, srcap, shp in [("x1scr", x1_scr, [TT, D]), ("xbfscr", xbf_scr, [TT, D])]:
                dout = nc.dram_tensor("dbg_" + nm, shp, BF16, kind="ExternalOutput").ap()
                p.op("gpsimd",
                     lambda en, d=dout, s=srcap, sh=shp: en.dma_start(out=d, in_=s[0:sh[0], :]),
                     waits=[(s_, v_) for s_, v_ in fin00.items() if v_ > 0],
                     inc="dS", amt=16)
            dbg_specs = [("qp", qp, BF16), ("kp", kp, BF16), ("vtm", vtm, BF16),
                         ("gfm", gfm, BF16), ("stm", stm, BF16), ("gof", gof, BF16),
                         ("r1", r1, BF16), ("x1b0", x1b[0], BF16), ("hb", hb, BF16),
                         ("r2", r2, BF16), ("xfm", xfm2[0], BF16), ("x1fm", x1fm2[0], BF16),
                         ("dft", DFt, BF16), ("dfi", DFI, BF16), ("xbf", xbf, BF16),
                         ("mstd", mst, F32), ("rstdd", rstd, F32)]
            fin0 = dict(p.cnt)
            w0 = [("ct", fin0["ct"]), ("cv", fin0["cv"]), ("cs", fin0["cs"])]
            for name, buf, dt in dbg_specs:
                shape = [buf.shape[0] if hasattr(buf, "shape") else 128, buf[:].shape[1]]
                dout = nc.dram_tensor("dbg_" + name, list(buf[:].shape), dt,
                                      kind="ExternalOutput").ap()
                p.op("sync", lambda en, d=dout, b=buf: en.dma_start(out=d, in_=b[:]),
                     waits=w0, inc="dS", amt=16)
                w0 = []

        fin = dict(p.cnt)
        p.op("gpsimd", lambda en: en.nop(),
             waits=[(s_, v_) for s_, v_ in fin.items() if v_ > 0])

        for _e, _ops in p.ops.items():
            for _w, _f, _i, _a in _ops:
                for _s, _v in _w:
                    assert _v <= p.cnt.get(_s, 0), (_e, _s, _v, p.cnt.get(_s, 0))
        _pos = {e_: 0 for e_ in p.ops}
        _cnt = {s_: 0 for s_ in p.cnt}
        _prog = True
        while _prog:
            _prog = False
            for _e, _ops in p.ops.items():
                while _pos[_e] < len(_ops):
                    _w, _f, _i, _a = _ops[_pos[_e]]
                    if all(_cnt.get(s_, 0) >= v_ for s_, v_ in _w):
                        if _i:
                            _cnt[_i] = _cnt.get(_i, 0) + _a
                        _pos[_e] += 1
                        _prog = True
                    else:
                        break
        assert all(_pos[e_] == len(p.ops[e_]) for e_ in p.ops), \
            {e_: (_pos[e_], len(p.ops[e_])) for e_ in p.ops if _pos[e_] < len(p.ops[e_])}

        sems = {s_: st.enter_context(nc.semaphore(s_)) for s_ in p.cnt}
        block = st.enter_context(nc.Block())

        def runner(ename):
            def f(eng):
                for waits, fn, inc, amt in p.ops[ename]:
                    for s_name, v in waits:
                        eng.wait_ge(sems[s_name], v)
                    ins = fn(eng)
                    if inc is not None:
                        ins.then_inc(sems[inc], amt)
            return f

        block.tensor(runner("tensor"))
        block.vector(runner("vector"))
        block.scalar(runner("scalar"))
        block.sync(runner("sync"))
        block.gpsimd(runner("gpsimd"))

    return nc


_NC_CACHE = {}


def _get_nc(ntiles=NTILES):
    if ntiles not in _NC_CACHE:
        _NC_CACHE[ntiles] = build_nc(ntiles)
    return _NC_CACHE[ntiles]


def _consts():
    import ml_dtypes
    tmod = np.tile((np.arange(TT) % 32).astype(np.float32), (128, 1))
    k = np.arange(128)
    c = np.arange(128)
    maskt = ((k[:, None] // 32 == c[None, :] // 32) & (c[None, :] >= k[:, None])
             ).astype(ml_dtypes.bfloat16)
    ident = np.eye(128).astype(ml_dtypes.bfloat16)
    return np.ascontiguousarray(tmod), np.ascontiguousarray(maskt), np.ascontiguousarray(ident)


def kernel(L0_attn_x, L0_attn_W_q, L0_attn_W_k, L0_attn_W_v, L0_attn_W_g, L0_attn_W_o,
           L0_attn_log_decay, L0_ffn_w_gate, L0_ffn_w_up, L0_ffn_w_down):
    x = np.ascontiguousarray(np.asarray(L0_attn_x, dtype=np.float32)).reshape(-1, D)
    tmod, maskt, ident = _consts()
    ld0 = np.ascontiguousarray(np.asarray(L0_attn_log_decay, np.float32)[0:1].reshape(1, 1))
    common = dict(
        wq=np.ascontiguousarray(np.asarray(L0_attn_W_q, np.float32)),
        wk=np.ascontiguousarray(np.asarray(L0_attn_W_k, np.float32)),
        wv=np.ascontiguousarray(np.asarray(L0_attn_W_v, np.float32)),
        wgt=np.ascontiguousarray(np.asarray(L0_attn_W_g, np.float32)),
        wo=np.ascontiguousarray(np.asarray(L0_attn_W_o, np.float32)),
        fg=np.ascontiguousarray(np.asarray(L0_ffn_w_gate, np.int32)),
        fu=np.ascontiguousarray(np.asarray(L0_ffn_w_up, np.int32)),
        fd=np.ascontiguousarray(np.asarray(L0_ffn_w_down, np.int32)),
        ld0=ld0, tmod=tmod, maskt=maskt, ident=ident,
    )
    in_maps = []
    for c_ in range(NCORES):
        m = dict(common)
        m["x"] = np.ascontiguousarray(x[c_ * NTOK:(c_ + 1) * NTOK])
        in_maps.append(m)
    nc = _get_nc()
    res = run_bass_kernel_spmd(nc, in_maps, core_ids=list(range(NCORES)))
    outs = [res.results[c_]["out"] for c_ in range(NCORES)]
    full = np.concatenate(outs, axis=0).reshape(np.asarray(L0_attn_x).shape)
    return full.astype(np.float32)
